# version 01
# baseline (speedup 1.0000x reference)
import numpy as np
import ml_dtypes
from contextlib import ExitStack

import concourse.bass as bass
import concourse.mybir as mybir
from concourse.bass_utils import run_bass_kernel_spmd

F32 = mybir.dt.float32
BF16 = mybir.dt.bfloat16
U8 = mybir.dt.uint8
AF = mybir.ActivationFunctionType
ALU = mybir.AluOpType
AX = mybir.AxisListType

NCORES = 8
B, TFULL, D = 2, 4096, 2048
NH, HD = 16, 128
HID = 5632
NKD = 16
NKH = 44
T = 1024
TH = 512
HALO = 3
NT = T + HALO
EPS = 1e-5

ENGS = ("pe", "ve", "ac", "sy", "gp")
ESEM = {"pe": "pe", "ve": "ve", "ac": "ac", "sy": "sy_c", "gp": "gp_c"}


class Sched:
    def __init__(self, nc, ctx):
        self.nc = nc
        self.prog = {e: [] for e in ENGS}
        self.cnt = {}
        self.seen = {e: {} for e in ENGS}
        self.sems = {}
        self.ctx = ctx

    def sem(self, name):
        if name not in self.sems:
            self.sems[name] = self.ctx.enter_context(self.nc.semaphore(name))
            self.cnt.setdefault(name, 0)
        return self.sems[name]

    def wait(self, eng, sem_name, val):
        if val is None or val <= 0:
            return
        if self.seen[eng].get(sem_name, 0) >= val:
            return
        self.seen[eng][sem_name] = val
        s = self.sem(sem_name)
        self.prog[eng].append(lambda e, s=s, v=val: e.wait_ge(s, v))

    def wait_pair(self, eng, pair):
        if pair is not None:
            self.wait(eng, pair[0], pair[1])

    def wait_frees(self, eng, res):
        for (s, v) in res.frees:
            self.wait(eng, s, v)

    def do(self, eng, fn, inc=None, n=1):
        if inc is None:
            self.prog[eng].append(fn)
            return None
        self.cnt[inc] = self.cnt.get(inc, 0) + n
        tgt = self.cnt[inc]
        s = self.sem(inc)
        self.prog[eng].append(lambda e, fn=fn, s=s, n=n: fn(e).then_inc(s, n))
        return tgt

    def cur(self, sem_name):
        return self.cnt.get(sem_name, 0)

    def pad(self, eng, n=2):
        for _ in range(n):
            self.prog[eng].append(lambda e: e.engine_nop())


class Res:
    __slots__ = ("ready", "frees")

    def __init__(self):
        self.ready = None
        self.frees = []


def _build(taps=(), upto=99, skip=()):
    nc = bass.Bass("TRN2", target_bir_lowering=False, debug=False)

    def I(name, shape, dt):
        return nc.dram_tensor(name, shape, dt, kind="ExternalInput").ap()

    x_t = I("x_t", [D, NT], F32)
    wq_t = I("wq_t", [D, D], F32)
    wk_t = I("wk_t", [D, D], F32)
    wv_t = I("wv_t", [D, D], F32)
    wo_t = I("wo_t", [D, D], F32)
    wig_t = I("wig_t", [D, D], F32)
    wog_t = I("wog_t", [D, D], F32)
    wg_t = I("wg_t", [D, NH], F32)
    wqg_t = I("wqg_t", [D, HID], U8)
    wqu_t = I("wqu_t", [D, HID], U8)
    wqd_t = I("wqd_t", [HID, D], U8)
    convw = I("convw", [128, 64], F32)
    convb = I("convb", [128, 16], F32)
    big = I("big", [128, 16], F32)
    bog = I("bog", [128, 16], F32)
    bg = I("bg", [16, 1], F32)
    gnw = I("gnw", [128, 16], F32)
    gnb = I("gnb", [128, 16], F32)
    mnw = I("mnw", [128, 16], F32)
    mnb = I("mnb", [128, 16], F32)
    sgc = I("sgc", [128, 44], F32)
    suc = I("suc", [128, 44], F32)
    sdc = I("sdc", [128, 16], F32)
    oneh2 = I("oneh2", [32, NH * 128], BF16)
    selc = I("selc", [128, 8], F32)
    invc = I("invc", [128, 8], F32)

    y_t = nc.dram_tensor("y_t", [D, T], F32, kind="ExternalOutput").ap()

    cc_in = nc.dram_tensor("cc_in", [128, 32], F32)
    cc_out = nc.dram_tensor("cc_out", [NCORES * 128, 32], F32)
    xn_sp = nc.dram_tensor("xn_sp", [128, NKD * T], BF16)
    ml_sp = nc.dram_tensor("ml_sp", [128, NH * T], BF16)
    og_sp = nc.dram_tensor("og_sp", [128, NH * T], BF16)
    q_sp = nc.dram_tensor("q_sp", [128, NH * T], BF16)
    gt_sp = nc.dram_tensor("gt_sp", [128, NH * T], BF16)

    tap_dram = {}
    for tname, shape, dt in [
        ("t_xn", [128, NKD * T], BF16), ("t_xconv", [128, NKD * T], BF16),
        ("t_memloc", [128, NH * T], BF16), ("t_q", [128, NH * T], BF16),
        ("t_og", [128, NH * T], BF16), ("t_gated", [128, NH * T], BF16),
        ("t_h", [128, NKH * TH], BF16), ("t_gamma", [16, T], F32),
        ("t_state", [128, 16], F32), ("t_gin", [128, 32], F32),
        ("t_gbuf", [128, 8 * 32], F32), ("t_w", [16, 104], F32), ("t_can", [128, 4096], F32), ("t_sel", [128, 64], F32),
    ]:
        if tname in taps:
            tap_dram[tname] = nc.dram_tensor(tname, shape, dt, kind="ExternalOutput").ap()

    ctx = ExitStack()
    S = Sched(nc, ctx)
    E = ctx.enter_context
    open_stacks = []

    def finish():
        if "t_can" in tap_dram:
            tc = tap_dram["t_can"]
            for eng2 in ("ve", "ac", "pe", "gp_c", "gp_w"):
                if eng2 in ("gp_c", "gp_w"):
                    S.wait("sy", eng2, S.cur(eng2))
                else:
                    S.wait("sy", eng2, S.cur(eng2))
            for (src_t, r0, r1, c0, c1) in [
                (convw_sb, 0, 128, 0, 64), (convb_sb, 0, 128, 64, 80), (big_sb, 0, 128, 80, 96),
                (bog_sb, 0, 128, 96, 112), (gnw_sb, 0, 128, 112, 128), (gnb_sb, 0, 128, 128, 144),
                (mnw_sb, 0, 128, 144, 160), (mnb_sb, 0, 128, 160, 176), (sg_sb, 0, 128, 176, 220),
                (su_sb, 0, 128, 220, 264), (sd_sb, 0, 128, 264, 280), (mb8_sb, 0, 16, 280, 344),
                (malog_sb, 0, 16, 344, 352), (bg_sb, 0, 16, 352, 353), (cst, 0, 128, 353, 361),
                (gam_sb, 0, 16, 1024, 2048), (gbuf, 0, 128, 2048, 2184), (state_sb, 0, 128, 2184, 2200),
            ]:
                S.do("sy", lambda e, s=src_t, r0=r0, r1=r1, c0=c0, c1=c1: e.dma_start(
                    out=tc[r0:r1, c0:c1], in_=s[0:r1 - r0, 0:c1 - c0],
                    allow_slow_non_contiguous=True), inc="sx", n=16)
        S.wait("sy", "sx", S.cur("sx"))
        S.wait("sy", "gp_c", S.cur("gp_c"))
        blk = ctx.enter_context(nc.Block())

        def run_for(key):
            def body(eng):
                for fn in S.prog[key]:
                    fn(eng)
            return body

        blk.sync(run_for("sy"))
        blk.vector(run_for("ve"))
        blk.scalar(run_for("ac"))
        blk.tensor(run_for("pe"))
        blk.gpsimd(run_for("gp"))
        for stk in reversed(open_stacks):
            stk.close()
        ctx.close()
        return nc

    convw_sb = E(nc.sbuf_tensor("convw_sb", [128, 64], F32))
    convb_sb = E(nc.sbuf_tensor("convb_sb", [128, 16], F32))
    big_sb = E(nc.sbuf_tensor("big_sb", [128, 16], F32))
    bog_sb = E(nc.sbuf_tensor("bog_sb", [128, 16], F32))
    bg_sb = E(nc.sbuf_tensor("bg_sb", [16, 1], F32))
    gnw_sb = E(nc.sbuf_tensor("gnw_sb", [128, 16], F32))
    gnb_sb = E(nc.sbuf_tensor("gnb_sb", [128, 16], F32))
    mnw_sb = E(nc.sbuf_tensor("mnw_sb", [128, 16], F32))
    mnb_sb = E(nc.sbuf_tensor("mnb_sb", [128, 16], F32))
    sg_sb = E(nc.sbuf_tensor("sg_sb", [128, 44], F32))
    su_sb = E(nc.sbuf_tensor("su_sb", [128, 44], F32))
    sd_sb = E(nc.sbuf_tensor("sd_sb", [128, 16], F32))
    oneh2_sb = E(nc.sbuf_tensor("oneh2_sb", [32, NH * 128], BF16))
    selc_sb = E(nc.sbuf_tensor("selc_sb", [128, 8], F32))
    invc_sb = E(nc.sbuf_tensor("invc_sb", [128, 8], F32))
    ones128 = E(nc.sbuf_tensor("ones128", [128, 128], BF16))
    ones2neg = E(nc.sbuf_tensor("ones2neg", [2, 128], BF16))
    cst = E(nc.sbuf_tensor("cst", [128, 8], F32))
    lowsl = [E(nc.sbuf_tensor(f"lowsl{i}", [128, NKD * 128], BF16)) for i in range(6)]
    dsl_lo = [E(nc.sbuf_tensor(f"dsllo{i}", [128, NKH * 128], BF16)) for i in range(2)]
    xb_lo = [E(nc.sbuf_tensor(f"xblo{i}", [128, NT], BF16)) for i in range(2)]
    wgt_sb = E(nc.sbuf_tensor("wgt_sb", [128, NKD * 16], BF16))
    gam_sb = E(nc.sbuf_tensor("gam_sb", [16, T], F32))
    gam_hilo = E(nc.sbuf_tensor("gam_hilo", [32, T], BF16))
    g_in = E(nc.sbuf_tensor("g_in", [128, 32], F32))
    gbuf = E(nc.sbuf_tensor("gbuf", [128, 8 * 32], F32))
    state_sb = E(nc.sbuf_tensor("state_sb", [128, 16], F32))
    xns_hilo = E(nc.sbuf_tensor("xns_hilo", [2, T], BF16))
    hs_hilo = E(nc.sbuf_tensor("hs_hilo", [2, T], BF16))
    rowa = E(nc.sbuf_tensor("rowa", [1, T], F32))
    rowb = E(nc.sbuf_tensor("rowb", [1, T], F32))
    rbf_a = E(nc.sbuf_tensor("rbf_a", [1, T], BF16))
    rbf_b = E(nc.sbuf_tensor("rbf_b", [1, T], BF16))
    small = E(nc.sbuf_tensor("small", [16, 104], F32))
    small2 = E(nc.sbuf_tensor("small2", [128, 48], F32))

    scrA = [E(nc.sbuf_tensor(f"scra{i}", [128, 1040], F32)) for i in range(4)]
    scrb = [E(nc.sbuf_tensor(f"scrb{i}", [128, 1040], BF16)) for i in range(4)]

    psS1 = E(nc.psum_tensor("psS1", [128, 1024], F32))
    psS2 = E(nc.psum_tensor("psS2", [128, 1024], F32))
    psS3 = E(nc.psum_tensor("psS3", [128, 1024], F32))
    psA = E(nc.psum_tensor("psA", [128, 512], F32))
    psB = E(nc.psum_tensor("psB", [128, 512], F32))
    pres = {"S1": Res(), "S2": Res(), "S3": Res(), "A": Res(), "B": Res()}
    pst = {"S1": psS1, "S2": psS2, "S3": psS3, "A": psA, "B": psB}

    stk_scrB = ExitStack()
    open_stacks.append(stk_scrB)
    scrB = [stk_scrB.enter_context(nc.sbuf_tensor(f"scrx{i}", [128, 1040], F32)) for i in range(4)]
    scr = scrA + scrB
    scr_r = [Res() for _ in range(8)]
    scrb_r = [Res() for _ in range(4)]

    marks = {}

    def mark(name, pair):
        marks[name] = pair

    def wmark(eng, name):
        S.wait_pair(eng, marks.get(name))

    def sready(res, pair):
        res.ready = pair
        res.frees = []

    def suse(eng, res_list, fn, ps_reads=(), also=()):
        for r in res_list:
            S.wait_pair(eng, r.ready)
        for p in ps_reads:
            S.wait_pair(eng, pres[p].ready)
        for (sn, v) in also:
            S.wait(eng, sn, v)
        v = S.do(eng, fn, inc=ESEM[eng])
        pr = (ESEM[eng], v)
        for r in res_list:
            r.frees.append(pr)
        for p in ps_reads:
            pres[p].frees.append(pr)
        return pr

    def swrite(eng, res, fn, reads=(), ps_reads=(), also=()):
        S.wait_frees(eng, res)
        for r in reads:
            S.wait_pair(eng, r.ready)
        for p in ps_reads:
            S.wait_pair(eng, pres[p].ready)
        for (sn, v) in also:
            S.wait(eng, sn, v)
        v = S.do(eng, fn, inc=ESEM[eng])
        pr = (ESEM[eng], v)
        sready(res, pr)
        for r in reads:
            r.frees.append(pr)
        for p in ps_reads:
            pres[p].frees.append(pr)
        return pr

    def mm_group(ps_name, mms, sub=None, pre_waits=()):
        r = pres[ps_name]
        S.wait_frees("pe", r)
        for pw in pre_waits:
            if pw is not None:
                S.wait("pe", pw[0], pw[1])
        out_ap = sub if sub is not None else pst[ps_name][:, :]
        n = len(mms)
        lv = None
        for i, (l, rr) in enumerate(mms):
            st, sp = (i == 0), (i == n - 1)
            if sp:
                lv = S.do("pe", lambda e, o=out_ap, l=l, rr=rr, st=st: e.matmul(o, l, rr, start=st, stop=True), inc="pe")
            else:
                S.do("pe", lambda e, o=out_ap, l=l, rr=rr, st=st: e.matmul(o, l, rr, start=st, stop=False))
        sready(r, ("pe", lv))
        return lv

    def consume(eng, ps_names, fn, also=()):
        for p in ps_names:
            S.wait_pair(eng, pres[p].ready)
        for (sn, v) in also:
            S.wait(eng, sn, v)
        v = S.do(eng, fn, inc=ESEM[eng])
        pr = (ESEM[eng], v)
        for p in ps_names:
            pres[p].frees.append(pr)
        return pr

    def wload(slot, slot_r, semn, wt_dram, c0, w, nk=NKD):
        S.wait_frees("gp", slot_r)
        S.do("gp", lambda e, s=slot, src=wt_dram, c0=c0, w=w, nk=nk: e.dma_start(
            out=s[:, 0:nk * w].rearrange("p (k w) -> p k w", w=w),
            in_=src[:, c0:c0 + w].rearrange("(k p) w -> p k w", p=128)), inc=semn, n=16)
        sready(slot_r, (semn, S.cur(semn)))

    for dst, src in [
        (convw_sb, convw), (convb_sb, convb), (big_sb, big), (bog_sb, bog),
        (bg_sb, bg), (gnw_sb, gnw), (gnb_sb, gnb), (mnw_sb, mnw), (mnb_sb, mnb),
        (sg_sb, sgc), (su_sb, suc), (sd_sb, sdc), (oneh2_sb, oneh2),
        (selc_sb, selc), (invc_sb, invc),
    ]:
        S.do("sy", lambda e, d=dst, s=src: e.dma_start(out=d[:, :], in_=s), inc="sx", n=16)
    c_consts = S.cur("sx")
    S.do("gp", lambda e: e.dma_start(
        out=wgt_sb[:, :].rearrange("p (k w) -> p k w", w=16),
        in_=wg_t.rearrange("(k p) w -> p k w", p=128)), inc="gp_w", n=16)
    g_wg = S.cur("gp_w")
    S.do("ve", lambda e: e.memset(ones128[:, :], 1.0), inc="ve")
    v_ones = S.cur("ve")
    S.do("ve", lambda e: e.memset(ones2neg[:, :], -128.0), inc="ve")
    v_ones2 = S.cur("ve")
    S.do("ve", lambda e: e.memset(cst[:, 0:1], EPS))
    S.do("ve", lambda e: e.memset(cst[:, 1:2], 1.0 / D))
    S.do("ve", lambda e: e.memset(cst[:, 2:3], 1.0 / HD))
    S.do("ve", lambda e: e.memset(cst[:, 3:4], 1e-24))
    S.do("ve", lambda e: e.memset(cst[:, 4:5], -80.0), inc="ve")
    v_cst = S.cur("ve")
    ceps, cinvD, cinvHD, c24 = cst[:, 0:1], cst[:, 1:2], cst[:, 2:3], cst[:, 3:4]
    cm80 = cst[:, 4:5]
    S.wait("ac", "ve", v_cst)
    for _eng in ("ve", "ac", "pe", "gp"):
        S.wait(_eng, "sx", c_consts)

    rP12 = ExitStack()
    open_stacks.append(rP12)
    xn_bf = rP12.enter_context(nc.sbuf_tensor("xn_bf", [128, NKD * T], BF16))
    xconv_bf = rP12.enter_context(nc.sbuf_tensor("xconv_bf", [128, NKD * T], BF16))

    r1a = ExitStack()
    open_stacks.append(r1a)
    xt = [r1a.enter_context(nc.sbuf_tensor(f"xt{i}", [128, NT], F32)) for i in range(2)]
    xb = xb_lo
    mu_bc = r1a.enter_context(nc.sbuf_tensor("mu_bc", [128, NT], F32))
    r_bc = r1a.enter_context(nc.sbuf_tensor("r_bc", [128, NT], F32))
    xt_r = [Res(), Res()]
    xb_r = [Res(), Res()]
    mu_r, r_r = Res(), Res()
    xsem = ["xA", "xB"]
    bsem = ["w0", "w1"]

    chunks = [(0, TH), (TH, 2 * TH), (2 * TH, NT)]
    if "nA" in skip:
        S.do("ve", lambda e: e.memset(mu_bc[:, :], 0.0))
        v = S.do("ve", lambda e: e.memset(r_bc[:, :], 1.0), inc="ve")
        sready(mu_r, ("ve", v)); sready(r_r, ("ve", v))
    for k in range(NKD if "nA" not in skip else 0):
        par = k % 2
        tr, br = xt_r[par], xb_r[par]
        S.wait_frees("sy", tr)
        S.do("sy", lambda e, k=k, p=par: e.dma_start(
            out=xt[p][:, :], in_=x_t[k * 128:(k + 1) * 128, :]), inc=xsem[par], n=16)
        sready(tr, (xsem[par], S.cur(xsem[par])))
        S.wait_frees("gp", br)
        S.do("gp", lambda e, k=k, p=par: e.dma_start(
            out=xb[p][:, :], in_=x_t[k * 128:(k + 1) * 128, :]), inc=bsem[par], n=16)
        sready(br, (bsem[par], S.cur(bsem[par])))
        sb = scrb_r[par]
        swrite("ac", sb, lambda e, p=par: e.square(out=scrb[p][:, 0:NT], in_=xt[p][:, :]), reads=[tr])
        S.wait("pe", "ve", v_ones)
        S.wait_pair("pe", br.ready)
        S.wait_pair("pe", sb.ready)
        if k == 0:
            for nm in ("S1", "S2", "A", "B"):
                S.wait_frees("pe", pres[nm])
        lv1 = None
        for ci, (c0, c1) in enumerate(chunks):
            tgt = psA[:, 0:3] if c1 == NT else psS1[:, c0:c1]
            lv1 = S.do("pe", lambda e, k=k, p=par, c0=c0, c1=c1, t=tgt: e.matmul(
                t, ones128[:, :], xb[p][:, c0:c1],
                start=(k == 0), stop=(k == NKD - 1)), inc="pe" if ci == 2 else None)
        br.frees.append(("pe", lv1))
        lv = None
        for ci, (c0, c1) in enumerate(chunks):
            tgt = psB[:, 0:3] if c1 == NT else psS2[:, c0:c1]
            lv = S.do("pe", lambda e, k=k, p=par, c0=c0, c1=c1, t=tgt: e.matmul(
                t, ones128[:, :], scrb[p][:, c0:c1],
                start=(k == 0), stop=(k == NKD - 1)), inc="pe" if ci == 2 else None)
        sb.frees.append(("pe", lv))
        tr.frees.append(("ac", sb.ready[1]))
    if "nA" not in skip:
        for nm in ("S1", "S2", "A", "B"):
            sready(pres[nm], ("pe", lv))
    if "nA" not in skip:
        p1 = consume("ac", ["S1"], lambda e: e.activation(out=mu_bc[:, 0:T], in_=psS1[:, :], func=AF.Copy, scale=cinvD))
        p2 = consume("ac", ["A"], lambda e: e.activation(out=mu_bc[:, T:NT], in_=psA[:, 0:3], func=AF.Copy, scale=cinvD))
        sready(mu_r, ("ac", p2[1]))
        swrite("ve", scr_r[6], lambda e: e.tensor_mul(scr[6][:, 0:NT], mu_bc[:, :], mu_bc[:, :]), reads=[mu_r])
        S.wait_frees("ve", scr_r[7])
        S.wait_pair("ve", pres["S2"].ready)
        S.wait_pair("ve", scr_r[6].ready)
        S.do("ve", lambda e: e.scalar_tensor_tensor(
            out=scr[7][:, 0:T], in0=psS2[:, :], scalar=cinvD, in1=scr[6][:, 0:T],
            op0=ALU.mult, op1=ALU.subtract), inc="ve")
        S.pad("ve")
        S.wait_pair("ve", pres["B"].ready)
        v2 = S.do("ve", lambda e: e.scalar_tensor_tensor(
            out=scr[7][:, T:NT], in0=psB[:, 0:3], scalar=cinvD, in1=scr[6][:, T:NT],
            op0=ALU.mult, op1=ALU.subtract), inc="ve")
        for nm in ("S2", "B"):
            pres[nm].frees.append(("ve", v2))
        scr_r[6].frees.append(("ve", v2))
        sready(scr_r[7], ("ve", v2))
        swrite("ac", scr_r[6], lambda e: e.activation(out=scr[6][:, 0:NT], in_=scr[7][:, 0:NT], func=AF.Sqrt, bias=ceps), reads=[scr_r[7]])
        pr = suse("ve", [scr_r[6]], lambda e: e.reciprocal(out=r_bc[:, :], in_=scr[6][:, 0:NT]))
        sready(r_r, pr)


    for k in range(NKD):
        par = k % 2
        tr = xt_r[par]
        S.wait_frees("sy", tr)
        S.do("sy", lambda e, k=k, p=par: e.dma_start(
            out=xt[p][:, :], in_=x_t[k * 128:(k + 1) * 128, :]), inc=xsem[par], n=16)
        sready(tr, (xsem[par], S.cur(xsem[par])))
        i0, i1, ia = 3 * par, 3 * par + 1, (2 if par == 0 else 6)
        t1r, xnr, accr = scr_r[i0], scr_r[i1], scr_r[ia]
        swrite("ve", t1r, lambda e, p=par, i0=i0: e.tensor_sub(scr[i0][:, 0:NT], xt[p][:, :], mu_bc[:, :]), reads=[tr, mu_r])
        swrite("ve", xnr, lambda e, i0=i0, i1=i1: e.tensor_mul(scr[i1][:, 0:NT], scr[i0][:, 0:NT], r_bc[:, :]), reads=[t1r, r_r])
        suse("ac", [xnr], lambda e, k=k, i1=i1: e.copy(out=xn_bf[:, k * T:(k + 1) * T], in_=scr[i1][:, HALO:NT]))
        swrite("ve", t1r, lambda e, k=k, i0=i0, i1=i1: e.tensor_scalar_mul(
            scr[i0][:, 0:T], scr[i1][:, 0:T], convw_sb[:, 4 * k:4 * k + 1]), reads=[xnr])
        swrite("ve", accr, lambda e, k=k, i0=i0, i1=i1, ia=ia: e.scalar_tensor_tensor(
            out=scr[ia][:, 0:T], in0=scr[i1][:, 1:T + 1], scalar=convw_sb[:, 4 * k + 1:4 * k + 2],
            in1=scr[i0][:, 0:T], op0=ALU.mult, op1=ALU.add), reads=[xnr, t1r])
        swrite("ve", t1r, lambda e, k=k, i0=i0, i1=i1, ia=ia: e.scalar_tensor_tensor(
            out=scr[i0][:, 0:T], in0=scr[i1][:, 2:T + 2], scalar=convw_sb[:, 4 * k + 2:4 * k + 3],
            in1=scr[ia][:, 0:T], op0=ALU.mult, op1=ALU.add), reads=[xnr, accr])
        swrite("ve", accr, lambda e, k=k, i0=i0, i1=i1, ia=ia: e.scalar_tensor_tensor(
            out=scr[ia][:, 0:T], in0=scr[i1][:, 3:T + 3], scalar=convw_sb[:, 4 * k + 3:4 * k + 4],
            in1=scr[i0][:, 0:T], op0=ALU.mult, op1=ALU.add), reads=[xnr, t1r])
        suse("ac", [accr], lambda e, k=k, ia=ia: e.activation(
            out=xconv_bf[:, k * T:(k + 1) * T], in_=scr[ia][:, 0:T], func=AF.Silu, bias=convb_sb[:, k:k + 1]))
    mark("xn", ("ac", S.cur("ac")))
    mark("xconv", ("ac", S.cur("ac")))

    for hf in range(2 if "nS" not in skip else 0):
        nm = "A" if hf == 0 else "B"
        mms = [(ones128[:, 0:1], xn_bf[:, k * T + hf * TH:k * T + hf * TH + TH]) for k in range(NKD)]
        mm_group(nm, mms, sub=pst[nm][0:1, :], pre_waits=[marks["xn"]])
    if "nS" not in skip:
        consume("ac", ["A"], lambda e: e.copy(out=rowa[:, 0:TH], in_=psA[0:1, :]))
        pa = consume("ac", ["B"], lambda e: e.copy(out=rowa[:, TH:T], in_=psB[0:1, :]))
        S.wait_pair("ve", pa)
        S.do("ve", lambda e: e.tensor_copy(out=rbf_a[:, :], in_=rowa[:, :]))
        S.pad("ve")
        S.do("ve", lambda e: e.tensor_sub(rowb[:, :], rowa[:, :], rbf_a[:, :]))
        S.pad("ve")
        S.do("ve", lambda e: e.tensor_copy(out=rbf_b[:, :], in_=rowb[:, :]), inc="ve")
        v_xns = S.cur("ve")
        S.wait("sy", "ve", v_xns)
        S.do("sy", lambda e: e.dma_start(out=xns_hilo[0:1, :], in_=rbf_a[:, :]), inc="sx", n=16)
        S.do("sy", lambda e: e.dma_start(out=xns_hilo[1:2, :], in_=rbf_b[:, :]), inc="sx", n=16)
    c_xns = S.cur("sx")

    if "nS" not in skip:
        wmark("sy", "xn")
        S.do("sy", lambda e: e.dma_start(out=xn_sp.ap(), in_=xn_bf[:, :]), inc="sx", n=16)
    c_spill = S.cur("sx")
    if "t_xn" in tap_dram:
        S.do("sy", lambda e: e.dma_start(out=tap_dram["t_xn"], in_=xn_bf[:, :]), inc="sx", n=16)
    if "t_xconv" in tap_dram:
        wmark("sy", "xconv")
        S.do("sy", lambda e: e.dma_start(out=tap_dram["t_xconv"], in_=xconv_bf[:, :]), inc="sx", n=16)
    r1a.close()
    open_stacks.remove(r1a)
    if upto <= 1:
        return finish()

    for hf in range(2 if "nG" not in skip else 0):
        sub = psS1[0:16, hf * TH:(hf + 1) * TH]
        mms = [(wgt_sb[:, k * 16:(k + 1) * 16],
                xconv_bf[:, k * T + hf * TH:k * T + hf * TH + TH]) for k in range(NKD)]
        mm_group("S1", mms, sub=sub, pre_waits=[("gp_w", g_wg), marks["xconv"]])
    if "nG" in skip:
        c_gh = S.cur("sx")
        v_lpt = S.cur("ve")
        pg = None
    _g_on = "nG" not in skip
    pg = pg if not _g_on else consume("ac", ["S1"], lambda e: e.activation(
        out=gam_sb[:, :], in_=psS1[0:16, :], func=AF.Sigmoid, bias=bg_sb[:, :]), also=[("sx", c_consts)])
    if _g_on:
        S.wait_pair("ve", pg)
        S.do("ve", lambda e: e.tensor_copy(out=scrb[0][0:16, 0:T], in_=gam_sb[:, :]))
        S.pad("ve")
        S.do("ve", lambda e: e.tensor_sub(scr[0][0:16, 0:T], gam_sb[:, :], scrb[0][0:16, 0:T]))
        S.pad("ve")
        S.do("ve", lambda e: e.tensor_copy(out=scrb[1][0:16, 0:T], in_=scr[0][0:16, 0:T]), inc="ve")
        v_gh = S.cur("ve")
        S.wait("sy", "ve", v_gh)
        S.do("sy", lambda e: e.dma_start(out=gam_hilo[0:16, :], in_=scrb[0][0:16, 0:T]), inc="sx", n=16)
        S.do("sy", lambda e: e.dma_start(out=gam_hilo[16:32, :], in_=scrb[1][0:16, 0:T]), inc="sx", n=16)
        c_gh = S.cur("sx")
        v_lpt = S.cur("ve")
        if "t_gamma" in tap_dram:
            S.wait_pair("sy", pg)
            S.do("sy", lambda e: e.dma_start(out=tap_dram["t_gamma"], in_=gam_sb[:, :]), inc="sx", n=16)
    if upto <= 2:
        return finish()

    r_slots = ExitStack()
    open_stacks.append(r_slots)
    ksl = [lowsl[i] for i in range(2)]
    vsl = [lowsl[2 + i] for i in range(2)]
    isl = [lowsl[4 + i] for i in range(2)]
    ksl_r = [Res(), Res()]
    vsl_r = [Res(), Res()]
    isl_r = [Res(), Res()]

    for h in range(NH):
        par = h % 2
        if "wl" not in skip:
            wload(ksl[par], ksl_r[par], f"w{2 + par}", wk_t, h * 128, 128)
            wload(vsl[par], vsl_r[par], f"w{4 + par}", wv_t, h * 128, 128)
            wload(isl[par], isl_r[par], f"w{6 + par}", wig_t, h * 128, 128)
        if "mm" in skip:
            continue
        for hf in range(2):
            xnsl = [xn_bf[:, k * T + hf * TH:k * T + hf * TH + TH] for k in range(NKD)]
            xcsl = [xconv_bf[:, k * T + hf * TH:k * T + hf * TH + TH] for k in range(NKD)]
            mm_group("A", [(ksl[par][:, k * 128:(k + 1) * 128], xnsl[k]) for k in range(NKD)],
                     pre_waits=[ksl_r[par].ready, marks["xn"]])
            ksl_r[par].frees.append(pres["A"].ready)
            mm_group("B", [(vsl[par][:, k * 128:(k + 1) * 128], xnsl[k]) for k in range(NKD)],
                     pre_waits=[vsl_r[par].ready])
            vsl_r[par].frees.append(pres["B"].ready)
            mm_group("S3", [(isl[par][:, k * 128:(k + 1) * 128], xcsl[k]) for k in range(NKD)],
                     sub=psS3[:, hf * TH:(hf + 1) * TH],
                     pre_waits=[isl_r[par].ready, marks["xconv"]])
            isl_r[par].frees.append(pres["S3"].ready)
            if "evac" in skip:
                continue
            if hf == 0:
                S.wait_frees("ac", scr_r[0]); scr_r[0].frees = []
                S.wait_frees("ac", scr_r[1]); scr_r[1].frees = []
                S.wait_frees("ac", scrb_r[2]); scrb_r[2].frees = []
                S.wait_frees("ve", scr_r[2]); scr_r[2].frees = []
            pr = consume("ac", ["B"], lambda e, hf=hf: e.copy(out=scr[0][:, hf * TH:(hf + 1) * TH], in_=psB[:, :]))
            scr_r[0].ready = pr
            pr = consume("ac", ["S3"], lambda e, hf=hf, h=h: e.activation(
                out=scr[1][:, hf * TH:(hf + 1) * TH], in_=psS3[:, hf * TH:(hf + 1) * TH],
                func=AF.Sigmoid, bias=big_sb[:, h:h + 1]))
            scr_r[1].ready = pr
            pr = consume("ac", ["A"], lambda e, hf=hf: e.square(
                out=scrb[2][:, hf * TH:(hf + 1) * TH], in_=psA[:, :]))
            scrb_r[2].ready = pr
            S.wait_pair("ve", scr_r[0].ready)
            pr = consume("ve", ["A"], lambda e, hf=hf: e.tensor_mul(
                scr[2][:, hf * TH:(hf + 1) * TH], psA[:, :], scr[0][:, hf * TH:(hf + 1) * TH]))
            scr_r[0].frees.append(pr)
            scr_r[2].ready = pr
        if "bt" in skip:
            continue
        r = pres["S2"]
        S.wait_frees("pe", r)
        S.wait_pair("pe", scrb_r[2].ready)
        for hf in range(2):
            lv = S.do("pe", lambda e, hf=hf: e.matmul(
                psS2[:, hf * TH:(hf + 1) * TH], ones128[:, :], scrb[2][:, hf * TH:(hf + 1) * TH],
                start=True, stop=True), inc="pe")
        sready(r, ("pe", lv))
        scrb_r[2].frees.append(("pe", lv))
        swrite("ve", scr_r[3], lambda e: e.tensor_scalar_max(scr[3][:, 0:T], psS2[:, :], c24), ps_reads=["S2"])
        swrite("ac", scr_r[4], lambda e: e.activation(out=scr[4][:, 0:T], in_=scr[3][:, 0:T], func=AF.Sqrt), reads=[scr_r[3]])
        swrite("ve", scr_r[3], lambda e: e.reciprocal(out=scr[3][:, 0:T], in_=scr[4][:, 0:T]), reads=[scr_r[4]])
        swrite("ve", scr_r[5], lambda e: e.tensor_mul(scr[5][:, 0:T], scr[2][:, 0:T], scr[1][:, 0:T]), reads=[scr_r[2], scr_r[1]])
        swrite("ve", scr_r[4], lambda e: e.tensor_mul(scr[4][:, 0:T], scr[5][:, 0:T], scr[3][:, 0:T]), reads=[scr_r[5], scr_r[3]])
        if "gbc" in skip:
            continue
        r = pres["S1"]
        S.wait_frees("pe", r)
        S.wait("pe", "sx", c_gh)
        for hf in range(2):
            lv = S.do("pe", lambda e, h=h, hf=hf: e.matmul(
                psS1[:, hf * TH:(hf + 1) * TH], oneh2_sb[:, h * 128:(h + 1) * 128],
                gam_hilo[:, hf * TH:(hf + 1) * TH], start=True, stop=True), inc="pe")
        sready(r, ("pe", lv))
        if "scan1" in skip:
            continue
        S.wait_frees("ve", scr_r[5])
        S.wait_pair("ve", pres["S1"].ready)
        S.wait_frees("ve", scr_r[2])
        vz = S.do("ve", lambda e: e.memset(scr[2][:, 0:T], 0.0))
        v = S.do("ve", lambda e: e.tensor_tensor_scan(
            out=scr[5][:, 0:T], data0=psS1[:, :], data1=scr[2][:, 0:T],
            initial=1.0, op0=ALU.mult, op1=ALU.add), inc="ve")
        pres["S1"].frees.append(("ve", v))
        sready(scr_r[5], ("ve", v))
        sready(scr_r[2], ("ve", v))
        S.pad("ve")
        suse("ve", [scr_r[5]], lambda e, h=h: e.tensor_copy(
            out=g_in[:, 16 + h:17 + h], in_=scr[5][:, T - 1:T]))
        S.pad("ve")
        S.wait_frees("ve", scrb_r[3])
        S.wait_pair("ve", scr_r[4].ready)
        v = S.do("ve", lambda e: e.tensor_tensor_scan(
            out=scrb[3][:, 0:T], data0=psS1[:, :], data1=scr[4][:, 0:T],
            initial=0.0, op0=ALU.mult, op1=ALU.add), inc="ve")
        pres["S1"].frees.append(("ve", v))
        scr_r[4].frees.append(("ve", v))
        sready(scrb_r[3], ("ve", v))
        S.pad("ve")
        suse("ve", [scrb_r[3]], lambda e, h=h: e.tensor_copy(
            out=g_in[:, h:h + 1], in_=scrb[3][:, T - 1:T]))
        S.pad("ve")
        S.wait_pair("sy", scrb_r[3].ready)
        sv = S.do("sy", lambda e, h=h: e.dma_start(
            out=ml_sp.ap()[:, h * T:(h + 1) * T], in_=scrb[3][:, 0:T]), inc="sx", n=16)
        scrb_r[3].frees.append(("sx", sv))
    c_ml = S.cur("sx")
    mark("gin_mf", ("ve", S.cur("ve")))
    for tn, sp_t2, cw2 in (("t_memloc", ml_sp, c_ml),):
        if tn in tap_dram and upto <= 3:
            S.wait("gp", "sx", cw2)
            S.do("gp", lambda e, d=tap_dram[tn], s=sp_t2: e.dma_start(out=d, in_=s.ap()), inc="gp_c", n=16)
    if upto <= 3:
        if "t_xconv" in tap_dram and "latetap" in skip:
            for sn in ("w2", "w3", "w4", "w5", "w6", "w7"):
                S.wait("sy", sn, S.cur(sn))
            S.do("sy", lambda e: e.dma_start(out=tap_dram["t_xconv"], in_=xn_bf[:, :]), inc="sx", n=16)
        return finish()

    wmark("ve", "gin_mf")
    S.wait("ve", "ve", v_lpt)
    S.do("ve", lambda e: e.engine_nop(), inc="ve")
    v_gin = S.cur("ve")
    S.wait("gp", "ve", v_gin)
    S.do("gp", lambda e: e.dma_start(out=cc_in.ap(), in_=g_in[:, :]), inc="gp_c", n=16)
    S.wait("gp", "gp_c", S.cur("gp_c"))
    S.do("gp", lambda e: e.collective_compute(
        "AllGather", ALU.bypass, replica_groups=[list(range(NCORES))],
        ins=[cc_in.ap()], outs=[cc_out.ap()]), inc="cc")
    v_cc = S.cur("cc")
    if "t_gin" in tap_dram:
        S.wait("sy", "ve", v_gin)
        S.do("sy", lambda e: e.dma_start(out=tap_dram["t_gin"], in_=g_in[:, :]), inc="sx", n=16)

    for m in range(NH):
        par = m % 2
        wload(ksl[par], ksl_r[par], f"w{2 + par}", wog_t, m * 128, 128)
        for hf in range(2):
            nm = "A" if hf == 0 else "B"
            mm_group(nm, [(ksl[par][:, k * 128:(k + 1) * 128],
                           xconv_bf[:, k * T + hf * TH:k * T + hf * TH + TH]) for k in range(NKD)],
                     pre_waits=[ksl_r[par].ready, marks["xconv"]])
            ksl_r[par].frees.append(pres[nm].ready)
            if hf == 0:
                S.wait_frees("ac", scrb_r[3])
                scrb_r[3].frees = []
            pr = consume("ac", [nm], lambda e, hf=hf, m=m, nm=nm: e.activation(
                out=scrb[3][:, hf * TH:(hf + 1) * TH], in_=pst[nm][:, :],
                func=AF.Sigmoid, bias=bog_sb[:, m:m + 1]))
            scrb_r[3].ready = pr
        S.wait_pair("sy", scrb_r[3].ready)
        sv = S.do("sy", lambda e, m=m: e.dma_start(
            out=og_sp.ap()[:, m * T:(m + 1) * T], in_=scrb[3][:, 0:T]), inc="sx", n=16)
        scrb_r[3].frees.append(("sx", sv))
    c_og = S.cur("sx")

    for m in range(NH):
        par = m % 2
        wload(vsl[par], vsl_r[par], f"w{4 + par}", wq_t, m * 128, 128)
        for hf in range(2):
            nm = "A" if hf == 0 else "B"
            mm_group(nm, [(vsl[par][:, k * 128:(k + 1) * 128],
                           xn_bf[:, k * T + hf * TH:k * T + hf * TH + TH]) for k in range(NKD)],
                     pre_waits=[vsl_r[par].ready, marks["xn"]])
            vsl_r[par].frees.append(pres[nm].ready)
            if hf == 0:
                S.wait_frees("ac", scrb_r[3])
                scrb_r[3].frees = []
            pr = consume("ac", [nm], lambda e, hf=hf, nm=nm: e.copy(
                out=scrb[3][:, hf * TH:(hf + 1) * TH], in_=pst[nm][:, :]))
            scrb_r[3].ready = pr
        S.wait_pair("sy", scrb_r[3].ready)
        sv = S.do("sy", lambda e, m=m: e.dma_start(
            out=q_sp.ap()[:, m * T:(m + 1) * T], in_=scrb[3][:, 0:T]), inc="sx", n=16)
        scrb_r[3].frees.append(("sx", sv))
    c_q = S.cur("sx")
    r_slots.close()
    open_stacks.remove(r_slots)
    rP12.close()
    open_stacks.remove(rP12)
    if upto <= 4:
        for tn, sp_t2, cw2 in (("t_memloc", ml_sp, c_ml), ("t_og", og_sp, c_og), ("t_q", q_sp, c_q)):
            if tn in tap_dram:
                S.wait("gp", "sx", cw2)
                S.do("gp", lambda e, d=tap_dram[tn], s=sp_t2: e.dma_start(out=d, in_=s.ap()), inc="gp_c", n=16)
        return finish()

    S.wait("gp", "cc", v_cc)
    S.do("gp", lambda e: e.dma_start(
        out=gbuf[:, :].rearrange("p (i c) -> p i c", c=32),
        in_=cc_out.ap().rearrange("(i p) c -> p i c", p=128)), inc="gp_c", n=16)
    g_gbuf = S.cur("gp_c")
    if "t_gbuf" in tap_dram:
        S.wait("sy", "gp_c", g_gbuf)
        S.do("sy", lambda e: e.dma_start(out=tap_dram["t_gbuf"], in_=gbuf[:, :]), inc="sx", n=16)
    S.wait("ve", "gp_c", g_gbuf)
    S.wait("ve", "sx", c_consts)
    S.do("ve", lambda e: e.memset(state_sb[:, :], 0.0))
    S.pad("ve")
    for i in range(NCORES):
        mf = gbuf[:, i * 32:i * 32 + 16]
        pt = gbuf[:, i * 32 + 16:i * 32 + 32]
        S.do("ve", lambda e, i=i, pt=pt: e.tensor_scalar(
            out=small2[:, 0:16], in0=pt, scalar1=selc_sb[:, i:i + 1], scalar2=invc_sb[:, i:i + 1],
            op0=ALU.mult, op1=ALU.add))
        S.pad("ve")
        S.do("ve", lambda e, i=i, mf=mf: e.tensor_scalar_mul(small2[:, 16:32], mf, selc_sb[:, i:i + 1]))
        S.pad("ve")
        S.do("ve", lambda e: e.tensor_mul(small2[:, 32:48], state_sb[:, :], small2[:, 0:16]))
        S.pad("ve")
        S.do("ve", lambda e: e.tensor_add(state_sb[:, :], small2[:, 32:48], small2[:, 16:32]))
        S.pad("ve")
    stv = ("ve", S.do("ve", lambda e: e.engine_nop(), inc="ve"))
    if "t_sel" in tap_dram:
        S.wait_pair("sy", stv)
        S.do("sy", lambda e: e.dma_start(out=tap_dram["t_sel"][:, 0:8], in_=selc_sb[:, :]), inc="sx", n=16)
        S.do("sy", lambda e: e.dma_start(out=tap_dram["t_sel"][:, 8:16], in_=invc_sb[:, :]), inc="sx", n=16)
        S.do("sy", lambda e: e.dma_start(out=tap_dram["t_sel"][:, 16:64], in_=small2[:, :]), inc="sx", n=16)
    mark("state", stv)
    if "t_w" in tap_dram:
        S.wait_pair("sy", stv)
        S.do("sy", lambda e: e.dma_start(out=tap_dram["t_w"], in_=small[:, :]), inc="sx", n=16)
    if "t_state" in tap_dram:
        S.wait_pair("sy", stv)
        S.do("sy", lambda e: e.dma_start(out=tap_dram["t_state"], in_=state_sb[:, :]), inc="sx", n=16)

    r5 = ExitStack()
    open_stacks.append(r5)
    qh = [r5.enter_context(nc.sbuf_tensor(f"qh{i}", [128, T], BF16)) for i in range(2)]
    oh = [r5.enter_context(nc.sbuf_tensor(f"oh{i}", [128, T], BF16)) for i in range(2)]
    mh = [r5.enter_context(nc.sbuf_tensor(f"mh{i}", [128, T], BF16)) for i in range(2)]
    qh_r = [Res(), Res()]
    oh_r = [Res(), Res()]
    mh_r = [Res(), Res()]
    zer_r = scr_r[7]
    swrite("ve", zer_r, lambda e: e.memset(scr[7][:, 0:T], 0.0))
    S.wait("sy", "sx", c_q)
    for h in range(NH):
        par = h % 2
        for (tiles, rr, sp, semn) in ((qh, qh_r, q_sp, f"w{2 + par}"), (oh, oh_r, og_sp, f"w{4 + par}"), (mh, mh_r, ml_sp, f"w{6 + par}")):
            S.wait_frees("sy", rr[par])
            S.do("sy", lambda e, t=tiles[par], s=sp, h=h: e.dma_start(
                out=t[:, :], in_=s.ap()[:, h * T:(h + 1) * T]), inc=semn, n=16)
            sready(rr[par], (semn, S.cur(semn)))
        r = pres["S1"]
        S.wait_frees("pe", r)
        for hf in range(2):
            lv = S.do("pe", lambda e, h=h, hf=hf: e.matmul(
                psS1[:, hf * TH:(hf + 1) * TH], oneh2_sb[:, h * 128:(h + 1) * 128],
                gam_hilo[:, hf * TH:(hf + 1) * TH], start=True, stop=True), inc="pe")
        sready(r, ("pe", lv))
        S.wait_frees("ve", scr_r[0])
        S.wait_pair("ve", pres["S1"].ready)
        S.wait_pair("ve", zer_r.ready)
        v = S.do("ve", lambda e: e.tensor_tensor_scan(
            out=scr[0][:, 0:T], data0=psS1[:, :], data1=scr[7][:, 0:T],
            initial=1.0, op0=ALU.mult, op1=ALU.add), inc="ve")
        pres["S1"].frees.append(("ve", v))
        zer_r.frees.append(("ve", v))
        sready(scr_r[0], ("ve", v))
        swrite("ve", scr_r[1], lambda e, h=h, p=par: e.scalar_tensor_tensor(
            out=scr[1][:, 0:T], in0=scr[0][:, 0:T], scalar=state_sb[:, h:h + 1],
            in1=mh[p][:, :], op0=ALU.mult, op1=ALU.add),
            reads=[scr_r[0], mh_r[par]], also=[marks["state"]])
        mh_r[par].frees.append(scr_r[1].ready)
        memr = scr_r[1]
        swrite("ac", scrb_r[0], lambda e: e.copy(out=scrb[0][:, 0:T], in_=scr[1][:, 0:T]), reads=[memr])
        swrite("ac", scrb_r[1], lambda e: e.square(out=scrb[1][:, 0:T], in_=scr[1][:, 0:T]), reads=[memr])
        for (nm, sbr, sbuf) in (("S2", scrb_r[0], scrb[0]), ("S3", scrb_r[1], scrb[1])):
            r = pres[nm]
            S.wait_frees("pe", r)
            S.wait_pair("pe", sbr.ready)
            for hf in range(2):
                lv = S.do("pe", lambda e, nm=nm, hf=hf, sb=sbuf: e.matmul(
                    pst[nm][:, hf * TH:(hf + 1) * TH], ones128[:, :], sb[:, hf * TH:(hf + 1) * TH],
                    start=True, stop=True), inc="pe")
            sready(r, ("pe", lv))
            sbr.frees.append(("pe", lv))
        swrite("ac", scr_r[2], lambda e: e.activation(out=scr[2][:, 0:T], in_=psS2[:, :], func=AF.Copy, scale=cinvHD), ps_reads=["S2"])
        swrite("ve", scr_r[3], lambda e: e.tensor_mul(scr[3][:, 0:T], scr[2][:, 0:T], scr[2][:, 0:T]), reads=[scr_r[2]])
        S.wait_frees("ve", scr_r[5])
        S.wait_pair("ve", pres["S3"].ready)
        S.wait_pair("ve", scr_r[3].ready)
        v = S.do("ve", lambda e: e.scalar_tensor_tensor(
            out=scr[5][:, 0:T], in0=psS3[:, :], scalar=cinvHD, in1=scr[3][:, 0:T],
            op0=ALU.mult, op1=ALU.subtract), inc="ve")
        pres["S3"].frees.append(("ve", v))
        scr_r[3].frees.append(("ve", v))
        sready(scr_r[5], ("ve", v))
        swrite("ac", scr_r[4], lambda e: e.activation(out=scr[4][:, 0:T], in_=scr[5][:, 0:T], func=AF.Sqrt, bias=ceps), reads=[scr_r[5]])
        swrite("ve", scr_r[5], lambda e: e.reciprocal(out=scr[5][:, 0:T], in_=scr[4][:, 0:T]), reads=[scr_r[4]])
        swrite("ve", scr_r[3], lambda e: e.tensor_sub(scr[3][:, 0:T], scr[1][:, 0:T], scr[2][:, 0:T]), reads=[memr, scr_r[2]])
        swrite("ve", scr_r[4], lambda e: e.tensor_mul(scr[4][:, 0:T], scr[3][:, 0:T], scr[5][:, 0:T]), reads=[scr_r[3], scr_r[5]])
        swrite("ve", scr_r[3], lambda e, h=h: e.tensor_scalar(
            out=scr[3][:, 0:T], in0=scr[4][:, 0:T], scalar1=mnw_sb[:, h:h + 1], scalar2=mnb_sb[:, h:h + 1],
            op0=ALU.mult, op1=ALU.add), reads=[scr_r[4]])
        swrite("ve", scr_r[6], lambda e, p=par: e.tensor_mul(scr[6][:, 0:T], scr[3][:, 0:T], qh[p][:, :]),
               reads=[scr_r[3], qh_r[par]])
        ur = scr_r[6]
        swrite("ac", scrb_r[0], lambda e: e.copy(out=scrb[0][:, 0:T], in_=scr[6][:, 0:T]), reads=[ur])
        swrite("ac", scrb_r[1], lambda e: e.square(out=scrb[1][:, 0:T], in_=scr[6][:, 0:T]), reads=[ur])
        pr = swrite("ac", scrb_r[2], lambda e, p=par: e.square(out=scrb[2][:, 0:T], in_=qh[p][:, :]), reads=[qh_r[par]])
        qh_r[par].frees.append(pr)
        for (nm, sbr, sbuf) in (("S2", scrb_r[0], scrb[0]), ("S3", scrb_r[1], scrb[1]), ("S1", scrb_r[2], scrb[2])):
            r = pres[nm]
            S.wait_frees("pe", r)
            S.wait_pair("pe", sbr.ready)
            for hf in range(2):
                lv = S.do("pe", lambda e, nm=nm, hf=hf, sb=sbuf: e.matmul(
                    pst[nm][:, hf * TH:(hf + 1) * TH], ones128[:, :], sb[:, hf * TH:(hf + 1) * TH],
                    start=True, stop=True), inc="pe")
            sready(r, ("pe", lv))
            sbr.frees.append(("pe", lv))
        swrite("ve", scr_r[3], lambda e: e.tensor_scalar_max(scr[3][:, 0:T], psS1[:, :], c24), ps_reads=["S1"])
        swrite("ac", scr_r[2], lambda e: e.activation(out=scr[2][:, 0:T], in_=scr[3][:, 0:T], func=AF.Sqrt), reads=[scr_r[3]])
        swrite("ve", scr_r[0], lambda e: e.reciprocal(out=scr[0][:, 0:T], in_=scr[2][:, 0:T]), reads=[scr_r[2]])
        rqr = scr_r[0]
        swrite("ac", scr_r[4], lambda e: e.activation(out=scr[4][:, 0:T], in_=psS2[:, :], func=AF.Copy, scale=cinvHD), ps_reads=["S2"])
        swrite("ve", scr_r[3], lambda e: e.tensor_mul(scr[3][:, 0:T], scr[4][:, 0:T], scr[4][:, 0:T]), reads=[scr_r[4]])
        S.wait_frees("ve", scr_r[5])
        S.wait_pair("ve", pres["S3"].ready)
        S.wait_pair("ve", scr_r[3].ready)
        v = S.do("ve", lambda e: e.scalar_tensor_tensor(
            out=scr[5][:, 0:T], in0=psS3[:, :], scalar=cinvHD, in1=scr[3][:, 0:T],
            op0=ALU.mult, op1=ALU.subtract), inc="ve")
        pres["S3"].frees.append(("ve", v))
        scr_r[3].frees.append(("ve", v))
        sready(scr_r[5], ("ve", v))
        swrite("ve", scr_r[1], lambda e: e.tensor_mul(scr[1][:, 0:T], scr[4][:, 0:T], scr[0][:, 0:T]), reads=[scr_r[4], rqr])
        swrite("ve", scr_r[2], lambda e: e.tensor_mul(scr[2][:, 0:T], scr[0][:, 0:T], scr[0][:, 0:T]), reads=[rqr])
        swrite("ve", scr_r[4], lambda e: e.tensor_mul(scr[4][:, 0:T], scr[2][:, 0:T], scr[5][:, 0:T]), reads=[scr_r[2], scr_r[5]])
        swrite("ac", scr_r[2], lambda e: e.activation(out=scr[2][:, 0:T], in_=scr[4][:, 0:T], func=AF.Sqrt, bias=ceps), reads=[scr_r[4]])
        swrite("ve", scr_r[5], lambda e: e.reciprocal(out=scr[5][:, 0:T], in_=scr[2][:, 0:T]), reads=[scr_r[2]])
        swrite("ve", scr_r[2], lambda e: e.tensor_mul(scr[2][:, 0:T], scr[6][:, 0:T], scr[0][:, 0:T]), reads=[ur, rqr])
        swrite("ve", scr_r[4], lambda e: e.tensor_sub(scr[4][:, 0:T], scr[2][:, 0:T], scr[1][:, 0:T]), reads=[scr_r[2], scr_r[1]])
        swrite("ve", scr_r[2], lambda e: e.tensor_mul(scr[2][:, 0:T], scr[4][:, 0:T], scr[5][:, 0:T]), reads=[scr_r[4], scr_r[5]])
        swrite("ve", scr_r[4], lambda e, h=h: e.tensor_scalar(
            out=scr[4][:, 0:T], in0=scr[2][:, 0:T], scalar1=gnw_sb[:, h:h + 1], scalar2=gnb_sb[:, h:h + 1],
            op0=ALU.mult, op1=ALU.add), reads=[scr_r[2]])
        pr = swrite("ve", scrb_r[3], lambda e, p=par: e.tensor_mul(
            scrb[3][:, 0:T], scr[4][:, 0:T], oh[p][:, :]), reads=[scr_r[4], oh_r[par]])
        oh_r[par].frees.append(pr)
        S.wait_pair("sy", scrb_r[3].ready)
        sv = S.do("sy", lambda e, h=h: e.dma_start(
            out=gt_sp.ap()[:, h * T:(h + 1) * T], in_=scrb[3][:, 0:T]), inc="sx", n=16)
        scrb_r[3].frees.append(("sx", sv))
    c_gt = S.cur("sx")
    r5.close()
    open_stacks.remove(r5)
    stk_scrB.close()
    open_stacks.remove(stk_scrB)

    for tn, sp_t, cwait in (("t_memloc", ml_sp, c_ml), ("t_og", og_sp, c_og),
                            ("t_q", q_sp, c_q), ("t_gated", gt_sp, c_gt)):
        if tn in tap_dram:
            S.wait("gp", "sx", cwait)
            S.do("gp", lambda e, d=tap_dram[tn], s=sp_t: e.dma_start(out=d, in_=s.ap()), inc="gp_c", n=16)
    if upto <= 5:
        return finish()

    r67 = ExitStack()
    open_stacks.append(r67)
    h_bf = r67.enter_context(nc.sbuf_tensor("h_bf", [128, NKH * TH], BF16))
    xnh = r67.enter_context(nc.sbuf_tensor("xnh", [128, NKD * TH], BF16))
    gth = r67.enter_context(nc.sbuf_tensor("gth", [128, NKD * TH], BF16))
    gsl = [lowsl[0], lowsl[1]]
    usl = [lowsl[2], lowsl[3]]
    dsl = dsl_lo
    osl = [lowsl[4], lowsl[5]]
    gsl_r = [Res(), Res()]
    usl_r = [Res(), Res()]
    dsl_r = [Res(), Res()]
    osl_r = [Res(), Res()]
    xres = [r67.enter_context(nc.sbuf_tensor(f"xres{i}", [128, TH], F32)) for i in range(2)]
    yout = [r67.enter_context(nc.sbuf_tensor(f"yout{i}", [128, TH], F32)) for i in range(2)]
    xres_r = [Res(), Res()]
    yout_r = [Res(), Res()]
    xnh_r = Res()
    gth_r = Res()
    hm = Res()

    S.wait("gp", "pe", S.cur("pe"))
    for HF in range(2):
        S.wait_frees("sy", xnh_r)
        S.wait("sy", "sx", c_spill)
        S.do("sy", lambda e, HF=HF: e.dma_start(
            out=xnh[:, :].rearrange("p (k t) -> p k t", t=TH),
            in_=xn_sp.ap().rearrange("p (k t) -> p k t", t=T)[:, :, HF * TH:(HF + 1) * TH]),
            inc="sx", n=16)
        sready(xnh_r, ("sx", S.cur("sx")))
        S.wait_frees("sy", gth_r)
        S.wait("sy", "sx", c_gt)
        S.do("sy", lambda e, HF=HF: e.dma_start(
            out=gth[:, :].rearrange("p (k t) -> p k t", t=TH),
            in_=gt_sp.ap().rearrange("p (k t) -> p k t", t=T)[:, :, HF * TH:(HF + 1) * TH]),
            inc="sx", n=16)
        sready(gth_r, ("sx", S.cur("sx")))
        S.wait_frees("ve", hm)
        hm.frees = []
        hlast = None
        for m in range(NKH):
            par = m % 2
            wload(gsl[par], gsl_r[par], f"w{par}", wqg_t, m * 128, 128)
            wload(usl[par], usl_r[par], f"w{2 + par}", wqu_t, m * 128, 128)
            mm_group("A", [(gsl[par][:, k * 128:(k + 1) * 128], xnh[:, k * TH:(k + 1) * TH]) for k in range(NKD)]
                     + [(ones2neg[:, :], xns_hilo[:, HF * TH:(HF + 1) * TH])],
                     pre_waits=[gsl_r[par].ready, xnh_r.ready, ("ve", v_ones2), ("sx", c_xns)])
            gsl_r[par].frees.append(pres["A"].ready)
            xnh_r.frees.append(pres["A"].ready)
            mm_group("B", [(usl[par][:, k * 128:(k + 1) * 128], xnh[:, k * TH:(k + 1) * TH]) for k in range(NKD)]
                     + [(ones2neg[:, :], xns_hilo[:, HF * TH:(HF + 1) * TH])],
                     pre_waits=[usl_r[par].ready])
            usl_r[par].frees.append(pres["B"].ready)
            xnh_r.frees.append(pres["B"].ready)
            swrite("ac", scr_r[0], lambda e, m=m: e.activation(
                out=scr[0][:, 0:TH], in_=psA[:, :], func=AF.Silu, scale=sg_sb[:, m:m + 1]), ps_reads=["A"])
            swrite("ve", scr_r[1], lambda e, m=m: e.tensor_scalar_mul(scr[1][:, 0:TH], psB[:, :], su_sb[:, m:m + 1]), ps_reads=["B"])
            hlast = suse("ve", [scr_r[0], scr_r[1]], lambda e, m=m: e.tensor_mul(
                h_bf[:, m * TH:(m + 1) * TH], scr[0][:, 0:TH], scr[1][:, 0:TH]))
        sready(hm, hlast)
        if "t_h" in tap_dram and HF == 0:
            S.wait_pair("sy", hlast)
            S.do("sy", lambda e: e.dma_start(out=tap_dram["t_h"], in_=h_bf[:, :]), inc="sx", n=16)

        mm_group("A", [(ones128[:, 0:1], h_bf[:, k * TH:(k + 1) * TH]) for k in range(NKH)],
                 sub=psA[0:1, :], pre_waits=[hlast])
        hm.frees.append(pres["A"].ready)
        pa = consume("ac", ["A"], lambda e: e.copy(out=rowa[:, 0:TH], in_=psA[0:1, :]))
        S.wait_pair("ve", pa)
        S.do("ve", lambda e: e.tensor_copy(out=rbf_a[:, 0:TH], in_=rowa[:, 0:TH]))
        S.pad("ve")
        S.do("ve", lambda e: e.tensor_sub(rowb[:, 0:TH], rowa[:, 0:TH], rbf_a[:, 0:TH]))
        S.pad("ve")
        S.do("ve", lambda e: e.tensor_copy(out=rbf_b[:, 0:TH], in_=rowb[:, 0:TH]), inc="ve")
        v_hs = S.cur("ve")
        S.wait("sy", "ve", v_hs)
        S.do("sy", lambda e: e.dma_start(out=hs_hilo[0:1, 0:TH], in_=rbf_a[:, 0:TH]), inc="sx", n=16)
        S.do("sy", lambda e: e.dma_start(out=hs_hilo[1:2, 0:TH], in_=rbf_b[:, 0:TH]), inc="sx", n=16)
        c_hs = S.cur("sx")

        for m in range(NH):
            par = m % 2
            wload(dsl[par], dsl_r[par], f"w{4 + par}", wqd_t, m * 128, 128, nk=NKH)
            wload(osl[par], osl_r[par], f"w{6 + par}", wo_t, m * 128, 128)
            xr = xres_r[par]
            S.wait_frees("sy", xr)
            S.do("sy", lambda e, m=m, HF=HF, i=par: e.dma_start(
                out=xres[i][:, :], in_=x_t[m * 128:(m + 1) * 128, HALO + HF * TH:HALO + (HF + 1) * TH]),
                inc=xsem[par], n=16)
            sready(xr, (xsem[par], S.cur(xsem[par])))
            mm_group("A", [(dsl[par][:, k * 128:(k + 1) * 128], h_bf[:, k * TH:(k + 1) * TH]) for k in range(NKH)]
                     + [(ones2neg[:, :], hs_hilo[:, 0:TH])],
                     pre_waits=[dsl_r[par].ready, hm.ready, ("sx", c_hs)])
            dsl_r[par].frees.append(pres["A"].ready)
            hm.frees.append(pres["A"].ready)
            mm_group("B", [(osl[par][:, k * 128:(k + 1) * 128], gth[:, k * TH:(k + 1) * TH]) for k in range(NKD)],
                     pre_waits=[osl_r[par].ready, gth_r.ready])
            osl_r[par].frees.append(pres["B"].ready)
            gth_r.frees.append(pres["B"].ready)
            swrite("ve", scr_r[0 + par], lambda e, m=m, i=par: e.tensor_scalar_mul(
                scr[i][:, 0:TH], psA[:, :], sd_sb[:, m:m + 1]), ps_reads=["A"])
            swrite("ve", scr_r[2 + par], lambda e, i=par: e.tensor_add(
                scr[2 + i][:, 0:TH], scr[i][:, 0:TH], psB[:, :]), reads=[scr_r[0 + par]], ps_reads=["B"])
            yr = yout_r[par]
            S.wait_frees("ve", yr)
            S.wait_pair("ve", scr_r[2 + par].ready)
            S.wait_pair("ve", xr.ready)
            v = S.do("ve", lambda e, i=par: e.tensor_add(
                yout[i][:, :], scr[2 + i][:, 0:TH], xres[i][:, :]), inc="ve")
            scr_r[2 + par].frees.append(("ve", v))
            xr.frees.append(("ve", v))
            sready(yr, ("ve", v))
            S.wait_pair("sy", yr.ready)
            sv = S.do("sy", lambda e, m=m, HF=HF, i=par: e.dma_start(
                out=y_t[m * 128:(m + 1) * 128, HF * TH:(HF + 1) * TH], in_=yout[i][:, :]),
                inc="sx", n=16)
            yr.frees.append(("sx", sv))

    return finish()



def _col16(v):
    return np.ascontiguousarray(np.asarray(v, np.float32).reshape(16, 128).T)


def _prep_shared(inp):
    d = {}
    f = np.float32
    d["wq_t"] = np.ascontiguousarray(np.asarray(inp["L0_attn_Wq"], f).T)
    d["wk_t"] = np.ascontiguousarray(np.asarray(inp["L0_attn_Wk"], f).T)
    d["wv_t"] = np.ascontiguousarray(np.asarray(inp["L0_attn_Wv"], f).T)
    d["wo_t"] = np.ascontiguousarray(np.asarray(inp["L0_attn_Wo"], f).T)
    d["wig_t"] = np.ascontiguousarray(np.asarray(inp["L0_attn_Wig"], f).T)
    d["wog_t"] = np.ascontiguousarray(np.asarray(inp["L0_attn_Wog"], f).T)
    d["wg_t"] = np.ascontiguousarray(np.asarray(inp["L0_attn_Wg"], f).T)

    def u8T(a):
        a = np.ascontiguousarray(np.asarray(a).astype("<i4", copy=False))
        u = a.view(np.uint8).reshape(a.shape[0], a.shape[1], 4)[:, :, 0]
        return np.ascontiguousarray(u.T)

    d["wqg_t"] = u8T(inp["L0_ffn_wq_gate"])
    d["wqu_t"] = u8T(inp["L0_ffn_wq_up"])
    d["wqd_t"] = u8T(inp["L0_ffn_wq_down"])
    cw = np.asarray(inp["L0_attn_conv_w"], f)
    d["convw"] = np.ascontiguousarray(cw.reshape(16, 128, 4).transpose(1, 0, 2).reshape(128, 64))
    d["convb"] = _col16(inp["L0_attn_conv_b"])
    d["big"] = _col16(inp["L0_attn_b_ig"])
    d["bog"] = _col16(inp["L0_attn_b_og"])
    d["bg"] = np.ascontiguousarray(np.asarray(inp["L0_attn_b_g"], f).reshape(16, 1))
    d["gnw"] = _col16(inp["L0_attn_gn_w"])
    d["gnb"] = _col16(inp["L0_attn_gn_b"])
    d["mnw"] = _col16(inp["L0_attn_mn_w"])
    d["mnb"] = _col16(inp["L0_attn_mn_b"])
    d["sgc"] = np.ascontiguousarray(np.asarray(inp["L0_ffn_s_gate"], f).reshape(44, 128).T)
    d["suc"] = np.ascontiguousarray(np.asarray(inp["L0_ffn_s_up"], f).reshape(44, 128).T)
    d["sdc"] = _col16(inp["L0_ffn_s_down"])
    oh2 = np.zeros((32, NH * 128), ml_dtypes.bfloat16)
    for h in range(NH):
        oh2[h, h * 128:(h + 1) * 128] = 1
        oh2[16 + h, h * 128:(h + 1) * 128] = 1
    d["oneh2"] = oh2
    return d


def _prep_core(inp, c):
    b, j = c // 4, c % 4
    x = np.asarray(inp["L0_attn_x"], np.float32)
    t0 = j * T
    if j == 0:
        seg = np.concatenate([np.zeros((HALO, D), np.float32), x[b, 0:T]], axis=0)
    else:
        seg = x[b, t0 - HALO:t0 + T]
    d = {"x_t": np.ascontiguousarray(seg.T)}
    selc = np.zeros((128, 8), np.float32)
    for i in range(8):
        bi, ji = i // 4, i % 4
        if bi == b and ji < j:
            selc[:, i] = 1.0
    d["selc"] = selc
    d["invc"] = np.ascontiguousarray(1.0 - selc)
    return d


def kernel(**inputs):
    shared = _prep_shared(inputs)
    nc = _build()
    in_maps = []
    for c in range(NCORES):
        m = dict(shared)
        m.update(_prep_core(inputs, c))
        in_maps.append(m)
    res = run_bass_kernel_spmd(nc, in_maps, core_ids=list(range(NCORES)))
    y = np.empty((B, TFULL, D), np.float32)
    for c in range(NCORES):
        b, j = c // 4, c % 4
        y[b, j * T:(j + 1) * T, :] = res.results[c]["y_t"].T
    return y


if __name__ == "__main__":
    nc = _build()
    print("build ok; instructions:", nc.n_instructions())


# version 02
# speedup vs baseline: 1.0759x; 1.0759x over previous
import numpy as np
import ml_dtypes
from contextlib import ExitStack

import concourse.bass as bass
import concourse.mybir as mybir
from concourse.bass_utils import run_bass_kernel_spmd

F32 = mybir.dt.float32
BF16 = mybir.dt.bfloat16
U8 = mybir.dt.uint8
AF = mybir.ActivationFunctionType
ALU = mybir.AluOpType
AX = mybir.AxisListType

NCORES = 8
B, TFULL, D = 2, 4096, 2048
NH, HD = 16, 128
HID = 5632
NKD = 16
NKH = 44
T = 1024
TH = 512
HALO = 3
NT = T + HALO
EPS = 1e-5

ENGS = ("pe", "ve", "ac", "sy", "gp")
ESEM = {"pe": "pe", "ve": "ve", "ac": "ac", "sy": "sy_c", "gp": "gp_c"}


class Sched:
    def __init__(self, nc, ctx):
        self.nc = nc
        self.prog = {e: [] for e in ENGS}
        self.cnt = {}
        self.seen = {e: {} for e in ENGS}
        self.sems = {}
        self.ctx = ctx

    def sem(self, name):
        if name not in self.sems:
            self.sems[name] = self.ctx.enter_context(self.nc.semaphore(name))
            self.cnt.setdefault(name, 0)
        return self.sems[name]

    def wait(self, eng, sem_name, val):
        if val is None or val <= 0:
            return
        if self.seen[eng].get(sem_name, 0) >= val:
            return
        self.seen[eng][sem_name] = val
        s = self.sem(sem_name)
        self.prog[eng].append(lambda e, s=s, v=val: e.wait_ge(s, v))

    def wait_pair(self, eng, pair):
        if pair is not None:
            self.wait(eng, pair[0], pair[1])

    def wait_frees(self, eng, res):
        for (s, v) in res.frees:
            self.wait(eng, s, v)

    def do(self, eng, fn, inc=None, n=1):
        if inc is None:
            self.prog[eng].append(fn)
            return None
        self.cnt[inc] = self.cnt.get(inc, 0) + n
        tgt = self.cnt[inc]
        s = self.sem(inc)
        self.prog[eng].append(lambda e, fn=fn, s=s, n=n: fn(e).then_inc(s, n))
        return tgt

    def cur(self, sem_name):
        return self.cnt.get(sem_name, 0)

    def pad(self, eng, n=2):
        for _ in range(n):
            self.prog[eng].append(lambda e: e.engine_nop())


class Res:
    __slots__ = ("ready", "frees")

    def __init__(self):
        self.ready = None
        self.frees = []


def _build(taps=(), upto=99, skip=()):
    nc = bass.Bass("TRN2", target_bir_lowering=False, debug=False)

    def I(name, shape, dt):
        return nc.dram_tensor(name, shape, dt, kind="ExternalInput").ap()

    x_t = I("x_t", [D, NT], F32)
    wq_t = I("wq_t", [D, D], F32)
    wk_t = I("wk_t", [D, D], F32)
    wv_t = I("wv_t", [D, D], F32)
    wo_t = I("wo_t", [D, D], F32)
    wig_t = I("wig_t", [D, D], F32)
    wog_t = I("wog_t", [D, D], F32)
    wg_t = I("wg_t", [D, NH], F32)
    wqg_t = I("wqg_t", [D, HID], U8)
    wqu_t = I("wqu_t", [D, HID], U8)
    wqd_t = I("wqd_t", [HID, D], U8)
    convw = I("convw", [128, 64], F32)
    convb = I("convb", [128, 16], F32)
    big = I("big", [128, 16], F32)
    bog = I("bog", [128, 16], F32)
    bg = I("bg", [16, 1], F32)
    gnw = I("gnw", [128, 16], F32)
    gnb = I("gnb", [128, 16], F32)
    mnw = I("mnw", [128, 16], F32)
    mnb = I("mnb", [128, 16], F32)
    sgc = I("sgc", [128, 44], F32)
    suc = I("suc", [128, 44], F32)
    sdc = I("sdc", [128, 16], F32)
    oneh2 = I("oneh2", [32, NH * 128], BF16)
    selc = I("selc", [128, 8], F32)
    invc = I("invc", [128, 8], F32)

    y_t = nc.dram_tensor("y_t", [D, T], F32, kind="ExternalOutput").ap()

    cc_in = nc.dram_tensor("cc_in", [128, 32], F32)
    cc_out = nc.dram_tensor("cc_out", [NCORES * 128, 32], F32)
    xn_sp = nc.dram_tensor("xn_sp", [128, NKD * T], BF16)
    ml_sp = nc.dram_tensor("ml_sp", [128, NH * T], BF16)
    og_sp = nc.dram_tensor("og_sp", [128, NH * T], BF16)
    q_sp = nc.dram_tensor("q_sp", [128, NH * T], BF16)
    gt_sp = nc.dram_tensor("gt_sp", [128, NH * T], BF16)

    tap_dram = {}
    for tname, shape, dt in [
        ("t_xn", [128, NKD * T], BF16), ("t_xconv", [128, NKD * T], BF16),
        ("t_memloc", [128, NH * T], BF16), ("t_q", [128, NH * T], BF16),
        ("t_og", [128, NH * T], BF16), ("t_gated", [128, NH * T], BF16),
        ("t_h", [128, NKH * TH], BF16), ("t_gamma", [16, T], F32),
        ("t_state", [128, 16], F32), ("t_gin", [128, 32], F32),
        ("t_gbuf", [128, 8 * 32], F32), ("t_w", [16, 104], F32), ("t_can", [128, 4096], F32), ("t_sel", [128, 64], F32),
    ]:
        if tname in taps:
            tap_dram[tname] = nc.dram_tensor(tname, shape, dt, kind="ExternalOutput").ap()

    ctx = ExitStack()
    S = Sched(nc, ctx)
    E = ctx.enter_context
    open_stacks = []

    def finish():
        if "t_can" in tap_dram:
            tc = tap_dram["t_can"]
            for eng2 in ("ve", "ac", "pe", "gp_c", "gp_w"):
                if eng2 in ("gp_c", "gp_w"):
                    S.wait("sy", eng2, S.cur(eng2))
                else:
                    S.wait("sy", eng2, S.cur(eng2))
            for (src_t, r0, r1, c0, c1) in [
                (convw_sb, 0, 128, 0, 64), (convb_sb, 0, 128, 64, 80), (big_sb, 0, 128, 80, 96),
                (bog_sb, 0, 128, 96, 112), (gnw_sb, 0, 128, 112, 128), (gnb_sb, 0, 128, 128, 144),
                (mnw_sb, 0, 128, 144, 160), (mnb_sb, 0, 128, 160, 176), (sg_sb, 0, 128, 176, 220),
                (su_sb, 0, 128, 220, 264), (sd_sb, 0, 128, 264, 280), (mb8_sb, 0, 16, 280, 344),
                (malog_sb, 0, 16, 344, 352), (bg_sb, 0, 16, 352, 353), (cst, 0, 128, 353, 361),
                (gam_sb, 0, 16, 1024, 2048), (gbuf, 0, 128, 2048, 2184), (state_sb, 0, 128, 2184, 2200),
            ]:
                S.do("sy", lambda e, s=src_t, r0=r0, r1=r1, c0=c0, c1=c1: e.dma_start(
                    out=tc[r0:r1, c0:c1], in_=s[0:r1 - r0, 0:c1 - c0],
                    allow_slow_non_contiguous=True), inc="sx", n=16)
        S.wait("sy", "sx", S.cur("sx"))
        S.wait("sy", "gp_c", S.cur("gp_c"))
        blk = ctx.enter_context(nc.Block())

        def run_for(key):
            def body(eng):
                for fn in S.prog[key]:
                    fn(eng)
            return body

        blk.sync(run_for("sy"))
        blk.vector(run_for("ve"))
        blk.scalar(run_for("ac"))
        blk.tensor(run_for("pe"))
        blk.gpsimd(run_for("gp"))
        for stk in reversed(open_stacks):
            stk.close()
        ctx.close()
        return nc

    convw_sb = E(nc.sbuf_tensor("convw_sb", [128, 64], F32))
    convb_sb = E(nc.sbuf_tensor("convb_sb", [128, 16], F32))
    big_sb = E(nc.sbuf_tensor("big_sb", [128, 16], F32))
    bog_sb = E(nc.sbuf_tensor("bog_sb", [128, 16], F32))
    bg_sb = E(nc.sbuf_tensor("bg_sb", [16, 1], F32))
    gnw_sb = E(nc.sbuf_tensor("gnw_sb", [128, 16], F32))
    gnb_sb = E(nc.sbuf_tensor("gnb_sb", [128, 16], F32))
    mnw_sb = E(nc.sbuf_tensor("mnw_sb", [128, 16], F32))
    mnb_sb = E(nc.sbuf_tensor("mnb_sb", [128, 16], F32))
    sg_sb = E(nc.sbuf_tensor("sg_sb", [128, 44], F32))
    su_sb = E(nc.sbuf_tensor("su_sb", [128, 44], F32))
    sd_sb = E(nc.sbuf_tensor("sd_sb", [128, 16], F32))
    oneh2_sb = E(nc.sbuf_tensor("oneh2_sb", [32, NH * 128], BF16))
    selc_sb = E(nc.sbuf_tensor("selc_sb", [128, 8], F32))
    invc_sb = E(nc.sbuf_tensor("invc_sb", [128, 8], F32))
    ones128 = E(nc.sbuf_tensor("ones128", [128, 128], BF16))
    ones2neg = E(nc.sbuf_tensor("ones2neg", [2, 128], BF16))
    cst = E(nc.sbuf_tensor("cst", [128, 8], F32))
    lowsl = [E(nc.sbuf_tensor(f"lowsl{i}", [128, NKD * 128], BF16)) for i in range(6)]
    dsl_lo = [E(nc.sbuf_tensor(f"dsllo{i}", [128, NKH * 128], BF16)) for i in range(2)]
    xb_lo = [E(nc.sbuf_tensor(f"xblo{i}", [128, NT], BF16)) for i in range(2)]
    wgt_sb = E(nc.sbuf_tensor("wgt_sb", [128, NKD * 16], BF16))
    gam_hilo = E(nc.sbuf_tensor("gam_hilo", [32, T], BF16))
    g_in = E(nc.sbuf_tensor("g_in", [128, 32], F32))
    gbuf = E(nc.sbuf_tensor("gbuf", [128, 8 * 32], F32))
    state_sb = E(nc.sbuf_tensor("state_sb", [128, 16], F32))
    xns_hilo = E(nc.sbuf_tensor("xns_hilo", [2, T], BF16))
    hs_hilo = E(nc.sbuf_tensor("hs_hilo", [2, T], BF16))
    rbf_a = E(nc.sbuf_tensor("rbf_a", [1, T], BF16))
    rbf_b = E(nc.sbuf_tensor("rbf_b", [1, T], BF16))
    small2 = E(nc.sbuf_tensor("small2", [128, 48], F32))

    scrA = [E(nc.sbuf_tensor(f"scra{i}", [128, 1040], F32)) for i in range(4)]
    scrb = [E(nc.sbuf_tensor(f"scrb{i}", [128, 1040], BF16)) for i in range(4)]

    psS1 = E(nc.psum_tensor("psS1", [128, 1024], F32))
    psS2 = E(nc.psum_tensor("psS2", [128, 1024], F32))
    psS3 = E(nc.psum_tensor("psS3", [128, 1024], F32))
    psA = E(nc.psum_tensor("psA", [128, 512], F32))
    psB = E(nc.psum_tensor("psB", [128, 512], F32))
    pres = {"S1": Res(), "S2": Res(), "S3": Res(), "A": Res(), "B": Res()}
    pst = {"S1": psS1, "S2": psS2, "S3": psS3, "A": psA, "B": psB}

    stk_scrB = ExitStack()
    open_stacks.append(stk_scrB)
    scrB = [stk_scrB.enter_context(nc.sbuf_tensor(f"scrx{i}", [128, 1040], F32)) for i in range(4)]
    scr = scrA + scrB
    scr_r = [Res() for _ in range(8)]
    scrb_r = [Res() for _ in range(4)]

    marks = {}

    def mark(name, pair):
        marks[name] = pair

    def wmark(eng, name):
        S.wait_pair(eng, marks.get(name))

    def sready(res, pair):
        res.ready = pair
        res.frees = []

    def suse(eng, res_list, fn, ps_reads=(), also=()):
        for r in res_list:
            S.wait_pair(eng, r.ready)
        for p in ps_reads:
            S.wait_pair(eng, pres[p].ready)
        for (sn, v) in also:
            S.wait(eng, sn, v)
        v = S.do(eng, fn, inc=ESEM[eng])
        pr = (ESEM[eng], v)
        for r in res_list:
            r.frees.append(pr)
        for p in ps_reads:
            pres[p].frees.append(pr)
        return pr

    def swrite(eng, res, fn, reads=(), ps_reads=(), also=()):
        S.wait_frees(eng, res)
        for r in reads:
            S.wait_pair(eng, r.ready)
        for p in ps_reads:
            S.wait_pair(eng, pres[p].ready)
        for (sn, v) in also:
            S.wait(eng, sn, v)
        v = S.do(eng, fn, inc=ESEM[eng])
        pr = (ESEM[eng], v)
        sready(res, pr)
        for r in reads:
            r.frees.append(pr)
        for p in ps_reads:
            pres[p].frees.append(pr)
        return pr

    def mm_group(ps_name, mms, sub=None, pre_waits=()):
        r = pres[ps_name]
        S.wait_frees("pe", r)
        for pw in pre_waits:
            if pw is not None:
                S.wait("pe", pw[0], pw[1])
        out_ap = sub if sub is not None else pst[ps_name][:, :]
        n = len(mms)
        lv = None
        for i, (l, rr) in enumerate(mms):
            st, sp = (i == 0), (i == n - 1)
            if sp:
                lv = S.do("pe", lambda e, o=out_ap, l=l, rr=rr, st=st: e.matmul(o, l, rr, start=st, stop=True), inc="pe")
            else:
                S.do("pe", lambda e, o=out_ap, l=l, rr=rr, st=st: e.matmul(o, l, rr, start=st, stop=False))
        sready(r, ("pe", lv))
        return lv

    def consume(eng, ps_names, fn, also=()):
        for p in ps_names:
            S.wait_pair(eng, pres[p].ready)
        for (sn, v) in also:
            S.wait(eng, sn, v)
        v = S.do(eng, fn, inc=ESEM[eng])
        pr = (ESEM[eng], v)
        for p in ps_names:
            pres[p].frees.append(pr)
        return pr

    def wload(slot, slot_r, semn, wt_dram, c0, w, nk=NKD):
        S.wait_frees("gp", slot_r)
        S.do("gp", lambda e, s=slot, src=wt_dram, c0=c0, w=w, nk=nk: e.dma_start(
            out=s[:, 0:nk * w].rearrange("p (k w) -> p k w", w=w),
            in_=src[:, c0:c0 + w].rearrange("(k p) w -> p k w", p=128)), inc=semn, n=16)
        sready(slot_r, (semn, S.cur(semn)))

    for dst, src in [
        (convw_sb, convw), (convb_sb, convb), (big_sb, big), (bog_sb, bog),
        (bg_sb, bg), (gnw_sb, gnw), (gnb_sb, gnb), (mnw_sb, mnw), (mnb_sb, mnb),
        (sg_sb, sgc), (su_sb, suc), (sd_sb, sdc), (oneh2_sb, oneh2),
        (selc_sb, selc), (invc_sb, invc),
    ]:
        S.do("sy", lambda e, d=dst, s=src: e.dma_start(out=d[:, :], in_=s), inc="sx", n=16)
    c_consts = S.cur("sx")
    S.do("gp", lambda e: e.dma_start(
        out=wgt_sb[:, :].rearrange("p (k w) -> p k w", w=16),
        in_=wg_t.rearrange("(k p) w -> p k w", p=128)), inc="gp_w", n=16)
    g_wg = S.cur("gp_w")
    S.do("ve", lambda e: e.memset(ones128[:, :], 1.0), inc="ve")
    v_ones = S.cur("ve")
    S.do("ve", lambda e: e.memset(ones2neg[:, :], -128.0), inc="ve")
    v_ones2 = S.cur("ve")
    S.do("ve", lambda e: e.memset(cst[:, 0:1], EPS))
    S.do("ve", lambda e: e.memset(cst[:, 1:2], 1.0 / D))
    S.do("ve", lambda e: e.memset(cst[:, 2:3], 1.0 / HD))
    S.do("ve", lambda e: e.memset(cst[:, 3:4], 1e-24))
    S.do("ve", lambda e: e.memset(cst[:, 4:5], -80.0), inc="ve")
    v_cst = S.cur("ve")
    ceps, cinvD, cinvHD, c24 = cst[:, 0:1], cst[:, 1:2], cst[:, 2:3], cst[:, 3:4]
    cm80 = cst[:, 4:5]
    S.wait("ac", "ve", v_cst)
    for _eng in ("ve", "ac", "pe", "gp"):
        S.wait(_eng, "sx", c_consts)

    rP12 = ExitStack()
    open_stacks.append(rP12)
    xn_bf = rP12.enter_context(nc.sbuf_tensor("xn_bf", [128, NKD * T], BF16))
    xconv_bf = rP12.enter_context(nc.sbuf_tensor("xconv_bf", [128, NKD * T], BF16))

    r1a = ExitStack()
    open_stacks.append(r1a)
    xt = [r1a.enter_context(nc.sbuf_tensor(f"xt{i}", [128, NT], F32)) for i in range(2)]
    xb = xb_lo
    mu_bc = r1a.enter_context(nc.sbuf_tensor("mu_bc", [128, NT], F32))
    r_bc = r1a.enter_context(nc.sbuf_tensor("r_bc", [128, NT], F32))
    xt_r = [Res(), Res()]
    xb_r = [Res(), Res()]
    mu_r, r_r = Res(), Res()
    xsem = ["xA", "xB"]
    bsem = ["w0", "w1"]

    chunks = [(0, TH), (TH, 2 * TH), (2 * TH, NT)]
    if "nA" in skip:
        S.do("ve", lambda e: e.memset(mu_bc[:, :], 0.0))
        v = S.do("ve", lambda e: e.memset(r_bc[:, :], 1.0), inc="ve")
        sready(mu_r, ("ve", v)); sready(r_r, ("ve", v))
    for k in range(NKD if "nA" not in skip else 0):
        par = k % 2
        tr, br = xt_r[par], xb_r[par]
        S.wait_frees("sy", tr)
        S.do("sy", lambda e, k=k, p=par: e.dma_start(
            out=xt[p][:, :], in_=x_t[k * 128:(k + 1) * 128, :]), inc=xsem[par], n=16)
        sready(tr, (xsem[par], S.cur(xsem[par])))
        S.wait_frees("gp", br)
        S.do("gp", lambda e, k=k, p=par: e.dma_start(
            out=xb[p][:, :], in_=x_t[k * 128:(k + 1) * 128, :]), inc=bsem[par], n=16)
        sready(br, (bsem[par], S.cur(bsem[par])))
        sb = scrb_r[par]
        swrite("ac", sb, lambda e, p=par: e.square(out=scrb[p][:, 0:NT], in_=xt[p][:, :]), reads=[tr])
        S.wait("pe", "ve", v_ones)
        S.wait_pair("pe", br.ready)
        S.wait_pair("pe", sb.ready)
        if k == 0:
            for nm in ("S1", "S2", "A", "B"):
                S.wait_frees("pe", pres[nm])
        lv1 = None
        for ci, (c0, c1) in enumerate(chunks):
            tgt = psA[:, 0:3] if c1 == NT else psS1[:, c0:c1]
            lv1 = S.do("pe", lambda e, k=k, p=par, c0=c0, c1=c1, t=tgt: e.matmul(
                t, ones128[:, :], xb[p][:, c0:c1],
                start=(k == 0), stop=(k == NKD - 1)), inc="pe" if ci == 2 else None)
        br.frees.append(("pe", lv1))
        lv = None
        for ci, (c0, c1) in enumerate(chunks):
            tgt = psB[:, 0:3] if c1 == NT else psS2[:, c0:c1]
            lv = S.do("pe", lambda e, k=k, p=par, c0=c0, c1=c1, t=tgt: e.matmul(
                t, ones128[:, :], scrb[p][:, c0:c1],
                start=(k == 0), stop=(k == NKD - 1)), inc="pe" if ci == 2 else None)
        sb.frees.append(("pe", lv))
        tr.frees.append(("ac", sb.ready[1]))
    if "nA" not in skip:
        for nm in ("S1", "S2", "A", "B"):
            sready(pres[nm], ("pe", lv))
    if "nA" not in skip:
        p1 = consume("ac", ["S1"], lambda e: e.activation(out=mu_bc[:, 0:T], in_=psS1[:, :], func=AF.Copy, scale=cinvD))
        p2 = consume("ac", ["A"], lambda e: e.activation(out=mu_bc[:, T:NT], in_=psA[:, 0:3], func=AF.Copy, scale=cinvD))
        sready(mu_r, ("ac", p2[1]))
        swrite("ve", scr_r[6], lambda e: e.tensor_mul(scr[6][:, 0:NT], mu_bc[:, :], mu_bc[:, :]), reads=[mu_r])
        S.wait_frees("ve", scr_r[7])
        S.wait_pair("ve", pres["S2"].ready)
        S.wait_pair("ve", scr_r[6].ready)
        S.do("ve", lambda e: e.scalar_tensor_tensor(
            out=scr[7][:, 0:T], in0=psS2[:, :], scalar=cinvD, in1=scr[6][:, 0:T],
            op0=ALU.mult, op1=ALU.subtract), inc="ve")
        S.pad("ve")
        S.wait_pair("ve", pres["B"].ready)
        v2 = S.do("ve", lambda e: e.scalar_tensor_tensor(
            out=scr[7][:, T:NT], in0=psB[:, 0:3], scalar=cinvD, in1=scr[6][:, T:NT],
            op0=ALU.mult, op1=ALU.subtract), inc="ve")
        for nm in ("S2", "B"):
            pres[nm].frees.append(("ve", v2))
        scr_r[6].frees.append(("ve", v2))
        sready(scr_r[7], ("ve", v2))
        swrite("ac", scr_r[6], lambda e: e.activation(out=scr[6][:, 0:NT], in_=scr[7][:, 0:NT], func=AF.Sqrt, bias=ceps), reads=[scr_r[7]])
        pr = suse("ve", [scr_r[6]], lambda e: e.reciprocal(out=r_bc[:, :], in_=scr[6][:, 0:NT]))
        sready(r_r, pr)


    for k in range(NKD):
        par = k % 2
        tr = xt_r[par]
        S.wait_frees("sy", tr)
        S.do("sy", lambda e, k=k, p=par: e.dma_start(
            out=xt[p][:, :], in_=x_t[k * 128:(k + 1) * 128, :]), inc=xsem[par], n=16)
        sready(tr, (xsem[par], S.cur(xsem[par])))
        i0, i1, ia = 3 * par, 3 * par + 1, (2 if par == 0 else 6)
        t1r, xnr, accr = scr_r[i0], scr_r[i1], scr_r[ia]
        swrite("ve", t1r, lambda e, p=par, i0=i0: e.tensor_sub(scr[i0][:, 0:NT], xt[p][:, :], mu_bc[:, :]), reads=[tr, mu_r])
        swrite("ve", xnr, lambda e, i0=i0, i1=i1: e.tensor_mul(scr[i1][:, 0:NT], scr[i0][:, 0:NT], r_bc[:, :]), reads=[t1r, r_r])
        suse("ac", [xnr], lambda e, k=k, i1=i1: e.copy(out=xn_bf[:, k * T:(k + 1) * T], in_=scr[i1][:, HALO:NT]))
        swrite("ve", t1r, lambda e, k=k, i0=i0, i1=i1: e.tensor_scalar_mul(
            scr[i0][:, 0:T], scr[i1][:, 0:T], convw_sb[:, 4 * k:4 * k + 1]), reads=[xnr])
        swrite("ve", accr, lambda e, k=k, i0=i0, i1=i1, ia=ia: e.scalar_tensor_tensor(
            out=scr[ia][:, 0:T], in0=scr[i1][:, 1:T + 1], scalar=convw_sb[:, 4 * k + 1:4 * k + 2],
            in1=scr[i0][:, 0:T], op0=ALU.mult, op1=ALU.add), reads=[xnr, t1r])
        swrite("ve", t1r, lambda e, k=k, i0=i0, i1=i1, ia=ia: e.scalar_tensor_tensor(
            out=scr[i0][:, 0:T], in0=scr[i1][:, 2:T + 2], scalar=convw_sb[:, 4 * k + 2:4 * k + 3],
            in1=scr[ia][:, 0:T], op0=ALU.mult, op1=ALU.add), reads=[xnr, accr])
        swrite("ve", accr, lambda e, k=k, i0=i0, i1=i1, ia=ia: e.scalar_tensor_tensor(
            out=scr[ia][:, 0:T], in0=scr[i1][:, 3:T + 3], scalar=convw_sb[:, 4 * k + 3:4 * k + 4],
            in1=scr[i0][:, 0:T], op0=ALU.mult, op1=ALU.add), reads=[xnr, t1r])
        suse("ac", [accr], lambda e, k=k, ia=ia: e.activation(
            out=xconv_bf[:, k * T:(k + 1) * T], in_=scr[ia][:, 0:T], func=AF.Silu, bias=convb_sb[:, k:k + 1]))
    mark("xn", ("ac", S.cur("ac")))
    mark("xconv", ("ac", S.cur("ac")))

    for hf in range(2 if "nS" not in skip else 0):
        nm = "A" if hf == 0 else "B"
        mms = [(ones128[:, 0:1], xn_bf[:, k * T + hf * TH:k * T + hf * TH + TH]) for k in range(NKD)]
        mm_group(nm, mms, sub=pst[nm][0:1, :], pre_waits=[marks["xn"]])
    if "nS" not in skip:
        consume("ac", ["A"], lambda e: e.copy(out=scr[2][0:1, 0:TH], in_=psA[0:1, :]))
        pa = consume("ac", ["B"], lambda e: e.copy(out=scr[2][0:1, TH:T], in_=psB[0:1, :]))
        S.wait_pair("ve", pa)
        S.do("ve", lambda e: e.tensor_copy(out=rbf_a[:, :], in_=scr[2][0:1, 0:T]))
        S.pad("ve")
        S.do("ve", lambda e: e.tensor_sub(rbf_b[:, :], scr[2][0:1, 0:T], rbf_a[:, :]), inc="ve")
        S.pad("ve")
        v_xns = S.cur("ve")
        S.wait("sy", "ve", v_xns)
        S.do("sy", lambda e: e.dma_start(out=xns_hilo[0:1, :], in_=rbf_a[:, :]), inc="sx", n=16)
        S.do("sy", lambda e: e.dma_start(out=xns_hilo[1:2, :], in_=rbf_b[:, :]), inc="sx", n=16)
    c_xns = S.cur("sx")

    if "nS" not in skip:
        wmark("sy", "xn")
        S.do("sy", lambda e: e.dma_start(out=xn_sp.ap(), in_=xn_bf[:, :]), inc="sx", n=16)
    c_spill = S.cur("sx")
    if "t_xn" in tap_dram:
        S.do("sy", lambda e: e.dma_start(out=tap_dram["t_xn"], in_=xn_bf[:, :]), inc="sx", n=16)
    if "t_xconv" in tap_dram:
        wmark("sy", "xconv")
        S.do("sy", lambda e: e.dma_start(out=tap_dram["t_xconv"], in_=xconv_bf[:, :]), inc="sx", n=16)
    r1a.close()
    open_stacks.remove(r1a)
    if upto <= 1:
        return finish()

    for hf in range(2 if "nG" not in skip else 0):
        sub = psS1[0:16, hf * TH:(hf + 1) * TH]
        mms = [(wgt_sb[:, k * 16:(k + 1) * 16],
                xconv_bf[:, k * T + hf * TH:k * T + hf * TH + TH]) for k in range(NKD)]
        mm_group("S1", mms, sub=sub, pre_waits=[("gp_w", g_wg), marks["xconv"]])
    if "nG" in skip:
        c_gh = S.cur("sx")
        v_lpt = S.cur("ve")
        pg = None
    _g_on = "nG" not in skip
    pg = pg if not _g_on else consume("ac", ["S1"], lambda e: e.activation(
        out=scr[0][0:16, 0:T], in_=psS1[0:16, :], func=AF.Sigmoid, bias=bg_sb[:, :]), also=[("sx", c_consts)])
    if _g_on:
        S.wait_pair("ve", pg)
        S.do("ve", lambda e: e.tensor_copy(out=scrb[0][0:16, 0:T], in_=scr[0][0:16, 0:T]))
        S.pad("ve")
        S.do("ve", lambda e: e.tensor_sub(scr[0][0:16, 0:T], scr[0][0:16, 0:T], scrb[0][0:16, 0:T]))
        S.pad("ve")
        S.do("ve", lambda e: e.tensor_copy(out=scrb[1][0:16, 0:T], in_=scr[0][0:16, 0:T]), inc="ve")
        v_gh = S.cur("ve")
        S.wait("sy", "ve", v_gh)
        S.do("sy", lambda e: e.dma_start(out=gam_hilo[0:16, :], in_=scrb[0][0:16, 0:T]), inc="sx", n=16)
        S.do("sy", lambda e: e.dma_start(out=gam_hilo[16:32, :], in_=scrb[1][0:16, 0:T]), inc="sx", n=16)
        c_gh = S.cur("sx")
        v_lpt = S.cur("ve")
        if "t_gamma" in tap_dram:
            S.wait_pair("sy", pg)
            S.do("sy", lambda e: e.dma_start(out=tap_dram["t_gamma"], in_=scr[0][0:16, 0:T]), inc="sx", n=16)
    if upto <= 2:
        return finish()

    r_slots = ExitStack()
    open_stacks.append(r_slots)
    ksl = [lowsl[i] for i in range(2)]
    vsl = [lowsl[2 + i] for i in range(2)]
    isl = [lowsl[4 + i] for i in range(2)]
    ksl_r = [Res(), Res()]
    vsl_r = [Res(), Res()]
    isl_r = [Res(), Res()]

    for h in range(NH):
        par = h % 2
        if "wl" not in skip:
            wload(ksl[par], ksl_r[par], f"w{2 + par}", wk_t, h * 128, 128)
            wload(vsl[par], vsl_r[par], f"w{4 + par}", wv_t, h * 128, 128)
            wload(isl[par], isl_r[par], f"w{6 + par}", wig_t, h * 128, 128)
        if "mm" in skip:
            continue
        for hf in range(2):
            xnsl = [xn_bf[:, k * T + hf * TH:k * T + hf * TH + TH] for k in range(NKD)]
            xcsl = [xconv_bf[:, k * T + hf * TH:k * T + hf * TH + TH] for k in range(NKD)]
            mm_group("A", [(ksl[par][:, k * 128:(k + 1) * 128], xnsl[k]) for k in range(NKD)],
                     pre_waits=[ksl_r[par].ready, marks["xn"]])
            ksl_r[par].frees.append(pres["A"].ready)
            mm_group("B", [(vsl[par][:, k * 128:(k + 1) * 128], xnsl[k]) for k in range(NKD)],
                     pre_waits=[vsl_r[par].ready])
            vsl_r[par].frees.append(pres["B"].ready)
            mm_group("S3", [(isl[par][:, k * 128:(k + 1) * 128], xcsl[k]) for k in range(NKD)],
                     sub=psS3[:, hf * TH:(hf + 1) * TH],
                     pre_waits=[isl_r[par].ready, marks["xconv"]])
            isl_r[par].frees.append(pres["S3"].ready)
            if "evac" in skip:
                continue
            if hf == 0:
                S.wait_frees("ac", scr_r[0]); scr_r[0].frees = []
                S.wait_frees("ac", scr_r[1]); scr_r[1].frees = []
                S.wait_frees("ac", scrb_r[2]); scrb_r[2].frees = []
                S.wait_frees("ve", scr_r[2]); scr_r[2].frees = []
            pr = consume("ac", ["B"], lambda e, hf=hf: e.copy(out=scr[0][:, hf * TH:(hf + 1) * TH], in_=psB[:, :]))
            scr_r[0].ready = pr
            pr = consume("ac", ["S3"], lambda e, hf=hf, h=h: e.activation(
                out=scr[1][:, hf * TH:(hf + 1) * TH], in_=psS3[:, hf * TH:(hf + 1) * TH],
                func=AF.Sigmoid, bias=big_sb[:, h:h + 1]))
            scr_r[1].ready = pr
            pr = consume("ac", ["A"], lambda e, hf=hf: e.square(
                out=scrb[2][:, hf * TH:(hf + 1) * TH], in_=psA[:, :]))
            scrb_r[2].ready = pr
            S.wait_pair("ve", scr_r[0].ready)
            pr = consume("ve", ["A"], lambda e, hf=hf: e.tensor_mul(
                scr[2][:, hf * TH:(hf + 1) * TH], psA[:, :], scr[0][:, hf * TH:(hf + 1) * TH]))
            scr_r[0].frees.append(pr)
            scr_r[2].ready = pr
        if "bt" in skip:
            continue
        r = pres["S2"]
        S.wait_frees("pe", r)
        S.wait_pair("pe", scrb_r[2].ready)
        for hf in range(2):
            lv = S.do("pe", lambda e, hf=hf: e.matmul(
                psS2[:, hf * TH:(hf + 1) * TH], ones128[:, :], scrb[2][:, hf * TH:(hf + 1) * TH],
                start=True, stop=True), inc="pe")
        sready(r, ("pe", lv))
        scrb_r[2].frees.append(("pe", lv))
        swrite("ve", scr_r[3], lambda e: e.tensor_scalar_max(scr[3][:, 0:T], psS2[:, :], c24), ps_reads=["S2"])
        swrite("ac", scr_r[4], lambda e: e.activation(out=scr[4][:, 0:T], in_=scr[3][:, 0:T], func=AF.Sqrt), reads=[scr_r[3]])
        swrite("ve", scr_r[3], lambda e: e.reciprocal(out=scr[3][:, 0:T], in_=scr[4][:, 0:T]), reads=[scr_r[4]])
        swrite("ve", scr_r[5], lambda e: e.tensor_mul(scr[5][:, 0:T], scr[2][:, 0:T], scr[1][:, 0:T]), reads=[scr_r[2], scr_r[1]])
        swrite("ve", scr_r[4], lambda e: e.tensor_mul(scr[4][:, 0:T], scr[5][:, 0:T], scr[3][:, 0:T]), reads=[scr_r[5], scr_r[3]])
        if "gbc" in skip:
            continue
        r = pres["S1"]
        S.wait_frees("pe", r)
        S.wait("pe", "sx", c_gh)
        for hf in range(2):
            lv = S.do("pe", lambda e, h=h, hf=hf: e.matmul(
                psS1[:, hf * TH:(hf + 1) * TH], oneh2_sb[:, h * 128:(h + 1) * 128],
                gam_hilo[:, hf * TH:(hf + 1) * TH], start=True, stop=True), inc="pe")
        sready(r, ("pe", lv))
        if "scan1" in skip:
            continue
        S.wait_frees("ve", scr_r[5])
        S.wait_pair("ve", pres["S1"].ready)
        S.wait_frees("ve", scr_r[2])
        vz = S.do("ve", lambda e: e.memset(scr[2][:, 0:T], 0.0))
        v = S.do("ve", lambda e: e.tensor_tensor_scan(
            out=scr[5][:, 0:T], data0=psS1[:, :], data1=scr[2][:, 0:T],
            initial=1.0, op0=ALU.mult, op1=ALU.add), inc="ve")
        pres["S1"].frees.append(("ve", v))
        sready(scr_r[5], ("ve", v))
        sready(scr_r[2], ("ve", v))
        S.pad("ve")
        suse("ve", [scr_r[5]], lambda e, h=h: e.tensor_copy(
            out=g_in[:, 16 + h:17 + h], in_=scr[5][:, T - 1:T]))
        S.pad("ve")
        S.wait_frees("ve", scrb_r[3])
        S.wait_pair("ve", scr_r[4].ready)
        v = S.do("ve", lambda e: e.tensor_tensor_scan(
            out=scrb[3][:, 0:T], data0=psS1[:, :], data1=scr[4][:, 0:T],
            initial=0.0, op0=ALU.mult, op1=ALU.add), inc="ve")
        pres["S1"].frees.append(("ve", v))
        scr_r[4].frees.append(("ve", v))
        sready(scrb_r[3], ("ve", v))
        S.pad("ve")
        suse("ve", [scrb_r[3]], lambda e, h=h: e.tensor_copy(
            out=g_in[:, h:h + 1], in_=scrb[3][:, T - 1:T]))
        S.pad("ve")
        S.wait_pair("sy", scrb_r[3].ready)
        sv = S.do("sy", lambda e, h=h: e.dma_start(
            out=ml_sp.ap()[:, h * T:(h + 1) * T], in_=scrb[3][:, 0:T]), inc="sx", n=16)
        scrb_r[3].frees.append(("sx", sv))
    c_ml = S.cur("sx")
    mark("gin_mf", ("ve", S.cur("ve")))
    for tn, sp_t2, cw2 in (("t_memloc", ml_sp, c_ml),):
        if tn in tap_dram and upto <= 3:
            S.wait("gp", "sx", cw2)
            S.do("gp", lambda e, d=tap_dram[tn], s=sp_t2: e.dma_start(out=d, in_=s.ap()), inc="gp_c", n=16)
    if upto <= 3:
        if "t_xconv" in tap_dram and "latetap" in skip:
            for sn in ("w2", "w3", "w4", "w5", "w6", "w7"):
                S.wait("sy", sn, S.cur(sn))
            S.do("sy", lambda e: e.dma_start(out=tap_dram["t_xconv"], in_=xn_bf[:, :]), inc="sx", n=16)
        return finish()

    wmark("ve", "gin_mf")
    S.wait("ve", "ve", v_lpt)
    S.do("ve", lambda e: e.engine_nop(), inc="ve")
    v_gin = S.cur("ve")
    S.wait("gp", "ve", v_gin)
    S.do("gp", lambda e: e.dma_start(out=cc_in.ap(), in_=g_in[:, :]), inc="gp_c", n=16)
    S.wait("gp", "gp_c", S.cur("gp_c"))
    S.do("gp", lambda e: e.collective_compute(
        "AllGather", ALU.bypass, replica_groups=[list(range(NCORES))],
        ins=[cc_in.ap()], outs=[cc_out.ap()]), inc="cc")
    v_cc = S.cur("cc")
    if "t_gin" in tap_dram:
        S.wait("sy", "ve", v_gin)
        S.do("sy", lambda e: e.dma_start(out=tap_dram["t_gin"], in_=g_in[:, :]), inc="sx", n=16)

    for m in range(NH):
        par = m % 2
        wload(ksl[par], ksl_r[par], f"w{2 + par}", wog_t, m * 128, 128)
        for hf in range(2):
            nm = "A" if hf == 0 else "B"
            mm_group(nm, [(ksl[par][:, k * 128:(k + 1) * 128],
                           xconv_bf[:, k * T + hf * TH:k * T + hf * TH + TH]) for k in range(NKD)],
                     pre_waits=[ksl_r[par].ready, marks["xconv"]])
            ksl_r[par].frees.append(pres[nm].ready)
            if hf == 0:
                S.wait_frees("ac", scrb_r[3])
                scrb_r[3].frees = []
            pr = consume("ac", [nm], lambda e, hf=hf, m=m, nm=nm: e.activation(
                out=scrb[3][:, hf * TH:(hf + 1) * TH], in_=pst[nm][:, :],
                func=AF.Sigmoid, bias=bog_sb[:, m:m + 1]))
            scrb_r[3].ready = pr
        S.wait_pair("sy", scrb_r[3].ready)
        sv = S.do("sy", lambda e, m=m: e.dma_start(
            out=og_sp.ap()[:, m * T:(m + 1) * T], in_=scrb[3][:, 0:T]), inc="sx", n=16)
        scrb_r[3].frees.append(("sx", sv))
    c_og = S.cur("sx")

    for m in range(NH):
        par = m % 2
        wload(vsl[par], vsl_r[par], f"w{4 + par}", wq_t, m * 128, 128)
        for hf in range(2):
            nm = "A" if hf == 0 else "B"
            mm_group(nm, [(vsl[par][:, k * 128:(k + 1) * 128],
                           xn_bf[:, k * T + hf * TH:k * T + hf * TH + TH]) for k in range(NKD)],
                     pre_waits=[vsl_r[par].ready, marks["xn"]])
            vsl_r[par].frees.append(pres[nm].ready)
            if hf == 0:
                S.wait_frees("ac", scrb_r[3])
                scrb_r[3].frees = []
            pr = consume("ac", [nm], lambda e, hf=hf, nm=nm: e.copy(
                out=scrb[3][:, hf * TH:(hf + 1) * TH], in_=pst[nm][:, :]))
            scrb_r[3].ready = pr
        S.wait_pair("sy", scrb_r[3].ready)
        sv = S.do("sy", lambda e, m=m: e.dma_start(
            out=q_sp.ap()[:, m * T:(m + 1) * T], in_=scrb[3][:, 0:T]), inc="sx", n=16)
        scrb_r[3].frees.append(("sx", sv))
    c_q = S.cur("sx")
    r_slots.close()
    open_stacks.remove(r_slots)
    rP12.close()
    open_stacks.remove(rP12)
    if upto <= 4:
        for tn, sp_t2, cw2 in (("t_memloc", ml_sp, c_ml), ("t_og", og_sp, c_og), ("t_q", q_sp, c_q)):
            if tn in tap_dram:
                S.wait("gp", "sx", cw2)
                S.do("gp", lambda e, d=tap_dram[tn], s=sp_t2: e.dma_start(out=d, in_=s.ap()), inc="gp_c", n=16)
        return finish()

    S.wait("gp", "cc", v_cc)
    S.do("gp", lambda e: e.dma_start(
        out=gbuf[:, :].rearrange("p (i c) -> p i c", c=32),
        in_=cc_out.ap().rearrange("(i p) c -> p i c", p=128)), inc="gp_c", n=16)
    g_gbuf = S.cur("gp_c")
    if "t_gbuf" in tap_dram:
        S.wait("sy", "gp_c", g_gbuf)
        S.do("sy", lambda e: e.dma_start(out=tap_dram["t_gbuf"], in_=gbuf[:, :]), inc="sx", n=16)
    S.wait("ve", "gp_c", g_gbuf)
    S.wait("ve", "sx", c_consts)
    S.do("ve", lambda e: e.memset(state_sb[:, :], 0.0))
    S.pad("ve")
    for i in range(NCORES):
        mf = gbuf[:, i * 32:i * 32 + 16]
        pt = gbuf[:, i * 32 + 16:i * 32 + 32]
        S.do("ve", lambda e, i=i, pt=pt: e.tensor_scalar(
            out=small2[:, 0:16], in0=pt, scalar1=selc_sb[:, i:i + 1], scalar2=invc_sb[:, i:i + 1],
            op0=ALU.mult, op1=ALU.add))
        S.pad("ve")
        S.do("ve", lambda e, i=i, mf=mf: e.tensor_scalar_mul(small2[:, 16:32], mf, selc_sb[:, i:i + 1]))
        S.pad("ve")
        S.do("ve", lambda e: e.tensor_mul(small2[:, 32:48], state_sb[:, :], small2[:, 0:16]))
        S.pad("ve")
        S.do("ve", lambda e: e.tensor_add(state_sb[:, :], small2[:, 32:48], small2[:, 16:32]))
        S.pad("ve")
    stv = ("ve", S.do("ve", lambda e: e.engine_nop(), inc="ve"))
    if "t_sel" in tap_dram:
        S.wait_pair("sy", stv)
        S.do("sy", lambda e: e.dma_start(out=tap_dram["t_sel"][:, 0:8], in_=selc_sb[:, :]), inc="sx", n=16)
        S.do("sy", lambda e: e.dma_start(out=tap_dram["t_sel"][:, 8:16], in_=invc_sb[:, :]), inc="sx", n=16)
        S.do("sy", lambda e: e.dma_start(out=tap_dram["t_sel"][:, 16:64], in_=small2[:, :]), inc="sx", n=16)
    mark("state", stv)
    if "t_w" in tap_dram:
        S.wait_pair("sy", stv)
        S.do("sy", lambda e: e.dma_start(out=tap_dram["t_w"], in_=small[:, :]), inc="sx", n=16)
    if "t_state" in tap_dram:
        S.wait_pair("sy", stv)
        S.do("sy", lambda e: e.dma_start(out=tap_dram["t_state"], in_=state_sb[:, :]), inc="sx", n=16)

    r67 = ExitStack()
    open_stacks.append(r67)
    h_bf = r67.enter_context(nc.sbuf_tensor("h_bf", [128, NKH * TH], BF16))
    xnh = r67.enter_context(nc.sbuf_tensor("xnh", [128, NKD * TH], BF16))
    gth = r67.enter_context(nc.sbuf_tensor("gth", [128, NKD * TH], BF16))
    p6a = r67.enter_context(nc.sbuf_tensor("p6a", [128, TH], F32))
    p6b = r67.enter_context(nc.sbuf_tensor("p6b", [128, TH], F32))
    p6a_r, p6b_r = Res(), Res()
    gsl = [lowsl[0], lowsl[1]]
    usl = [lowsl[2], lowsl[3]]
    dsl = dsl_lo
    osl = [lowsl[4], lowsl[5]]
    gsl_r = [Res(), Res()]
    usl_r = [Res(), Res()]
    dsl_r = [Res(), Res()]
    osl_r = [Res(), Res()]
    xres = [r67.enter_context(nc.sbuf_tensor(f"xres{i}", [128, TH], F32)) for i in range(2)]
    yout = [r67.enter_context(nc.sbuf_tensor(f"yout{i}", [128, TH], F32)) for i in range(2)]
    xres_r = [Res(), Res()]
    yout_r = [Res(), Res()]
    xnh_r = Res()
    gth_r = Res()
    hm = Res()

    S.wait("gp", "pe", S.cur("pe"))

    def load_xnh(HF):
        S.wait_frees("sy", xnh_r)
        S.wait("sy", "sx", c_spill)
        S.do("sy", lambda e, HF=HF: e.dma_start(
            out=xnh[:, :].rearrange("p (k t) -> p k t", t=TH),
            in_=xn_sp.ap().rearrange("p (k t) -> p k t", t=T)[:, :, HF * TH:(HF + 1) * TH]),
            inc="sx", n=16)
        sready(xnh_r, ("sx", S.cur("sx")))

    def emit_gateup(m, HF):
        par = m % 2
        wload(gsl[par], gsl_r[par], f"w{par}", wqg_t, m * 128, 128)
        wload(usl[par], usl_r[par], f"w{2 + par}", wqu_t, m * 128, 128)
        mm_group("A", [(gsl[par][:, k * 128:(k + 1) * 128], xnh[:, k * TH:(k + 1) * TH]) for k in range(NKD)]
                 + [(ones2neg[:, :], xns_hilo[:, HF * TH:(HF + 1) * TH])],
                 pre_waits=[gsl_r[par].ready, xnh_r.ready, ("ve", v_ones2), ("sx", c_xns)])
        gsl_r[par].frees.append(pres["A"].ready)
        xnh_r.frees.append(pres["A"].ready)
        mm_group("B", [(usl[par][:, k * 128:(k + 1) * 128], xnh[:, k * TH:(k + 1) * TH]) for k in range(NKD)]
                 + [(ones2neg[:, :], xns_hilo[:, HF * TH:(HF + 1) * TH])],
                 pre_waits=[usl_r[par].ready])
        usl_r[par].frees.append(pres["B"].ready)
        xnh_r.frees.append(pres["B"].ready)
        swrite("ac", p6a_r, lambda e, m=m: e.activation(
            out=p6a[:, :], in_=psA[:, :], func=AF.Silu, scale=sg_sb[:, m:m + 1]), ps_reads=["A"])
        swrite("ve", p6b_r, lambda e, m=m: e.tensor_scalar_mul(p6b[:, :], psB[:, :], su_sb[:, m:m + 1]), ps_reads=["B"])
        return suse("ve", [p6a_r, p6b_r], lambda e, m=m: e.tensor_mul(
            h_bf[:, m * TH:(m + 1) * TH], p6a[:, :], p6b[:, :]))

    r5 = ExitStack()
    open_stacks.append(r5)
    _qh0 = r5.enter_context(nc.sbuf_tensor("qh0", [128, T], BF16))
    _oh0 = r5.enter_context(nc.sbuf_tensor("oh0", [128, T], BF16))
    _mh0 = r5.enter_context(nc.sbuf_tensor("mh0", [128, T], BF16))
    qh = [_qh0, _qh0]
    oh = [_oh0, _oh0]
    mh = [_mh0, _mh0]
    _qr, _or, _mr = Res(), Res(), Res()
    qh_r = [_qr, _qr]
    oh_r = [_or, _or]
    mh_r = [_mr, _mr]
    zer_r = scr_r[7]
    swrite("ve", zer_r, lambda e: e.memset(scr[7][:, 0:T], 0.0))
    S.wait("sy", "sx", c_q)
    load_xnh(0)
    _mbounds = [(h * NKH) // NH for h in range(NH + 1)]
    hlast = None
    for h in range(NH):
        par = h % 2
        for (tiles, rr, sp, semn) in ((qh, qh_r, q_sp, f"w{2 + par}"), (oh, oh_r, og_sp, f"w{4 + par}"), (mh, mh_r, ml_sp, f"w{6 + par}")):
            S.wait_frees("sy", rr[par])
            S.do("sy", lambda e, t=tiles[par], s=sp, h=h: e.dma_start(
                out=t[:, :], in_=s.ap()[:, h * T:(h + 1) * T]), inc=semn, n=16)
            sready(rr[par], (semn, S.cur(semn)))
        r = pres["S1"]
        S.wait_frees("pe", r)
        for hf in range(2):
            lv = S.do("pe", lambda e, h=h, hf=hf: e.matmul(
                psS1[:, hf * TH:(hf + 1) * TH], oneh2_sb[:, h * 128:(h + 1) * 128],
                gam_hilo[:, hf * TH:(hf + 1) * TH], start=True, stop=True), inc="pe")
        sready(r, ("pe", lv))
        S.wait_frees("ve", scr_r[0])
        S.wait_pair("ve", pres["S1"].ready)
        S.wait_pair("ve", zer_r.ready)
        v = S.do("ve", lambda e: e.tensor_tensor_scan(
            out=scr[0][:, 0:T], data0=psS1[:, :], data1=scr[7][:, 0:T],
            initial=1.0, op0=ALU.mult, op1=ALU.add), inc="ve")
        pres["S1"].frees.append(("ve", v))
        zer_r.frees.append(("ve", v))
        sready(scr_r[0], ("ve", v))
        swrite("ve", scr_r[1], lambda e, h=h, p=par: e.scalar_tensor_tensor(
            out=scr[1][:, 0:T], in0=scr[0][:, 0:T], scalar=state_sb[:, h:h + 1],
            in1=mh[p][:, :], op0=ALU.mult, op1=ALU.add),
            reads=[scr_r[0], mh_r[par]], also=[marks["state"]])
        mh_r[par].frees.append(scr_r[1].ready)
        memr = scr_r[1]
        swrite("ac", scrb_r[0], lambda e: e.copy(out=scrb[0][:, 0:T], in_=scr[1][:, 0:T]), reads=[memr])
        swrite("ac", scrb_r[1], lambda e: e.square(out=scrb[1][:, 0:T], in_=scr[1][:, 0:T]), reads=[memr])
        for (nm, sbr, sbuf) in (("S2", scrb_r[0], scrb[0]), ("S3", scrb_r[1], scrb[1])):
            r = pres[nm]
            S.wait_frees("pe", r)
            S.wait_pair("pe", sbr.ready)
            for hf in range(2):
                lv = S.do("pe", lambda e, nm=nm, hf=hf, sb=sbuf: e.matmul(
                    pst[nm][:, hf * TH:(hf + 1) * TH], ones128[:, :], sb[:, hf * TH:(hf + 1) * TH],
                    start=True, stop=True), inc="pe")
            sready(r, ("pe", lv))
            sbr.frees.append(("pe", lv))
        swrite("ac", scr_r[2], lambda e: e.activation(out=scr[2][:, 0:T], in_=psS2[:, :], func=AF.Copy, scale=cinvHD), ps_reads=["S2"])
        swrite("ve", scr_r[3], lambda e: e.tensor_mul(scr[3][:, 0:T], scr[2][:, 0:T], scr[2][:, 0:T]), reads=[scr_r[2]])
        S.wait_frees("ve", scr_r[5])
        S.wait_pair("ve", pres["S3"].ready)
        S.wait_pair("ve", scr_r[3].ready)
        v = S.do("ve", lambda e: e.scalar_tensor_tensor(
            out=scr[5][:, 0:T], in0=psS3[:, :], scalar=cinvHD, in1=scr[3][:, 0:T],
            op0=ALU.mult, op1=ALU.subtract), inc="ve")
        pres["S3"].frees.append(("ve", v))
        scr_r[3].frees.append(("ve", v))
        sready(scr_r[5], ("ve", v))
        swrite("ac", scr_r[4], lambda e: e.activation(out=scr[4][:, 0:T], in_=scr[5][:, 0:T], func=AF.Sqrt, bias=ceps), reads=[scr_r[5]])
        swrite("ve", scr_r[5], lambda e: e.reciprocal(out=scr[5][:, 0:T], in_=scr[4][:, 0:T]), reads=[scr_r[4]])
        swrite("ve", scr_r[3], lambda e: e.tensor_sub(scr[3][:, 0:T], scr[1][:, 0:T], scr[2][:, 0:T]), reads=[memr, scr_r[2]])
        swrite("ve", scr_r[4], lambda e: e.tensor_mul(scr[4][:, 0:T], scr[3][:, 0:T], scr[5][:, 0:T]), reads=[scr_r[3], scr_r[5]])
        swrite("ve", scr_r[3], lambda e, h=h: e.tensor_scalar(
            out=scr[3][:, 0:T], in0=scr[4][:, 0:T], scalar1=mnw_sb[:, h:h + 1], scalar2=mnb_sb[:, h:h + 1],
            op0=ALU.mult, op1=ALU.add), reads=[scr_r[4]])
        swrite("ve", scr_r[6], lambda e, p=par: e.tensor_mul(scr[6][:, 0:T], scr[3][:, 0:T], qh[p][:, :]),
               reads=[scr_r[3], qh_r[par]])
        ur = scr_r[6]
        swrite("ac", scrb_r[0], lambda e: e.copy(out=scrb[0][:, 0:T], in_=scr[6][:, 0:T]), reads=[ur])
        swrite("ac", scrb_r[1], lambda e: e.square(out=scrb[1][:, 0:T], in_=scr[6][:, 0:T]), reads=[ur])
        pr = swrite("ac", scrb_r[2], lambda e, p=par: e.square(out=scrb[2][:, 0:T], in_=qh[p][:, :]), reads=[qh_r[par]])
        qh_r[par].frees.append(pr)
        for (nm, sbr, sbuf) in (("S2", scrb_r[0], scrb[0]), ("S3", scrb_r[1], scrb[1]), ("S1", scrb_r[2], scrb[2])):
            r = pres[nm]
            S.wait_frees("pe", r)
            S.wait_pair("pe", sbr.ready)
            for hf in range(2):
                lv = S.do("pe", lambda e, nm=nm, hf=hf, sb=sbuf: e.matmul(
                    pst[nm][:, hf * TH:(hf + 1) * TH], ones128[:, :], sb[:, hf * TH:(hf + 1) * TH],
                    start=True, stop=True), inc="pe")
            sready(r, ("pe", lv))
            sbr.frees.append(("pe", lv))
        swrite("ve", scr_r[3], lambda e: e.tensor_scalar_max(scr[3][:, 0:T], psS1[:, :], c24), ps_reads=["S1"])
        swrite("ac", scr_r[2], lambda e: e.activation(out=scr[2][:, 0:T], in_=scr[3][:, 0:T], func=AF.Sqrt), reads=[scr_r[3]])
        swrite("ve", scr_r[0], lambda e: e.reciprocal(out=scr[0][:, 0:T], in_=scr[2][:, 0:T]), reads=[scr_r[2]])
        rqr = scr_r[0]
        swrite("ac", scr_r[4], lambda e: e.activation(out=scr[4][:, 0:T], in_=psS2[:, :], func=AF.Copy, scale=cinvHD), ps_reads=["S2"])
        swrite("ve", scr_r[3], lambda e: e.tensor_mul(scr[3][:, 0:T], scr[4][:, 0:T], scr[4][:, 0:T]), reads=[scr_r[4]])
        S.wait_frees("ve", scr_r[5])
        S.wait_pair("ve", pres["S3"].ready)
        S.wait_pair("ve", scr_r[3].ready)
        v = S.do("ve", lambda e: e.scalar_tensor_tensor(
            out=scr[5][:, 0:T], in0=psS3[:, :], scalar=cinvHD, in1=scr[3][:, 0:T],
            op0=ALU.mult, op1=ALU.subtract), inc="ve")
        pres["S3"].frees.append(("ve", v))
        scr_r[3].frees.append(("ve", v))
        sready(scr_r[5], ("ve", v))
        swrite("ve", scr_r[1], lambda e: e.tensor_mul(scr[1][:, 0:T], scr[4][:, 0:T], scr[0][:, 0:T]), reads=[scr_r[4], rqr])
        swrite("ve", scr_r[2], lambda e: e.tensor_mul(scr[2][:, 0:T], scr[0][:, 0:T], scr[0][:, 0:T]), reads=[rqr])
        swrite("ve", scr_r[4], lambda e: e.tensor_mul(scr[4][:, 0:T], scr[2][:, 0:T], scr[5][:, 0:T]), reads=[scr_r[2], scr_r[5]])
        swrite("ac", scr_r[2], lambda e: e.activation(out=scr[2][:, 0:T], in_=scr[4][:, 0:T], func=AF.Sqrt, bias=ceps), reads=[scr_r[4]])
        swrite("ve", scr_r[5], lambda e: e.reciprocal(out=scr[5][:, 0:T], in_=scr[2][:, 0:T]), reads=[scr_r[2]])
        swrite("ve", scr_r[2], lambda e: e.tensor_mul(scr[2][:, 0:T], scr[6][:, 0:T], scr[0][:, 0:T]), reads=[ur, rqr])
        swrite("ve", scr_r[4], lambda e: e.tensor_sub(scr[4][:, 0:T], scr[2][:, 0:T], scr[1][:, 0:T]), reads=[scr_r[2], scr_r[1]])
        swrite("ve", scr_r[2], lambda e: e.tensor_mul(scr[2][:, 0:T], scr[4][:, 0:T], scr[5][:, 0:T]), reads=[scr_r[4], scr_r[5]])
        swrite("ve", scr_r[4], lambda e, h=h: e.tensor_scalar(
            out=scr[4][:, 0:T], in0=scr[2][:, 0:T], scalar1=gnw_sb[:, h:h + 1], scalar2=gnb_sb[:, h:h + 1],
            op0=ALU.mult, op1=ALU.add), reads=[scr_r[2]])
        pr = swrite("ve", scrb_r[3], lambda e, p=par: e.tensor_mul(
            scrb[3][:, 0:T], scr[4][:, 0:T], oh[p][:, :]), reads=[scr_r[4], oh_r[par]])
        oh_r[par].frees.append(pr)
        S.wait_pair("sy", scrb_r[3].ready)
        sv = S.do("sy", lambda e, h=h: e.dma_start(
            out=gt_sp.ap()[:, h * T:(h + 1) * T], in_=scrb[3][:, 0:T]), inc="sx", n=16)
        scrb_r[3].frees.append(("sx", sv))

        for m in range(_mbounds[h], _mbounds[h + 1]):
            hlast = emit_gateup(m, 0)
    c_gt = S.cur("sx")
    sready(hm, hlast)

    for tn, sp_t, cwait in (("t_memloc", ml_sp, c_ml), ("t_og", og_sp, c_og),
                            ("t_q", q_sp, c_q), ("t_gated", gt_sp, c_gt)):
        if tn in tap_dram:
            S.wait("gp", "sx", cwait)
            S.do("gp", lambda e, d=tap_dram[tn], s=sp_t: e.dma_start(out=d, in_=s.ap()), inc="gp_c", n=16)
    if upto <= 5:
        return finish()

    def emit_hsum(HF):
        hl = hm.ready
        S.wait_frees("ac", scr_r[2])
        mm_group("A", [(ones128[:, 0:1], h_bf[:, k * TH:(k + 1) * TH]) for k in range(NKH)],
                 sub=psA[0:1, :], pre_waits=[hl])
        hm.frees.append(pres["A"].ready)
        pa = consume("ac", ["A"], lambda e: e.copy(out=scr[2][0:1, 0:TH], in_=psA[0:1, :]))
        S.wait_pair("ve", pa)
        S.do("ve", lambda e: e.tensor_copy(out=rbf_a[:, 0:TH], in_=scr[2][0:1, 0:TH]))
        S.pad("ve")
        S.do("ve", lambda e: e.tensor_sub(rbf_b[:, 0:TH], scr[2][0:1, 0:TH], rbf_a[:, 0:TH]), inc="ve")
        S.pad("ve")
        v_hs = S.cur("ve")
        S.wait("sy", "ve", v_hs)
        S.do("sy", lambda e: e.dma_start(out=hs_hilo[0:1, 0:TH], in_=rbf_a[:, 0:TH]), inc="sx", n=16)
        S.do("sy", lambda e: e.dma_start(out=hs_hilo[1:2, 0:TH], in_=rbf_b[:, 0:TH]), inc="sx", n=16)
        scr_r[2].frees.append(("sx", S.cur("sx")))
        scr_r[2].frees.append(("ve", S.cur("ve")))
        return S.cur("sx")

    def emit_p7(HF, c_hs):
        S.wait_frees("sy", gth_r)
        S.wait("sy", "sx", c_gt)
        S.do("sy", lambda e, HF=HF: e.dma_start(
            out=gth[:, :].rearrange("p (k t) -> p k t", t=TH),
            in_=gt_sp.ap().rearrange("p (k t) -> p k t", t=T)[:, :, HF * TH:(HF + 1) * TH]),
            inc="sx", n=16)
        sready(gth_r, ("sx", S.cur("sx")))
        for m in range(NH):
            par = m % 2
            wload(dsl[par], dsl_r[par], f"w{4 + par}", wqd_t, m * 128, 128, nk=NKH)
            wload(osl[par], osl_r[par], f"w{6 + par}", wo_t, m * 128, 128)
            xr = xres_r[par]
            S.wait_frees("sy", xr)
            S.do("sy", lambda e, m=m, HF=HF, i=par: e.dma_start(
                out=xres[i][:, :], in_=x_t[m * 128:(m + 1) * 128, HALO + HF * TH:HALO + (HF + 1) * TH]),
                inc=xsem[par], n=16)
            sready(xr, (xsem[par], S.cur(xsem[par])))
            mm_group("A", [(dsl[par][:, k * 128:(k + 1) * 128], h_bf[:, k * TH:(k + 1) * TH]) for k in range(NKH)]
                     + [(ones2neg[:, :], hs_hilo[:, 0:TH])],
                     pre_waits=[dsl_r[par].ready, hm.ready, ("sx", c_hs)])
            dsl_r[par].frees.append(pres["A"].ready)
            hm.frees.append(pres["A"].ready)
            mm_group("B", [(osl[par][:, k * 128:(k + 1) * 128], gth[:, k * TH:(k + 1) * TH]) for k in range(NKD)],
                     pre_waits=[osl_r[par].ready, gth_r.ready])
            osl_r[par].frees.append(pres["B"].ready)
            gth_r.frees.append(pres["B"].ready)
            swrite("ve", scr_r[0 + par], lambda e, m=m, i=par: e.tensor_scalar_mul(
                scr[i][:, 0:TH], psA[:, :], sd_sb[:, m:m + 1]), ps_reads=["A"])
            swrite("ve", scr_r[2 + par], lambda e, i=par: e.tensor_add(
                scr[2 + i][:, 0:TH], scr[i][:, 0:TH], psB[:, :]), reads=[scr_r[0 + par]], ps_reads=["B"])
            yr = yout_r[par]
            S.wait_frees("ve", yr)
            S.wait_pair("ve", scr_r[2 + par].ready)
            S.wait_pair("ve", xr.ready)
            v = S.do("ve", lambda e, i=par: e.tensor_add(
                yout[i][:, :], scr[2 + i][:, 0:TH], xres[i][:, :]), inc="ve")
            scr_r[2 + par].frees.append(("ve", v))
            xr.frees.append(("ve", v))
            sready(yr, ("ve", v))
            S.wait_pair("sy", yr.ready)
            sv = S.do("sy", lambda e, m=m, HF=HF, i=par: e.dma_start(
                out=y_t[m * 128:(m + 1) * 128, HF * TH:(HF + 1) * TH], in_=yout[i][:, :]),
                inc="sx", n=16)
            yr.frees.append(("sx", sv))

    c_hs0 = emit_hsum(0)
    if "t_h" in tap_dram:
        S.wait_pair("sy", hm.ready)
        S.do("sy", lambda e: e.dma_start(out=tap_dram["t_h"], in_=h_bf[:, :]), inc="sx", n=16)
    emit_p7(0, c_hs0)
    load_xnh(1)
    S.wait_frees("ve", hm)
    hm.frees = []
    hlast = None
    for m in range(NKH):
        hlast = emit_gateup(m, 1)
    sready(hm, hlast)
    c_hs1 = emit_hsum(1)
    emit_p7(1, c_hs1)

    return finish()



def _col16(v):
    return np.ascontiguousarray(np.asarray(v, np.float32).reshape(16, 128).T)


def _prep_shared(inp):
    d = {}
    f = np.float32
    d["wq_t"] = np.ascontiguousarray(np.asarray(inp["L0_attn_Wq"], f).T)
    d["wk_t"] = np.ascontiguousarray(np.asarray(inp["L0_attn_Wk"], f).T)
    d["wv_t"] = np.ascontiguousarray(np.asarray(inp["L0_attn_Wv"], f).T)
    d["wo_t"] = np.ascontiguousarray(np.asarray(inp["L0_attn_Wo"], f).T)
    d["wig_t"] = np.ascontiguousarray(np.asarray(inp["L0_attn_Wig"], f).T)
    d["wog_t"] = np.ascontiguousarray(np.asarray(inp["L0_attn_Wog"], f).T)
    d["wg_t"] = np.ascontiguousarray(np.asarray(inp["L0_attn_Wg"], f).T)

    def u8T(a):
        a = np.ascontiguousarray(np.asarray(a).astype("<i4", copy=False))
        u = a.view(np.uint8).reshape(a.shape[0], a.shape[1], 4)[:, :, 0]
        return np.ascontiguousarray(u.T)

    d["wqg_t"] = u8T(inp["L0_ffn_wq_gate"])
    d["wqu_t"] = u8T(inp["L0_ffn_wq_up"])
    d["wqd_t"] = u8T(inp["L0_ffn_wq_down"])
    cw = np.asarray(inp["L0_attn_conv_w"], f)
    d["convw"] = np.ascontiguousarray(cw.reshape(16, 128, 4).transpose(1, 0, 2).reshape(128, 64))
    d["convb"] = _col16(inp["L0_attn_conv_b"])
    d["big"] = _col16(inp["L0_attn_b_ig"])
    d["bog"] = _col16(inp["L0_attn_b_og"])
    d["bg"] = np.ascontiguousarray(np.asarray(inp["L0_attn_b_g"], f).reshape(16, 1))
    d["gnw"] = _col16(inp["L0_attn_gn_w"])
    d["gnb"] = _col16(inp["L0_attn_gn_b"])
    d["mnw"] = _col16(inp["L0_attn_mn_w"])
    d["mnb"] = _col16(inp["L0_attn_mn_b"])
    d["sgc"] = np.ascontiguousarray(np.asarray(inp["L0_ffn_s_gate"], f).reshape(44, 128).T)
    d["suc"] = np.ascontiguousarray(np.asarray(inp["L0_ffn_s_up"], f).reshape(44, 128).T)
    d["sdc"] = _col16(inp["L0_ffn_s_down"])
    oh2 = np.zeros((32, NH * 128), ml_dtypes.bfloat16)
    for h in range(NH):
        oh2[h, h * 128:(h + 1) * 128] = 1
        oh2[16 + h, h * 128:(h + 1) * 128] = 1
    d["oneh2"] = oh2
    return d


def _prep_core(inp, c):
    b, j = c // 4, c % 4
    x = np.asarray(inp["L0_attn_x"], np.float32)
    t0 = j * T
    if j == 0:
        seg = np.concatenate([np.zeros((HALO, D), np.float32), x[b, 0:T]], axis=0)
    else:
        seg = x[b, t0 - HALO:t0 + T]
    d = {"x_t": np.ascontiguousarray(seg.T)}
    selc = np.zeros((128, 8), np.float32)
    for i in range(8):
        bi, ji = i // 4, i % 4
        if bi == b and ji < j:
            selc[:, i] = 1.0
    d["selc"] = selc
    d["invc"] = np.ascontiguousarray(1.0 - selc)
    return d


def kernel(**inputs):
    shared = _prep_shared(inputs)
    nc = _build()
    in_maps = []
    for c in range(NCORES):
        m = dict(shared)
        m.update(_prep_core(inputs, c))
        in_maps.append(m)
    res = run_bass_kernel_spmd(nc, in_maps, core_ids=list(range(NCORES)))
    y = np.empty((B, TFULL, D), np.float32)
    for c in range(NCORES):
        b, j = c // 4, c % 4
        y[b, j * T:(j + 1) * T, :] = res.results[c]["y_t"].T
    return y


if __name__ == "__main__":
    nc = _build()
    print("build ok; instructions:", nc.n_instructions())


# version 03
# speedup vs baseline: 1.1563x; 1.0746x over previous
import numpy as np
import ml_dtypes
from contextlib import ExitStack

import concourse.bass as bass
import concourse.mybir as mybir
from concourse.bass_utils import run_bass_kernel_spmd

F32 = mybir.dt.float32
BF16 = mybir.dt.bfloat16
U8 = mybir.dt.uint8
AF = mybir.ActivationFunctionType
ALU = mybir.AluOpType
AX = mybir.AxisListType

NCORES = 8
B, TFULL, D = 2, 4096, 2048
NH, HD = 16, 128
HID = 5632
NKD = 16
NKH = 44
T = 1024
TH = 512
HALO = 3
NT = T + HALO
EPS = 1e-5

ENGS = ("pe", "ve", "ac", "sy", "gp")
ESEM = {"pe": "pe", "ve": "ve", "ac": "ac", "sy": "sy_c", "gp": "gp_c"}


class Sched:
    def __init__(self, nc, ctx):
        self.nc = nc
        self.prog = {e: [] for e in ENGS}
        self.cnt = {}
        self.seen = {e: {} for e in ENGS}
        self.sems = {}
        self.ctx = ctx

    def sem(self, name):
        if name not in self.sems:
            self.sems[name] = self.ctx.enter_context(self.nc.semaphore(name))
            self.cnt.setdefault(name, 0)
        return self.sems[name]

    def wait(self, eng, sem_name, val):
        if val is None or val <= 0:
            return
        if self.seen[eng].get(sem_name, 0) >= val:
            return
        self.seen[eng][sem_name] = val
        s = self.sem(sem_name)
        self.prog[eng].append(lambda e, s=s, v=val: e.wait_ge(s, v))

    def wait_pair(self, eng, pair):
        if pair is not None:
            self.wait(eng, pair[0], pair[1])

    def wait_frees(self, eng, res):
        for (s, v) in res.frees:
            self.wait(eng, s, v)

    def do(self, eng, fn, inc=None, n=1):
        if inc is None:
            self.prog[eng].append(fn)
            return None
        self.cnt[inc] = self.cnt.get(inc, 0) + n
        tgt = self.cnt[inc]
        s = self.sem(inc)
        self.prog[eng].append(lambda e, fn=fn, s=s, n=n: fn(e).then_inc(s, n))
        return tgt

    def cur(self, sem_name):
        return self.cnt.get(sem_name, 0)

    def pad(self, eng, n=2):
        sem = {"pe": "pe", "ve": "ve", "ac": "ac", "sy": "sy_c", "gp": "gp_c"}[eng]
        self.cnt[sem] = self.cnt.get(sem, 0) + 1
        tgt = self.cnt[sem]
        s = self.sem(sem)
        self.prog[eng].append(lambda e, s=s: e.engine_nop().then_inc(s, 1))
        self.prog[eng].append(lambda e, s=s, v=tgt: e.wait_ge(s, v))
        self.seen[eng][sem] = max(self.seen[eng].get(sem, 0), tgt)


class Res:
    __slots__ = ("ready", "frees")

    def __init__(self):
        self.ready = None
        self.frees = []


def _build(taps=(), upto=99, skip=()):
    nc = bass.Bass("TRN2", target_bir_lowering=False, debug=False)

    def I(name, shape, dt):
        return nc.dram_tensor(name, shape, dt, kind="ExternalInput").ap()

    x_t = I("x_t", [D, NT], F32)
    wq_t = I("wq_t", [D, D], F32)
    wk_t = I("wk_t", [D, D], F32)
    wv_t = I("wv_t", [D, D], F32)
    wo_t = I("wo_t", [D, D], F32)
    wig_t = I("wig_t", [D, D], F32)
    wog_t = I("wog_t", [D, D], F32)
    wg_t = I("wg_t", [D, NH], F32)
    wqg_t = I("wqg_t", [D, HID], U8)
    wqu_t = I("wqu_t", [D, HID], U8)
    wqd_t = I("wqd_t", [HID, D], U8)
    convw = I("convw", [128, 64], F32)
    convb = I("convb", [128, 16], F32)
    big = I("big", [128, 16], F32)
    bog = I("bog", [128, 16], F32)
    bg = I("bg", [16, 1], F32)
    gnw = I("gnw", [128, 16], F32)
    gnb = I("gnb", [128, 16], F32)
    mnw = I("mnw", [128, 16], F32)
    mnb = I("mnb", [128, 16], F32)
    sgc = I("sgc", [128, 44], F32)
    suc = I("suc", [128, 44], F32)
    sdc = I("sdc", [128, 16], F32)
    oneh2 = I("oneh2", [32, NH * 128], BF16)
    selc = I("selc", [128, 8], F32)
    invc = I("invc", [128, 8], F32)

    y_t = nc.dram_tensor("y_t", [D, T], F32, kind="ExternalOutput").ap()

    cc_in = nc.dram_tensor("cc_in", [128, 32], F32)
    cc_out = nc.dram_tensor("cc_out", [NCORES * 128, 32], F32)
    xn_sp = nc.dram_tensor("xn_sp", [128, NKD * T], BF16)
    ml_sp = nc.dram_tensor("ml_sp", [128, NH * T], BF16)
    og_sp = nc.dram_tensor("og_sp", [128, NH * T], BF16)
    q_sp = nc.dram_tensor("q_sp", [128, NH * T], BF16)
    gt_sp = nc.dram_tensor("gt_sp", [128, NH * T], BF16)

    tap_dram = {}
    for tname, shape, dt in [
        ("t_xn", [128, NKD * T], BF16), ("t_xconv", [128, NKD * T], BF16),
        ("t_memloc", [128, NH * T], BF16), ("t_q", [128, NH * T], BF16),
        ("t_og", [128, NH * T], BF16), ("t_gated", [128, NH * T], BF16),
        ("t_h", [128, NKH * TH], BF16), ("t_gamma", [16, T], F32),
        ("t_state", [128, 16], F32), ("t_gin", [128, 32], F32),
        ("t_gbuf", [128, 8 * 32], F32), ("t_w", [16, 104], F32), ("t_can", [128, 4096], F32), ("t_sel", [128, 64], F32),
    ]:
        if tname in taps:
            tap_dram[tname] = nc.dram_tensor(tname, shape, dt, kind="ExternalOutput").ap()

    ctx = ExitStack()
    S = Sched(nc, ctx)
    E = ctx.enter_context
    open_stacks = []

    def finish():
        if "t_can" in tap_dram:
            tc = tap_dram["t_can"]
            for eng2 in ("ve", "ac", "pe", "gp_c", "gp_w"):
                if eng2 in ("gp_c", "gp_w"):
                    S.wait("sy", eng2, S.cur(eng2))
                else:
                    S.wait("sy", eng2, S.cur(eng2))
            for (src_t, r0, r1, c0, c1) in [
                (convw_sb, 0, 128, 0, 64), (convb_sb, 0, 128, 64, 80), (big_sb, 0, 128, 80, 96),
                (bog_sb, 0, 128, 96, 112), (gnw_sb, 0, 128, 112, 128), (gnb_sb, 0, 128, 128, 144),
                (mnw_sb, 0, 128, 144, 160), (mnb_sb, 0, 128, 160, 176), (sg_sb, 0, 128, 176, 220),
                (su_sb, 0, 128, 220, 264), (sd_sb, 0, 128, 264, 280), (mb8_sb, 0, 16, 280, 344),
                (malog_sb, 0, 16, 344, 352), (bg_sb, 0, 16, 352, 353), (cst, 0, 128, 353, 361),
                (gam_sb, 0, 16, 1024, 2048), (gbuf, 0, 128, 2048, 2184), (state_sb, 0, 128, 2184, 2200),
            ]:
                S.do("sy", lambda e, s=src_t, r0=r0, r1=r1, c0=c0, c1=c1: e.dma_start(
                    out=tc[r0:r1, c0:c1], in_=s[0:r1 - r0, 0:c1 - c0],
                    allow_slow_non_contiguous=True), inc="sx", n=16)
        S.wait("sy", "sx", S.cur("sx"))
        S.wait("sy", "gp_c", S.cur("gp_c"))
        blk = ctx.enter_context(nc.Block())

        def run_for(key):
            def body(eng):
                for fn in S.prog[key]:
                    fn(eng)
            return body

        blk.sync(run_for("sy"))
        blk.vector(run_for("ve"))
        blk.scalar(run_for("ac"))
        blk.tensor(run_for("pe"))
        blk.gpsimd(run_for("gp"))
        for stk in reversed(open_stacks):
            stk.close()
        ctx.close()
        return nc

    convw_sb = E(nc.sbuf_tensor("convw_sb", [128, 64], F32))
    convb_sb = E(nc.sbuf_tensor("convb_sb", [128, 16], F32))
    big_sb = E(nc.sbuf_tensor("big_sb", [128, 16], F32))
    bog_sb = E(nc.sbuf_tensor("bog_sb", [128, 16], F32))
    bg_sb = E(nc.sbuf_tensor("bg_sb", [16, 1], F32))
    gnw_sb = E(nc.sbuf_tensor("gnw_sb", [128, 16], F32))
    gnb_sb = E(nc.sbuf_tensor("gnb_sb", [128, 16], F32))
    mnw_sb = E(nc.sbuf_tensor("mnw_sb", [128, 16], F32))
    mnb_sb = E(nc.sbuf_tensor("mnb_sb", [128, 16], F32))
    sg_sb = E(nc.sbuf_tensor("sg_sb", [128, 44], F32))
    su_sb = E(nc.sbuf_tensor("su_sb", [128, 44], F32))
    sd_sb = E(nc.sbuf_tensor("sd_sb", [128, 16], F32))
    oneh2_sb = E(nc.sbuf_tensor("oneh2_sb", [32, NH * 128], BF16))
    selc_sb = E(nc.sbuf_tensor("selc_sb", [128, 8], F32))
    invc_sb = E(nc.sbuf_tensor("invc_sb", [128, 8], F32))
    ones128 = E(nc.sbuf_tensor("ones128", [128, 128], BF16))
    ones2neg = E(nc.sbuf_tensor("ones2neg", [2, 128], BF16))
    cst = E(nc.sbuf_tensor("cst", [128, 8], F32))
    lowsl = [E(nc.sbuf_tensor(f"lowsl{i}", [128, NKD * 128], BF16)) for i in range(6)]
    dsl_lo = [E(nc.sbuf_tensor(f"dsllo{i}", [128, NKH * 128], BF16)) for i in range(2)]
    xb_lo = [E(nc.sbuf_tensor(f"xblo{i}", [128, NT], BF16)) for i in range(2)]
    wgt_sb = E(nc.sbuf_tensor("wgt_sb", [128, NKD * 16], BF16))
    gam_hilo = E(nc.sbuf_tensor("gam_hilo", [32, T], BF16))
    g_in = E(nc.sbuf_tensor("g_in", [128, 32], F32))
    gbuf = E(nc.sbuf_tensor("gbuf", [128, 8 * 32], F32))
    state_sb = E(nc.sbuf_tensor("state_sb", [128, 16], F32))
    xns_hilo = E(nc.sbuf_tensor("xns_hilo", [2, T], BF16))
    hs_hilo = E(nc.sbuf_tensor("hs_hilo", [2, T], BF16))
    rbf_a = E(nc.sbuf_tensor("rbf_a", [1, T], BF16))
    rbf_b = E(nc.sbuf_tensor("rbf_b", [1, T], BF16))
    small2 = E(nc.sbuf_tensor("small2", [128, 48], F32))

    scrA = [E(nc.sbuf_tensor(f"scra{i}", [128, 1040], F32)) for i in range(4)]
    scrb = [E(nc.sbuf_tensor(f"scrb{i}", [128, 1040], BF16)) for i in range(4)]

    psS1 = E(nc.psum_tensor("psS1", [128, 1024], F32))
    psS2 = E(nc.psum_tensor("psS2", [128, 1024], F32))
    psS3 = E(nc.psum_tensor("psS3", [128, 1024], F32))
    psA = E(nc.psum_tensor("psA", [128, 512], F32))
    psB = E(nc.psum_tensor("psB", [128, 512], F32))
    pres = {"S1": Res(), "S2": Res(), "S3": Res(), "A": Res(), "B": Res()}
    pst = {"S1": psS1, "S2": psS2, "S3": psS3, "A": psA, "B": psB}

    stk_scrB = ExitStack()
    open_stacks.append(stk_scrB)
    scrB = [stk_scrB.enter_context(nc.sbuf_tensor(f"scrx{i}", [128, 1040], F32)) for i in range(4)]
    scr = scrA + scrB
    scr_r = [Res() for _ in range(8)]
    scrb_r = [Res() for _ in range(4)]

    marks = {}

    def mark(name, pair):
        marks[name] = pair

    def wmark(eng, name):
        S.wait_pair(eng, marks.get(name))

    def sready(res, pair):
        res.ready = pair
        res.frees = []

    def suse(eng, res_list, fn, ps_reads=(), also=()):
        for r in res_list:
            S.wait_pair(eng, r.ready)
        for p in ps_reads:
            S.wait_pair(eng, pres[p].ready)
        for (sn, v) in also:
            S.wait(eng, sn, v)
        v = S.do(eng, fn, inc=ESEM[eng])
        pr = (ESEM[eng], v)
        for r in res_list:
            r.frees.append(pr)
        for p in ps_reads:
            pres[p].frees.append(pr)
        return pr

    def swrite(eng, res, fn, reads=(), ps_reads=(), also=()):
        S.wait_frees(eng, res)
        for r in reads:
            S.wait_pair(eng, r.ready)
        for p in ps_reads:
            S.wait_pair(eng, pres[p].ready)
        for (sn, v) in also:
            S.wait(eng, sn, v)
        v = S.do(eng, fn, inc=ESEM[eng])
        pr = (ESEM[eng], v)
        sready(res, pr)
        for r in reads:
            r.frees.append(pr)
        for p in ps_reads:
            pres[p].frees.append(pr)
        return pr

    def mm_group(ps_name, mms, sub=None, pre_waits=()):
        r = pres[ps_name]
        S.wait_frees("pe", r)
        for pw in pre_waits:
            if pw is not None:
                S.wait("pe", pw[0], pw[1])
        out_ap = sub if sub is not None else pst[ps_name][:, :]
        n = len(mms)
        lv = None
        for i, (l, rr) in enumerate(mms):
            st, sp = (i == 0), (i == n - 1)
            if sp:
                lv = S.do("pe", lambda e, o=out_ap, l=l, rr=rr, st=st: e.matmul(o, l, rr, start=st, stop=True), inc="pe")
            else:
                S.do("pe", lambda e, o=out_ap, l=l, rr=rr, st=st: e.matmul(o, l, rr, start=st, stop=False))
        sready(r, ("pe", lv))
        return lv

    def mm_group_pair(nmA, nmB, lhsTs, rhsA, rhsB, subA=None, subB=None, pre_waits=()):
        rA, rB = pres[nmA], pres[nmB]
        S.wait_frees("pe", rA)
        S.wait_frees("pe", rB)
        for pw in pre_waits:
            if pw is not None:
                S.wait("pe", pw[0], pw[1])
        oA = subA if subA is not None else pst[nmA][:, :]
        oB = subB if subB is not None else pst[nmB][:, :]
        n = len(lhsTs)
        for i in range(n):
            st, sp = (i == 0), (i == n - 1)
            if sp:
                lvA = S.do("pe", lambda e, o=oA, l=lhsTs[i], rr=rhsA[i], st=st: e.matmul(o, l, rr, start=st, stop=True), inc="pe")
                lvB = S.do("pe", lambda e, o=oB, l=lhsTs[i], rr=rhsB[i], st=st: e.matmul(o, l, rr, start=st, stop=True), inc="pe")
            else:
                S.do("pe", lambda e, o=oA, l=lhsTs[i], rr=rhsA[i], st=st: e.matmul(o, l, rr, start=st, stop=False))
                S.do("pe", lambda e, o=oB, l=lhsTs[i], rr=rhsB[i], st=st: e.matmul(o, l, rr, start=st, stop=False))
        sready(rA, ("pe", lvA))
        sready(rB, ("pe", lvB))
        return lvA, lvB

    def consume(eng, ps_names, fn, also=()):
        for p in ps_names:
            S.wait_pair(eng, pres[p].ready)
        for (sn, v) in also:
            S.wait(eng, sn, v)
        v = S.do(eng, fn, inc=ESEM[eng])
        pr = (ESEM[eng], v)
        for p in ps_names:
            pres[p].frees.append(pr)
        return pr

    def wload(slot, slot_r, semn, wt_dram, c0, w, nk=NKD):
        S.wait_frees("gp", slot_r)
        S.do("gp", lambda e, s=slot, src=wt_dram, c0=c0, w=w, nk=nk: e.dma_start(
            out=s[:, 0:nk * w].rearrange("p (k w) -> p k w", w=w),
            in_=src[:, c0:c0 + w].rearrange("(k p) w -> p k w", p=128)), inc=semn, n=16)
        sready(slot_r, (semn, S.cur(semn)))

    for dst, src in [
        (convw_sb, convw), (convb_sb, convb), (big_sb, big), (bog_sb, bog),
        (bg_sb, bg), (gnw_sb, gnw), (gnb_sb, gnb), (mnw_sb, mnw), (mnb_sb, mnb),
        (sg_sb, sgc), (su_sb, suc), (sd_sb, sdc), (oneh2_sb, oneh2),
        (selc_sb, selc), (invc_sb, invc),
    ]:
        S.do("sy", lambda e, d=dst, s=src: e.dma_start(out=d[:, :], in_=s), inc="sx", n=16)
    c_consts = S.cur("sx")
    S.do("gp", lambda e: e.dma_start(
        out=wgt_sb[:, :].rearrange("p (k w) -> p k w", w=16),
        in_=wg_t.rearrange("(k p) w -> p k w", p=128)), inc="gp_w", n=16)
    g_wg = S.cur("gp_w")
    S.do("ve", lambda e: e.memset(ones128[:, :], 1.0), inc="ve")
    v_ones = S.cur("ve")
    S.do("ve", lambda e: e.memset(ones2neg[:, :], -128.0), inc="ve")
    v_ones2 = S.cur("ve")
    S.do("ve", lambda e: e.memset(cst[:, 0:1], EPS))
    S.do("ve", lambda e: e.memset(cst[:, 1:2], 1.0 / D))
    S.do("ve", lambda e: e.memset(cst[:, 2:3], 1.0 / HD))
    S.do("ve", lambda e: e.memset(cst[:, 3:4], 1e-24))
    S.do("ve", lambda e: e.memset(cst[:, 4:5], -80.0))
    S.do("ve", lambda e: e.memset(cst[:, 5:6], -0.5), inc="ve")
    v_cst = S.cur("ve")
    ceps, cinvD, cinvHD, c24 = cst[:, 0:1], cst[:, 1:2], cst[:, 2:3], cst[:, 3:4]
    cm80 = cst[:, 4:5]
    cmhalf = cst[:, 5:6]
    S.wait("ac", "ve", v_cst)
    for _eng in ("ve", "ac", "pe", "gp"):
        S.wait(_eng, "sx", c_consts)

    rP12 = ExitStack()
    open_stacks.append(rP12)
    xn_bf = rP12.enter_context(nc.sbuf_tensor("xn_bf", [128, NKD * T], BF16))
    xconv_bf = rP12.enter_context(nc.sbuf_tensor("xconv_bf", [128, NKD * T], BF16))

    r1a = ExitStack()
    open_stacks.append(r1a)
    xt = [r1a.enter_context(nc.sbuf_tensor(f"xt{i}", [128, NT], F32)) for i in range(2)]
    xb = xb_lo
    mu_bc = r1a.enter_context(nc.sbuf_tensor("mu_bc", [128, NT], F32))
    r_bc = r1a.enter_context(nc.sbuf_tensor("r_bc", [128, NT], F32))
    xt_r = [Res(), Res()]
    xb_r = [Res(), Res()]
    mu_r, r_r = Res(), Res()
    xsem = ["xA", "xB"]
    bsem = ["w0", "w1"]

    chunks = [(0, TH), (TH, 2 * TH), (2 * TH, NT)]
    if "nA" in skip:
        S.do("ve", lambda e: e.memset(mu_bc[:, :], 0.0))
        v = S.do("ve", lambda e: e.memset(r_bc[:, :], 1.0), inc="ve")
        sready(mu_r, ("ve", v)); sready(r_r, ("ve", v))
    for k in range(NKD if "nA" not in skip else 0):
        par = k % 2
        tr, br = xt_r[par], xb_r[par]
        S.wait_frees("sy", tr)
        S.do("sy", lambda e, k=k, p=par: e.dma_start(
            out=xt[p][:, :], in_=x_t[k * 128:(k + 1) * 128, :]), inc=xsem[par], n=16)
        sready(tr, (xsem[par], S.cur(xsem[par])))
        S.wait_frees("gp", br)
        S.do("gp", lambda e, k=k, p=par: e.dma_start(
            out=xb[p][:, :], in_=x_t[k * 128:(k + 1) * 128, :]), inc=bsem[par], n=16)
        sready(br, (bsem[par], S.cur(bsem[par])))
        sb = scrb_r[par]
        swrite("ac", sb, lambda e, p=par: e.square(out=scrb[p][:, 0:NT], in_=xt[p][:, :]), reads=[tr])
        S.wait("pe", "ve", v_ones)
        S.wait_pair("pe", br.ready)
        S.wait_pair("pe", sb.ready)
        if k == 0:
            for nm in ("S1", "S2", "A", "B"):
                S.wait_frees("pe", pres[nm])
        lv1 = None
        for ci, (c0, c1) in enumerate(chunks):
            tgt = psA[:, 0:3] if c1 == NT else psS1[:, c0:c1]
            lv1 = S.do("pe", lambda e, k=k, p=par, c0=c0, c1=c1, t=tgt: e.matmul(
                t, ones128[:, :], xb[p][:, c0:c1],
                start=(k == 0), stop=(k == NKD - 1)), inc="pe" if ci == 2 else None)
        br.frees.append(("pe", lv1))
        lv = None
        for ci, (c0, c1) in enumerate(chunks):
            tgt = psB[:, 0:3] if c1 == NT else psS2[:, c0:c1]
            lv = S.do("pe", lambda e, k=k, p=par, c0=c0, c1=c1, t=tgt: e.matmul(
                t, ones128[:, :], scrb[p][:, c0:c1],
                start=(k == 0), stop=(k == NKD - 1)), inc="pe" if ci == 2 else None)
        sb.frees.append(("pe", lv))
        tr.frees.append(("ac", sb.ready[1]))
    if "nA" not in skip:
        for nm in ("S1", "S2", "A", "B"):
            sready(pres[nm], ("pe", lv))
    if "nA" not in skip:
        p1 = consume("ac", ["S1"], lambda e: e.activation(out=mu_bc[:, 0:T], in_=psS1[:, :], func=AF.Copy, scale=cinvD))
        p2 = consume("ac", ["A"], lambda e: e.activation(out=mu_bc[:, T:NT], in_=psA[:, 0:3], func=AF.Copy, scale=cinvD))
        sready(mu_r, ("ac", p2[1]))
        swrite("ve", scr_r[6], lambda e: e.tensor_mul(scr[6][:, 0:NT], mu_bc[:, :], mu_bc[:, :]), reads=[mu_r])
        S.wait_frees("ve", scr_r[7])
        S.wait_pair("ve", pres["S2"].ready)
        S.wait_pair("ve", scr_r[6].ready)
        S.do("ve", lambda e: e.scalar_tensor_tensor(
            out=scr[7][:, 0:T], in0=psS2[:, :], scalar=cinvD, in1=scr[6][:, 0:T],
            op0=ALU.mult, op1=ALU.subtract), inc="ve")
        S.pad("ve")
        S.wait_pair("ve", pres["B"].ready)
        v2 = S.do("ve", lambda e: e.scalar_tensor_tensor(
            out=scr[7][:, T:NT], in0=psB[:, 0:3], scalar=cinvD, in1=scr[6][:, T:NT],
            op0=ALU.mult, op1=ALU.subtract), inc="ve")
        for nm in ("S2", "B"):
            pres[nm].frees.append(("ve", v2))
        scr_r[6].frees.append(("ve", v2))
        sready(scr_r[7], ("ve", v2))
        swrite("ac", scr_r[6], lambda e: e.activation(out=scr[6][:, 0:NT], in_=scr[7][:, 0:NT], func=AF.Ln, bias=ceps), reads=[scr_r[7]])
        pr = suse("ac", [scr_r[6]], lambda e: e.activation(out=r_bc[:, :], in_=scr[6][:, 0:NT], func=AF.Exp, scale=cmhalf))
        sready(r_r, pr)


    for k in range(NKD):
        par = k % 2
        tr = xt_r[par]
        S.wait_frees("sy", tr)
        S.do("sy", lambda e, k=k, p=par: e.dma_start(
            out=xt[p][:, :], in_=x_t[k * 128:(k + 1) * 128, :]), inc=xsem[par], n=16)
        sready(tr, (xsem[par], S.cur(xsem[par])))
        i0, i1, ia = 3 * par, 3 * par + 1, (2 if par == 0 else 6)
        t1r, xnr, accr = scr_r[i0], scr_r[i1], scr_r[ia]
        swrite("ve", t1r, lambda e, p=par, i0=i0: e.tensor_sub(scr[i0][:, 0:NT], xt[p][:, :], mu_bc[:, :]), reads=[tr, mu_r])
        swrite("ve", xnr, lambda e, i0=i0, i1=i1: e.tensor_mul(scr[i1][:, 0:NT], scr[i0][:, 0:NT], r_bc[:, :]), reads=[t1r, r_r])
        suse("ac", [xnr], lambda e, k=k, i1=i1: e.copy(out=xn_bf[:, k * T:(k + 1) * T], in_=scr[i1][:, HALO:NT]))
        swrite("ve", t1r, lambda e, k=k, i0=i0, i1=i1: e.tensor_scalar_mul(
            scr[i0][:, 0:T], scr[i1][:, 0:T], convw_sb[:, 4 * k:4 * k + 1]), reads=[xnr])
        swrite("ve", accr, lambda e, k=k, i0=i0, i1=i1, ia=ia: e.scalar_tensor_tensor(
            out=scr[ia][:, 0:T], in0=scr[i1][:, 1:T + 1], scalar=convw_sb[:, 4 * k + 1:4 * k + 2],
            in1=scr[i0][:, 0:T], op0=ALU.mult, op1=ALU.add), reads=[xnr, t1r])
        swrite("ve", t1r, lambda e, k=k, i0=i0, i1=i1, ia=ia: e.scalar_tensor_tensor(
            out=scr[i0][:, 0:T], in0=scr[i1][:, 2:T + 2], scalar=convw_sb[:, 4 * k + 2:4 * k + 3],
            in1=scr[ia][:, 0:T], op0=ALU.mult, op1=ALU.add), reads=[xnr, accr])
        swrite("ve", accr, lambda e, k=k, i0=i0, i1=i1, ia=ia: e.scalar_tensor_tensor(
            out=scr[ia][:, 0:T], in0=scr[i1][:, 3:T + 3], scalar=convw_sb[:, 4 * k + 3:4 * k + 4],
            in1=scr[i0][:, 0:T], op0=ALU.mult, op1=ALU.add), reads=[xnr, t1r])
        suse("ac", [accr], lambda e, k=k, ia=ia: e.activation(
            out=xconv_bf[:, k * T:(k + 1) * T], in_=scr[ia][:, 0:T], func=AF.Silu, bias=convb_sb[:, k:k + 1]))
    mark("xn", ("ac", S.cur("ac")))
    mark("xconv", ("ac", S.cur("ac")))

    for hf in range(2 if "nS" not in skip else 0):
        nm = "A" if hf == 0 else "B"
        mms = [(ones128[:, 0:1], xn_bf[:, k * T + hf * TH:k * T + hf * TH + TH]) for k in range(NKD)]
        mm_group(nm, mms, sub=pst[nm][0:1, :], pre_waits=[marks["xn"]])
    if "nS" not in skip:
        consume("ac", ["A"], lambda e: e.copy(out=scr[2][0:1, 0:TH], in_=psA[0:1, :]))
        pa = consume("ac", ["B"], lambda e: e.copy(out=scr[2][0:1, TH:T], in_=psB[0:1, :]))
        S.wait_pair("ve", pa)
        S.do("ve", lambda e: e.tensor_copy(out=rbf_a[:, :], in_=scr[2][0:1, 0:T]))
        S.pad("ve")
        S.do("ve", lambda e: e.tensor_sub(rbf_b[:, :], scr[2][0:1, 0:T], rbf_a[:, :]), inc="ve")
        S.pad("ve")
        v_xns = S.cur("ve")
        S.wait("sy", "ve", v_xns)
        S.do("sy", lambda e: e.dma_start(out=xns_hilo[0:1, :], in_=rbf_a[:, :]), inc="sx", n=16)
        S.do("sy", lambda e: e.dma_start(out=xns_hilo[1:2, :], in_=rbf_b[:, :]), inc="sx", n=16)
    c_xns = S.cur("sx")

    if "nS" not in skip:
        wmark("sy", "xn")
        S.do("sy", lambda e: e.dma_start(out=xn_sp.ap(), in_=xn_bf[:, :]), inc="sx", n=16)
    c_spill = S.cur("sx")
    if "t_xn" in tap_dram:
        S.do("sy", lambda e: e.dma_start(out=tap_dram["t_xn"], in_=xn_bf[:, :]), inc="sx", n=16)
    if "t_xconv" in tap_dram:
        wmark("sy", "xconv")
        S.do("sy", lambda e: e.dma_start(out=tap_dram["t_xconv"], in_=xconv_bf[:, :]), inc="sx", n=16)
    r1a.close()
    open_stacks.remove(r1a)
    if upto <= 1:
        return finish()

    for hf in range(2 if "nG" not in skip else 0):
        sub = psS1[0:16, hf * TH:(hf + 1) * TH]
        mms = [(wgt_sb[:, k * 16:(k + 1) * 16],
                xconv_bf[:, k * T + hf * TH:k * T + hf * TH + TH]) for k in range(NKD)]
        mm_group("S1", mms, sub=sub, pre_waits=[("gp_w", g_wg), marks["xconv"]])
    if "nG" in skip:
        c_gh = S.cur("sx")
        v_lpt = S.cur("ve")
        pg = None
    _g_on = "nG" not in skip
    if _g_on:
        S.wait_frees("ac", scr_r[0])
    pg = pg if not _g_on else consume("ac", ["S1"], lambda e: e.activation(
        out=scr[0][0:16, 0:T], in_=psS1[0:16, :], func=AF.Sigmoid, bias=bg_sb[:, :]), also=[("sx", c_consts)])
    if _g_on:
        sready(scr_r[0], pg)
        S.wait_pair("ve", pg)
        S.do("ve", lambda e: e.tensor_copy(out=scrb[0][0:16, 0:T], in_=scr[0][0:16, 0:T]))
        S.pad("ve")
        S.do("ve", lambda e: e.tensor_sub(scr[0][0:16, 0:T], scr[0][0:16, 0:T], scrb[0][0:16, 0:T]))
        S.pad("ve")
        S.do("ve", lambda e: e.tensor_copy(out=scrb[1][0:16, 0:T], in_=scr[0][0:16, 0:T]), inc="ve")
        v_gh = S.cur("ve")
        scr_r[0].frees.append(("ve", v_gh))
        S.wait("sy", "ve", v_gh)
        S.do("sy", lambda e: e.dma_start(out=gam_hilo[0:16, :], in_=scrb[0][0:16, 0:T]), inc="sx", n=16)
        S.do("sy", lambda e: e.dma_start(out=gam_hilo[16:32, :], in_=scrb[1][0:16, 0:T]), inc="sx", n=16)
        c_gh = S.cur("sx")
        v_lpt = S.cur("ve")
        if "t_gamma" in tap_dram:
            S.wait_pair("sy", pg)
            S.do("sy", lambda e: e.dma_start(out=tap_dram["t_gamma"], in_=scr[0][0:16, 0:T]), inc="sx", n=16)
            scr_r[0].frees.append(("sx", S.cur("sx")))
    if upto <= 2:
        return finish()

    r_slots = ExitStack()
    open_stacks.append(r_slots)
    ksl = [lowsl[i] for i in range(2)]
    vsl = [lowsl[2 + i] for i in range(2)]
    isl = [lowsl[4 + i] for i in range(2)]
    ksl_r = [Res(), Res()]
    vsl_r = [Res(), Res()]
    isl_r = [Res(), Res()]

    for h in range(NH):
        par = h % 2
        if "wl" not in skip:
            wload(ksl[par], ksl_r[par], f"w{2 + par}", wk_t, h * 128, 128)
            wload(vsl[par], vsl_r[par], f"w{4 + par}", wv_t, h * 128, 128)
            wload(isl[par], isl_r[par], f"w{6 + par}", wig_t, h * 128, 128)
        if "mm" in skip:
            continue
        xk = [ksl[par][:, k * 128:(k + 1) * 128] for k in range(NKD)]
        xv = [vsl[par][:, k * 128:(k + 1) * 128] for k in range(NKD)]
        xi = [isl[par][:, k * 128:(k + 1) * 128] for k in range(NKD)]
        xnA = [xn_bf[:, k * T:k * T + TH] for k in range(NKD)]
        xnB = [xn_bf[:, k * T + TH:k * T + T] for k in range(NKD)]
        xcA = [xconv_bf[:, k * T:k * T + TH] for k in range(NKD)]
        xcB = [xconv_bf[:, k * T + TH:k * T + T] for k in range(NKD)]
        if "serk" in skip:
            mm_group("A", list(zip(xk, xnA)), pre_waits=[ksl_r[par].ready, marks["xn"]])
            mm_group("B", list(zip(xk, xnB)))
        else:
            mm_group_pair("A", "B", xk, xnA, xnB, pre_waits=[ksl_r[par].ready, marks["xn"]])
        ksl_r[par].frees.append(pres["A"].ready)
        ksl_r[par].frees.append(pres["B"].ready)
        if "serv" in skip:
            mm_group("S1", list(zip(xv, xnA)), sub=psS1[:, 0:TH], pre_waits=[vsl_r[par].ready])
            mm_group("S1", list(zip(xv, xnB)), sub=psS1[:, TH:2 * TH])
        else:
            mm_group_pair("S1", "S1", xv, xnA, xnB, subA=psS1[:, 0:TH], subB=psS1[:, TH:2 * TH],
                          pre_waits=[vsl_r[par].ready])
        vsl_r[par].frees.append(pres["S1"].ready)
        if "seri" in skip:
            mm_group("S3", list(zip(xi, xcA)), sub=psS3[:, 0:TH], pre_waits=[isl_r[par].ready, marks["xconv"]])
            mm_group("S3", list(zip(xi, xcB)), sub=psS3[:, TH:2 * TH])
        else:
            mm_group_pair("S3", "S3", xi, xcA, xcB, subA=psS3[:, 0:TH], subB=psS3[:, TH:2 * TH],
                          pre_waits=[isl_r[par].ready, marks["xconv"]])
        isl_r[par].frees.append(pres["S3"].ready)
        if "evac" in skip:
            continue
        S.wait_frees("ac", scr_r[0]); scr_r[0].frees = []
        S.wait_frees("ac", scr_r[1]); scr_r[1].frees = []
        S.wait_frees("ac", scrb_r[2]); scrb_r[2].frees = []
        S.wait_frees("ve", scr_r[2]); scr_r[2].frees = []
        for hf in range(2):
            kps = psA if hf == 0 else psB
            knm = "A" if hf == 0 else "B"
            if "e1" not in skip:
                pr = consume("ac", ["S1"], lambda e, hf=hf: e.copy(out=scr[0][:, hf * TH:(hf + 1) * TH], in_=psS1[:, hf * TH:(hf + 1) * TH]))
                scr_r[0].ready = pr
            if "e2" not in skip:
                pr = consume("ac", ["S3"], lambda e, hf=hf, h=h: e.activation(
                    out=scr[1][:, hf * TH:(hf + 1) * TH], in_=psS3[:, hf * TH:(hf + 1) * TH],
                    func=AF.Sigmoid, bias=big_sb[:, h:h + 1]))
                scr_r[1].ready = pr
            if "e3" not in skip:
                pr = consume("ac", [knm], lambda e, hf=hf, kps=kps: e.square(
                    out=scrb[2][:, hf * TH:(hf + 1) * TH], in_=kps[:, :]))
                scrb_r[2].ready = pr
            if "e4" not in skip:
                S.wait_pair("ve", scrb_r[2].ready)
                S.wait_pair("ve", scr_r[0].ready)
                pr = consume("ve", [knm], lambda e, hf=hf, kps=kps: e.tensor_mul(
                    scr[2][:, hf * TH:(hf + 1) * TH], kps[:, :], scr[0][:, hf * TH:(hf + 1) * TH]))
                scr_r[0].frees.append(pr)
                scr_r[2].ready = pr
        if "bt" in skip:
            continue
        r = pres["S2"]
        S.wait_frees("pe", r)
        S.wait_pair("pe", scrb_r[2].ready)
        for hf in range(2):
            lv = S.do("pe", lambda e, hf=hf: e.matmul(
                psS2[:, hf * TH:(hf + 1) * TH], ones128[:, :], scrb[2][:, hf * TH:(hf + 1) * TH],
                start=True, stop=True), inc="pe")
        sready(r, ("pe", lv))
        scrb_r[2].frees.append(("pe", lv))
        swrite("ve", scr_r[3], lambda e: e.tensor_scalar_max(scr[3][:, 0:T], psS2[:, :], c24), ps_reads=["S2"])
        swrite("ac", scr_r[4], lambda e: e.activation(out=scr[4][:, 0:T], in_=scr[3][:, 0:T], func=AF.Ln), reads=[scr_r[3]])
        swrite("ac", scr_r[3], lambda e: e.activation(out=scr[3][:, 0:T], in_=scr[4][:, 0:T], func=AF.Exp, scale=cmhalf), reads=[scr_r[4]])
        swrite("ve", scr_r[5], lambda e: e.tensor_mul(scr[5][:, 0:T], scr[2][:, 0:T], scr[1][:, 0:T]), reads=[scr_r[2], scr_r[1]])
        swrite("ve", scr_r[4], lambda e: e.tensor_mul(scr[4][:, 0:T], scr[5][:, 0:T], scr[3][:, 0:T]), reads=[scr_r[5], scr_r[3]])
        if "gbc" in skip:
            continue
        r = pres["S1"]
        S.wait_frees("pe", r)
        S.wait("pe", "sx", c_gh)
        for hf in range(2):
            lv = S.do("pe", lambda e, h=h, hf=hf: e.matmul(
                psS1[:, hf * TH:(hf + 1) * TH], oneh2_sb[:, h * 128:(h + 1) * 128],
                gam_hilo[:, hf * TH:(hf + 1) * TH], start=True, stop=True), inc="pe")
        sready(r, ("pe", lv))
        if "scan1" in skip:
            continue
        S.wait_frees("ve", scr_r[5])
        S.wait_pair("ve", pres["S1"].ready)
        S.wait_frees("ve", scr_r[2])
        vz = S.do("ve", lambda e: e.memset(scr[2][:, 0:T], 0.0))
        S.pad("ve")
        v = S.do("ve", lambda e: e.tensor_tensor_scan(
            out=scr[5][:, 0:T], data0=psS1[:, :], data1=scr[2][:, 0:T],
            initial=1.0, op0=ALU.mult, op1=ALU.add), inc="ve")
        pres["S1"].frees.append(("ve", v))
        sready(scr_r[5], ("ve", v))
        sready(scr_r[2], ("ve", v))
        S.pad("ve")
        suse("ve", [scr_r[5]], lambda e, h=h: e.tensor_copy(
            out=g_in[:, 16 + h:17 + h], in_=scr[5][:, T - 1:T]))
        S.pad("ve")
        S.wait_frees("ve", scrb_r[3])
        S.wait_pair("ve", scr_r[4].ready)
        v = S.do("ve", lambda e: e.tensor_tensor_scan(
            out=scrb[3][:, 0:T], data0=psS1[:, :], data1=scr[4][:, 0:T],
            initial=0.0, op0=ALU.mult, op1=ALU.add), inc="ve")
        pres["S1"].frees.append(("ve", v))
        scr_r[4].frees.append(("ve", v))
        sready(scrb_r[3], ("ve", v))
        S.pad("ve")
        suse("ve", [scrb_r[3]], lambda e, h=h: e.tensor_copy(
            out=g_in[:, h:h + 1], in_=scrb[3][:, T - 1:T]))
        S.pad("ve")
        S.wait_pair("sy", scrb_r[3].ready)
        sv = S.do("sy", lambda e, h=h: e.dma_start(
            out=ml_sp.ap()[:, h * T:(h + 1) * T], in_=scrb[3][:, 0:T]), inc="sx", n=16)
        scrb_r[3].frees.append(("sx", sv))
    c_ml = S.cur("sx")
    mark("gin_mf", ("ve", S.cur("ve")))
    for tn, sp_t2, cw2 in (("t_memloc", ml_sp, c_ml),):
        if tn in tap_dram and upto <= 3:
            S.wait("gp", "sx", cw2)
            S.do("gp", lambda e, d=tap_dram[tn], s=sp_t2: e.dma_start(out=d, in_=s.ap()), inc="gp_c", n=16)
    if upto <= 3:
        if "t_xconv" in tap_dram and "latetap" in skip:
            for sn in ("w2", "w3", "w4", "w5", "w6", "w7"):
                S.wait("sy", sn, S.cur(sn))
            S.do("sy", lambda e: e.dma_start(out=tap_dram["t_xconv"], in_=xn_bf[:, :]), inc="sx", n=16)
        return finish()

    wmark("ve", "gin_mf")
    S.wait("ve", "ve", v_lpt)
    S.do("ve", lambda e: e.engine_nop(), inc="ve")
    v_gin = S.cur("ve")
    S.wait("gp", "ve", v_gin)
    S.do("gp", lambda e: e.dma_start(out=cc_in.ap(), in_=g_in[:, :]), inc="gp_c", n=16)
    S.wait("gp", "gp_c", S.cur("gp_c"))
    S.do("gp", lambda e: e.collective_compute(
        "AllGather", ALU.bypass, replica_groups=[list(range(NCORES))],
        ins=[cc_in.ap()], outs=[cc_out.ap()]), inc="cc")
    v_cc = S.cur("cc")
    if "t_gin" in tap_dram:
        S.wait("sy", "ve", v_gin)
        S.do("sy", lambda e: e.dma_start(out=tap_dram["t_gin"], in_=g_in[:, :]), inc="sx", n=16)

    for m in range(NH):
        par = m % 2
        wload(ksl[par], ksl_r[par], f"w{2 + par}", wog_t, m * 128, 128)
        mm_group_pair("A", "B",
                      [ksl[par][:, k * 128:(k + 1) * 128] for k in range(NKD)],
                      [xconv_bf[:, k * T:k * T + TH] for k in range(NKD)],
                      [xconv_bf[:, k * T + TH:k * T + T] for k in range(NKD)],
                      pre_waits=[ksl_r[par].ready, marks["xconv"]])
        ksl_r[par].frees.append(pres["A"].ready)
        ksl_r[par].frees.append(pres["B"].ready)
        S.wait_frees("ac", scrb_r[3])
        scrb_r[3].frees = []
        for hf in range(2):
            nm = "A" if hf == 0 else "B"
            pr = consume("ac", [nm], lambda e, hf=hf, m=m, nm=nm: e.activation(
                out=scrb[3][:, hf * TH:(hf + 1) * TH], in_=pst[nm][:, :],
                func=AF.Sigmoid, bias=bog_sb[:, m:m + 1]))
            scrb_r[3].ready = pr
        S.wait_pair("sy", scrb_r[3].ready)
        sv = S.do("sy", lambda e, m=m: e.dma_start(
            out=og_sp.ap()[:, m * T:(m + 1) * T], in_=scrb[3][:, 0:T]), inc="sx", n=16)
        scrb_r[3].frees.append(("sx", sv))
    c_og = S.cur("sx")

    for m in range(NH):
        par = m % 2
        wload(vsl[par], vsl_r[par], f"w{4 + par}", wq_t, m * 128, 128)
        mm_group_pair("A", "B",
                      [vsl[par][:, k * 128:(k + 1) * 128] for k in range(NKD)],
                      [xn_bf[:, k * T:k * T + TH] for k in range(NKD)],
                      [xn_bf[:, k * T + TH:k * T + T] for k in range(NKD)],
                      pre_waits=[vsl_r[par].ready, marks["xn"]])
        vsl_r[par].frees.append(pres["A"].ready)
        vsl_r[par].frees.append(pres["B"].ready)
        S.wait_frees("ac", scrb_r[3])
        scrb_r[3].frees = []
        for hf in range(2):
            nm = "A" if hf == 0 else "B"
            pr = consume("ac", [nm], lambda e, hf=hf, nm=nm: e.copy(
                out=scrb[3][:, hf * TH:(hf + 1) * TH], in_=pst[nm][:, :]))
            scrb_r[3].ready = pr
        S.wait_pair("sy", scrb_r[3].ready)
        sv = S.do("sy", lambda e, m=m: e.dma_start(
            out=q_sp.ap()[:, m * T:(m + 1) * T], in_=scrb[3][:, 0:T]), inc="sx", n=16)
        scrb_r[3].frees.append(("sx", sv))
    c_q = S.cur("sx")
    r_slots.close()
    open_stacks.remove(r_slots)
    rP12.close()
    open_stacks.remove(rP12)
    if upto <= 4:
        for tn, sp_t2, cw2 in (("t_memloc", ml_sp, c_ml), ("t_og", og_sp, c_og), ("t_q", q_sp, c_q)):
            if tn in tap_dram:
                S.wait("gp", "sx", cw2)
                S.do("gp", lambda e, d=tap_dram[tn], s=sp_t2: e.dma_start(out=d, in_=s.ap()), inc="gp_c", n=16)
        return finish()

    S.wait("gp", "cc", v_cc)
    S.do("gp", lambda e: e.dma_start(
        out=gbuf[:, :].rearrange("p (i c) -> p i c", c=32),
        in_=cc_out.ap().rearrange("(i p) c -> p i c", p=128)), inc="gp_c", n=16)
    g_gbuf = S.cur("gp_c")
    if "t_gbuf" in tap_dram:
        S.wait("sy", "gp_c", g_gbuf)
        S.do("sy", lambda e: e.dma_start(out=tap_dram["t_gbuf"], in_=gbuf[:, :]), inc="sx", n=16)
    S.wait("ve", "gp_c", g_gbuf)
    S.wait("ve", "sx", c_consts)
    S.do("ve", lambda e: e.memset(state_sb[:, :], 0.0))
    S.pad("ve")
    for i in range(NCORES):
        mf = gbuf[:, i * 32:i * 32 + 16]
        pt = gbuf[:, i * 32 + 16:i * 32 + 32]
        S.do("ve", lambda e, i=i, pt=pt: e.tensor_scalar(
            out=small2[:, 0:16], in0=pt, scalar1=selc_sb[:, i:i + 1], scalar2=invc_sb[:, i:i + 1],
            op0=ALU.mult, op1=ALU.add))
        S.pad("ve")
        S.do("ve", lambda e, i=i, mf=mf: e.tensor_scalar_mul(small2[:, 16:32], mf, selc_sb[:, i:i + 1]))
        S.pad("ve")
        S.do("ve", lambda e: e.tensor_mul(small2[:, 32:48], state_sb[:, :], small2[:, 0:16]))
        S.pad("ve")
        S.do("ve", lambda e: e.tensor_add(state_sb[:, :], small2[:, 32:48], small2[:, 16:32]))
        S.pad("ve")
    stv = ("ve", S.do("ve", lambda e: e.engine_nop(), inc="ve"))
    if "t_sel" in tap_dram:
        S.wait_pair("sy", stv)
        S.do("sy", lambda e: e.dma_start(out=tap_dram["t_sel"][:, 0:8], in_=selc_sb[:, :]), inc="sx", n=16)
        S.do("sy", lambda e: e.dma_start(out=tap_dram["t_sel"][:, 8:16], in_=invc_sb[:, :]), inc="sx", n=16)
        S.do("sy", lambda e: e.dma_start(out=tap_dram["t_sel"][:, 16:64], in_=small2[:, :]), inc="sx", n=16)
    mark("state", stv)
    if "t_w" in tap_dram:
        S.wait_pair("sy", stv)
        S.do("sy", lambda e: e.dma_start(out=tap_dram["t_w"], in_=small[:, :]), inc="sx", n=16)
    if "t_state" in tap_dram:
        S.wait_pair("sy", stv)
        S.do("sy", lambda e: e.dma_start(out=tap_dram["t_state"], in_=state_sb[:, :]), inc="sx", n=16)

    r67 = ExitStack()
    open_stacks.append(r67)
    h_bf = r67.enter_context(nc.sbuf_tensor("h_bf", [128, NKH * TH], BF16))
    xnh = r67.enter_context(nc.sbuf_tensor("xnh", [128, NKD * TH], BF16))
    gth = r67.enter_context(nc.sbuf_tensor("gth", [128, NKD * TH], BF16))
    p6a = r67.enter_context(nc.sbuf_tensor("p6a", [128, TH], F32))
    p6b = r67.enter_context(nc.sbuf_tensor("p6b", [128, TH], F32))
    p6a_r, p6b_r = Res(), Res()
    gsl = [lowsl[0], lowsl[1]]
    usl = [lowsl[2], lowsl[3]]
    dsl = dsl_lo
    osl = [lowsl[4], lowsl[5]]
    gsl_r = [Res(), Res()]
    usl_r = [Res(), Res()]
    dsl_r = [Res(), Res()]
    osl_r = [Res(), Res()]
    xres = [r67.enter_context(nc.sbuf_tensor(f"xres{i}", [128, TH], F32)) for i in range(2)]
    yout = [r67.enter_context(nc.sbuf_tensor(f"yout{i}", [128, TH], F32)) for i in range(2)]
    xres_r = [Res(), Res()]
    yout_r = [Res(), Res()]
    xnh_r = Res()
    gth_r = Res()
    hm = Res()

    S.wait("gp", "pe", S.cur("pe"))

    def load_xnh(HF):
        S.wait_frees("sy", xnh_r)
        S.wait("sy", "sx", c_spill)
        S.do("sy", lambda e, HF=HF: e.dma_start(
            out=xnh[:, :].rearrange("p (k t) -> p k t", t=TH),
            in_=xn_sp.ap().rearrange("p (k t) -> p k t", t=T)[:, :, HF * TH:(HF + 1) * TH]),
            inc="sx", n=16)
        sready(xnh_r, ("sx", S.cur("sx")))

    def emit_gateup(m, HF):
        par = m % 2
        wload(gsl[par], gsl_r[par], f"w{par}", wqg_t, m * 128, 128)
        wload(usl[par], usl_r[par], f"w{2 + par}", wqu_t, m * 128, 128)
        mm_group("A", [(gsl[par][:, k * 128:(k + 1) * 128], xnh[:, k * TH:(k + 1) * TH]) for k in range(NKD)]
                 + [(ones2neg[:, :], xns_hilo[:, HF * TH:(HF + 1) * TH])],
                 pre_waits=[gsl_r[par].ready, xnh_r.ready, ("ve", v_ones2), ("sx", c_xns)])
        gsl_r[par].frees.append(pres["A"].ready)
        xnh_r.frees.append(pres["A"].ready)
        mm_group("B", [(usl[par][:, k * 128:(k + 1) * 128], xnh[:, k * TH:(k + 1) * TH]) for k in range(NKD)]
                 + [(ones2neg[:, :], xns_hilo[:, HF * TH:(HF + 1) * TH])],
                 pre_waits=[usl_r[par].ready])
        usl_r[par].frees.append(pres["B"].ready)
        xnh_r.frees.append(pres["B"].ready)
        swrite("ac", p6a_r, lambda e, m=m: e.activation(
            out=p6a[:, :], in_=psA[:, :], func=AF.Silu, scale=sg_sb[:, m:m + 1]), ps_reads=["A"])
        swrite("ve", p6b_r, lambda e, m=m: e.tensor_scalar_mul(p6b[:, :], psB[:, :], su_sb[:, m:m + 1]), ps_reads=["B"])
        return suse("ve", [p6a_r, p6b_r], lambda e, m=m: e.tensor_mul(
            h_bf[:, m * TH:(m + 1) * TH], p6a[:, :], p6b[:, :]))

    r5 = ExitStack()
    open_stacks.append(r5)
    _qh0 = r5.enter_context(nc.sbuf_tensor("qh0", [128, T], BF16))
    _oh0 = r5.enter_context(nc.sbuf_tensor("oh0", [128, T], BF16))
    _mh0 = r5.enter_context(nc.sbuf_tensor("mh0", [128, T], BF16))
    qh = [_qh0, _qh0]
    oh = [_oh0, _oh0]
    mh = [_mh0, _mh0]
    _qr, _or, _mr = Res(), Res(), Res()
    qh_r = [_qr, _qr]
    oh_r = [_or, _or]
    mh_r = [_mr, _mr]
    zer_r = scr_r[7]
    swrite("ve", zer_r, lambda e: e.memset(scr[7][:, 0:T], 0.0))
    S.wait("sy", "sx", c_q)
    load_xnh(0)
    _mbounds = [(h * NKH) // NH for h in range(NH + 1)]
    hlast = None
    for h in range(NH):
        par = h % 2
        for (tiles, rr, sp, semn) in ((qh, qh_r, q_sp, f"w{2 + par}"), (oh, oh_r, og_sp, f"w{4 + par}"), (mh, mh_r, ml_sp, f"w{6 + par}")):
            S.wait_frees("sy", rr[par])
            S.do("sy", lambda e, t=tiles[par], s=sp, h=h: e.dma_start(
                out=t[:, :], in_=s.ap()[:, h * T:(h + 1) * T]), inc=semn, n=16)
            sready(rr[par], (semn, S.cur(semn)))
        r = pres["S1"]
        S.wait_frees("pe", r)
        for hf in range(2):
            lv = S.do("pe", lambda e, h=h, hf=hf: e.matmul(
                psS1[:, hf * TH:(hf + 1) * TH], oneh2_sb[:, h * 128:(h + 1) * 128],
                gam_hilo[:, hf * TH:(hf + 1) * TH], start=True, stop=True), inc="pe")
        sready(r, ("pe", lv))
        S.wait_frees("ve", scr_r[0])
        S.wait_pair("ve", pres["S1"].ready)
        S.wait_pair("ve", zer_r.ready)
        v = S.do("ve", lambda e: e.tensor_tensor_scan(
            out=scr[0][:, 0:T], data0=psS1[:, :], data1=scr[7][:, 0:T],
            initial=1.0, op0=ALU.mult, op1=ALU.add), inc="ve")
        pres["S1"].frees.append(("ve", v))
        zer_r.frees.append(("ve", v))
        sready(scr_r[0], ("ve", v))
        swrite("ve", scr_r[1], lambda e, h=h, p=par: e.scalar_tensor_tensor(
            out=scr[1][:, 0:T], in0=scr[0][:, 0:T], scalar=state_sb[:, h:h + 1],
            in1=mh[p][:, :], op0=ALU.mult, op1=ALU.add),
            reads=[scr_r[0], mh_r[par]], also=[marks["state"]])
        mh_r[par].frees.append(scr_r[1].ready)
        memr = scr_r[1]
        swrite("ac", scrb_r[0], lambda e: e.copy(out=scrb[0][:, 0:T], in_=scr[1][:, 0:T]), reads=[memr])
        swrite("ac", scrb_r[1], lambda e: e.square(out=scrb[1][:, 0:T], in_=scr[1][:, 0:T]), reads=[memr])
        for (nm, sbr, sbuf) in (("S2", scrb_r[0], scrb[0]), ("S3", scrb_r[1], scrb[1])):
            r = pres[nm]
            S.wait_frees("pe", r)
            S.wait_pair("pe", sbr.ready)
            for hf in range(2):
                lv = S.do("pe", lambda e, nm=nm, hf=hf, sb=sbuf: e.matmul(
                    pst[nm][:, hf * TH:(hf + 1) * TH], ones128[:, :], sb[:, hf * TH:(hf + 1) * TH],
                    start=True, stop=True), inc="pe")
            sready(r, ("pe", lv))
            sbr.frees.append(("pe", lv))
        swrite("ac", scr_r[2], lambda e: e.activation(out=scr[2][:, 0:T], in_=psS2[:, :], func=AF.Copy, scale=cinvHD), ps_reads=["S2"])
        swrite("ve", scr_r[3], lambda e: e.tensor_mul(scr[3][:, 0:T], scr[2][:, 0:T], scr[2][:, 0:T]), reads=[scr_r[2]])
        S.wait_frees("ve", scr_r[5])
        S.wait_pair("ve", pres["S3"].ready)
        S.wait_pair("ve", scr_r[3].ready)
        v = S.do("ve", lambda e: e.scalar_tensor_tensor(
            out=scr[5][:, 0:T], in0=psS3[:, :], scalar=cinvHD, in1=scr[3][:, 0:T],
            op0=ALU.mult, op1=ALU.subtract), inc="ve")
        pres["S3"].frees.append(("ve", v))
        scr_r[3].frees.append(("ve", v))
        sready(scr_r[5], ("ve", v))
        swrite("ac", scr_r[4], lambda e: e.activation(out=scr[4][:, 0:T], in_=scr[5][:, 0:T], func=AF.Ln, bias=ceps), reads=[scr_r[5]])
        swrite("ac", scr_r[5], lambda e: e.activation(out=scr[5][:, 0:T], in_=scr[4][:, 0:T], func=AF.Exp, scale=cmhalf), reads=[scr_r[4]])
        swrite("ve", scr_r[3], lambda e: e.tensor_sub(scr[3][:, 0:T], scr[1][:, 0:T], scr[2][:, 0:T]), reads=[memr, scr_r[2]])
        swrite("ve", scr_r[4], lambda e: e.tensor_mul(scr[4][:, 0:T], scr[3][:, 0:T], scr[5][:, 0:T]), reads=[scr_r[3], scr_r[5]])
        swrite("ve", scr_r[3], lambda e, h=h: e.tensor_scalar(
            out=scr[3][:, 0:T], in0=scr[4][:, 0:T], scalar1=mnw_sb[:, h:h + 1], scalar2=mnb_sb[:, h:h + 1],
            op0=ALU.mult, op1=ALU.add), reads=[scr_r[4]])
        swrite("ve", scr_r[6], lambda e, p=par: e.tensor_mul(scr[6][:, 0:T], scr[3][:, 0:T], qh[p][:, :]),
               reads=[scr_r[3], qh_r[par]])
        ur = scr_r[6]
        swrite("ac", scrb_r[0], lambda e: e.copy(out=scrb[0][:, 0:T], in_=scr[6][:, 0:T]), reads=[ur])
        swrite("ac", scrb_r[1], lambda e: e.square(out=scrb[1][:, 0:T], in_=scr[6][:, 0:T]), reads=[ur])
        pr = swrite("ac", scrb_r[2], lambda e, p=par: e.square(out=scrb[2][:, 0:T], in_=qh[p][:, :]), reads=[qh_r[par]])
        qh_r[par].frees.append(pr)
        for (nm, sbr, sbuf) in (("S2", scrb_r[0], scrb[0]), ("S3", scrb_r[1], scrb[1]), ("S1", scrb_r[2], scrb[2])):
            r = pres[nm]
            S.wait_frees("pe", r)
            S.wait_pair("pe", sbr.ready)
            for hf in range(2):
                lv = S.do("pe", lambda e, nm=nm, hf=hf, sb=sbuf: e.matmul(
                    pst[nm][:, hf * TH:(hf + 1) * TH], ones128[:, :], sb[:, hf * TH:(hf + 1) * TH],
                    start=True, stop=True), inc="pe")
            sready(r, ("pe", lv))
            sbr.frees.append(("pe", lv))
        swrite("ve", scr_r[3], lambda e: e.tensor_scalar_max(scr[3][:, 0:T], psS1[:, :], c24), ps_reads=["S1"])
        swrite("ac", scr_r[2], lambda e: e.activation(out=scr[2][:, 0:T], in_=scr[3][:, 0:T], func=AF.Ln), reads=[scr_r[3]])
        swrite("ac", scr_r[0], lambda e: e.activation(out=scr[0][:, 0:T], in_=scr[2][:, 0:T], func=AF.Exp, scale=cmhalf), reads=[scr_r[2]])
        rqr = scr_r[0]
        swrite("ac", scr_r[4], lambda e: e.activation(out=scr[4][:, 0:T], in_=psS2[:, :], func=AF.Copy, scale=cinvHD), ps_reads=["S2"])
        swrite("ve", scr_r[3], lambda e: e.tensor_mul(scr[3][:, 0:T], scr[4][:, 0:T], scr[4][:, 0:T]), reads=[scr_r[4]])
        S.wait_frees("ve", scr_r[5])
        S.wait_pair("ve", pres["S3"].ready)
        S.wait_pair("ve", scr_r[3].ready)
        v = S.do("ve", lambda e: e.scalar_tensor_tensor(
            out=scr[5][:, 0:T], in0=psS3[:, :], scalar=cinvHD, in1=scr[3][:, 0:T],
            op0=ALU.mult, op1=ALU.subtract), inc="ve")
        pres["S3"].frees.append(("ve", v))
        scr_r[3].frees.append(("ve", v))
        sready(scr_r[5], ("ve", v))
        swrite("ve", scr_r[1], lambda e: e.tensor_mul(scr[1][:, 0:T], scr[4][:, 0:T], scr[0][:, 0:T]), reads=[scr_r[4], rqr])
        swrite("ve", scr_r[2], lambda e: e.tensor_mul(scr[2][:, 0:T], scr[0][:, 0:T], scr[0][:, 0:T]), reads=[rqr])
        swrite("ve", scr_r[4], lambda e: e.tensor_mul(scr[4][:, 0:T], scr[2][:, 0:T], scr[5][:, 0:T]), reads=[scr_r[2], scr_r[5]])
        swrite("ac", scr_r[2], lambda e: e.activation(out=scr[2][:, 0:T], in_=scr[4][:, 0:T], func=AF.Ln, bias=ceps), reads=[scr_r[4]])
        swrite("ac", scr_r[5], lambda e: e.activation(out=scr[5][:, 0:T], in_=scr[2][:, 0:T], func=AF.Exp, scale=cmhalf), reads=[scr_r[2]])
        swrite("ve", scr_r[2], lambda e: e.tensor_mul(scr[2][:, 0:T], scr[6][:, 0:T], scr[0][:, 0:T]), reads=[ur, rqr])
        swrite("ve", scr_r[4], lambda e: e.tensor_sub(scr[4][:, 0:T], scr[2][:, 0:T], scr[1][:, 0:T]), reads=[scr_r[2], scr_r[1]])
        swrite("ve", scr_r[2], lambda e: e.tensor_mul(scr[2][:, 0:T], scr[4][:, 0:T], scr[5][:, 0:T]), reads=[scr_r[4], scr_r[5]])
        swrite("ve", scr_r[4], lambda e, h=h: e.tensor_scalar(
            out=scr[4][:, 0:T], in0=scr[2][:, 0:T], scalar1=gnw_sb[:, h:h + 1], scalar2=gnb_sb[:, h:h + 1],
            op0=ALU.mult, op1=ALU.add), reads=[scr_r[2]])
        pr = swrite("ve", scrb_r[3], lambda e, p=par: e.tensor_mul(
            scrb[3][:, 0:T], scr[4][:, 0:T], oh[p][:, :]), reads=[scr_r[4], oh_r[par]])
        oh_r[par].frees.append(pr)
        S.wait_pair("sy", scrb_r[3].ready)
        sv = S.do("sy", lambda e, h=h: e.dma_start(
            out=gt_sp.ap()[:, h * T:(h + 1) * T], in_=scrb[3][:, 0:T]), inc="sx", n=16)
        scrb_r[3].frees.append(("sx", sv))

        for m in range(_mbounds[h], _mbounds[h + 1]):
            hlast = emit_gateup(m, 0)
    c_gt = S.cur("sx")
    sready(hm, hlast)

    for tn, sp_t, cwait in (("t_memloc", ml_sp, c_ml), ("t_og", og_sp, c_og),
                            ("t_q", q_sp, c_q), ("t_gated", gt_sp, c_gt)):
        if tn in tap_dram:
            S.wait("gp", "sx", cwait)
            S.do("gp", lambda e, d=tap_dram[tn], s=sp_t: e.dma_start(out=d, in_=s.ap()), inc="gp_c", n=16)
    if upto <= 5:
        return finish()

    def emit_hsum(HF):
        hl = hm.ready
        S.wait_frees("ac", scr_r[2])
        mm_group("A", [(ones128[:, 0:1], h_bf[:, k * TH:(k + 1) * TH]) for k in range(NKH)],
                 sub=psA[0:1, :], pre_waits=[hl])
        hm.frees.append(pres["A"].ready)
        pa = consume("ac", ["A"], lambda e: e.copy(out=scr[2][0:1, 0:TH], in_=psA[0:1, :]))
        S.wait_pair("ve", pa)
        S.do("ve", lambda e: e.tensor_copy(out=rbf_a[:, 0:TH], in_=scr[2][0:1, 0:TH]))
        S.pad("ve")
        S.do("ve", lambda e: e.tensor_sub(rbf_b[:, 0:TH], scr[2][0:1, 0:TH], rbf_a[:, 0:TH]), inc="ve")
        S.pad("ve")
        v_hs = S.cur("ve")
        S.wait("sy", "ve", v_hs)
        S.do("sy", lambda e: e.dma_start(out=hs_hilo[0:1, 0:TH], in_=rbf_a[:, 0:TH]), inc="sx", n=16)
        S.do("sy", lambda e: e.dma_start(out=hs_hilo[1:2, 0:TH], in_=rbf_b[:, 0:TH]), inc="sx", n=16)
        scr_r[2].frees.append(("sx", S.cur("sx")))
        scr_r[2].frees.append(("ve", S.cur("ve")))
        return S.cur("sx")

    def emit_p7(HF, c_hs):
        S.wait_frees("sy", gth_r)
        S.wait("sy", "sx", c_gt)
        S.do("sy", lambda e, HF=HF: e.dma_start(
            out=gth[:, :].rearrange("p (k t) -> p k t", t=TH),
            in_=gt_sp.ap().rearrange("p (k t) -> p k t", t=T)[:, :, HF * TH:(HF + 1) * TH]),
            inc="sx", n=16)
        sready(gth_r, ("sx", S.cur("sx")))
        for m in range(NH):
            par = m % 2
            wload(dsl[par], dsl_r[par], f"w{4 + par}", wqd_t, m * 128, 128, nk=NKH)
            wload(osl[par], osl_r[par], f"w{6 + par}", wo_t, m * 128, 128)
            xr = xres_r[par]
            S.wait_frees("sy", xr)
            S.do("sy", lambda e, m=m, HF=HF, i=par: e.dma_start(
                out=xres[i][:, :], in_=x_t[m * 128:(m + 1) * 128, HALO + HF * TH:HALO + (HF + 1) * TH]),
                inc=xsem[par], n=16)
            sready(xr, (xsem[par], S.cur(xsem[par])))
            mm_group("A", [(dsl[par][:, k * 128:(k + 1) * 128], h_bf[:, k * TH:(k + 1) * TH]) for k in range(NKH)]
                     + [(ones2neg[:, :], hs_hilo[:, 0:TH])],
                     pre_waits=[dsl_r[par].ready, hm.ready, ("sx", c_hs)])
            dsl_r[par].frees.append(pres["A"].ready)
            hm.frees.append(pres["A"].ready)
            mm_group("B", [(osl[par][:, k * 128:(k + 1) * 128], gth[:, k * TH:(k + 1) * TH]) for k in range(NKD)],
                     pre_waits=[osl_r[par].ready, gth_r.ready])
            osl_r[par].frees.append(pres["B"].ready)
            gth_r.frees.append(pres["B"].ready)
            swrite("ve", scr_r[0 + par], lambda e, m=m, i=par: e.tensor_scalar_mul(
                scr[i][:, 0:TH], psA[:, :], sd_sb[:, m:m + 1]), ps_reads=["A"])
            swrite("ve", scr_r[2 + par], lambda e, i=par: e.tensor_add(
                scr[2 + i][:, 0:TH], scr[i][:, 0:TH], psB[:, :]), reads=[scr_r[0 + par]], ps_reads=["B"])
            yr = yout_r[par]
            S.wait_frees("ve", yr)
            S.wait_pair("ve", scr_r[2 + par].ready)
            S.wait_pair("ve", xr.ready)
            v = S.do("ve", lambda e, i=par: e.tensor_add(
                yout[i][:, :], scr[2 + i][:, 0:TH], xres[i][:, :]), inc="ve")
            scr_r[2 + par].frees.append(("ve", v))
            xr.frees.append(("ve", v))
            sready(yr, ("ve", v))
            S.wait_pair("sy", yr.ready)
            sv = S.do("sy", lambda e, m=m, HF=HF, i=par: e.dma_start(
                out=y_t[m * 128:(m + 1) * 128, HF * TH:(HF + 1) * TH], in_=yout[i][:, :]),
                inc="sx", n=16)
            yr.frees.append(("sx", sv))

    c_hs0 = emit_hsum(0)
    if "t_h" in tap_dram:
        S.wait_pair("sy", hm.ready)
        S.do("sy", lambda e: e.dma_start(out=tap_dram["t_h"], in_=h_bf[:, :]), inc="sx", n=16)
    emit_p7(0, c_hs0)
    load_xnh(1)
    S.wait_frees("ve", hm)
    hm.frees = []
    hlast = None
    for m in range(NKH):
        hlast = emit_gateup(m, 1)
    sready(hm, hlast)
    c_hs1 = emit_hsum(1)
    emit_p7(1, c_hs1)

    return finish()



def _col16(v):
    return np.ascontiguousarray(np.asarray(v, np.float32).reshape(16, 128).T)


def _prep_shared(inp):
    d = {}
    f = np.float32
    d["wq_t"] = np.ascontiguousarray(np.asarray(inp["L0_attn_Wq"], f).T)
    d["wk_t"] = np.ascontiguousarray(np.asarray(inp["L0_attn_Wk"], f).T)
    d["wv_t"] = np.ascontiguousarray(np.asarray(inp["L0_attn_Wv"], f).T)
    d["wo_t"] = np.ascontiguousarray(np.asarray(inp["L0_attn_Wo"], f).T)
    d["wig_t"] = np.ascontiguousarray(np.asarray(inp["L0_attn_Wig"], f).T)
    d["wog_t"] = np.ascontiguousarray(np.asarray(inp["L0_attn_Wog"], f).T)
    d["wg_t"] = np.ascontiguousarray(np.asarray(inp["L0_attn_Wg"], f).T)

    def u8T(a):
        a = np.ascontiguousarray(np.asarray(a).astype("<i4", copy=False))
        u = a.view(np.uint8).reshape(a.shape[0], a.shape[1], 4)[:, :, 0]
        return np.ascontiguousarray(u.T)

    d["wqg_t"] = u8T(inp["L0_ffn_wq_gate"])
    d["wqu_t"] = u8T(inp["L0_ffn_wq_up"])
    d["wqd_t"] = u8T(inp["L0_ffn_wq_down"])
    cw = np.asarray(inp["L0_attn_conv_w"], f)
    d["convw"] = np.ascontiguousarray(cw.reshape(16, 128, 4).transpose(1, 0, 2).reshape(128, 64))
    d["convb"] = _col16(inp["L0_attn_conv_b"])
    d["big"] = _col16(inp["L0_attn_b_ig"])
    d["bog"] = _col16(inp["L0_attn_b_og"])
    d["bg"] = np.ascontiguousarray(np.asarray(inp["L0_attn_b_g"], f).reshape(16, 1))
    d["gnw"] = _col16(inp["L0_attn_gn_w"])
    d["gnb"] = _col16(inp["L0_attn_gn_b"])
    d["mnw"] = _col16(inp["L0_attn_mn_w"])
    d["mnb"] = _col16(inp["L0_attn_mn_b"])
    d["sgc"] = np.ascontiguousarray(np.asarray(inp["L0_ffn_s_gate"], f).reshape(44, 128).T)
    d["suc"] = np.ascontiguousarray(np.asarray(inp["L0_ffn_s_up"], f).reshape(44, 128).T)
    d["sdc"] = _col16(inp["L0_ffn_s_down"])
    oh2 = np.zeros((32, NH * 128), ml_dtypes.bfloat16)
    for h in range(NH):
        oh2[h, h * 128:(h + 1) * 128] = 1
        oh2[16 + h, h * 128:(h + 1) * 128] = 1
    d["oneh2"] = oh2
    return d


def _prep_core(inp, c):
    b, j = c // 4, c % 4
    x = np.asarray(inp["L0_attn_x"], np.float32)
    t0 = j * T
    if j == 0:
        seg = np.concatenate([np.zeros((HALO, D), np.float32), x[b, 0:T]], axis=0)
    else:
        seg = x[b, t0 - HALO:t0 + T]
    d = {"x_t": np.ascontiguousarray(seg.T)}
    selc = np.zeros((128, 8), np.float32)
    for i in range(8):
        bi, ji = i // 4, i % 4
        if bi == b and ji < j:
            selc[:, i] = 1.0
    d["selc"] = selc
    d["invc"] = np.ascontiguousarray(1.0 - selc)
    return d


def kernel(**inputs):
    shared = _prep_shared(inputs)
    nc = _build()
    in_maps = []
    for c in range(NCORES):
        m = dict(shared)
        m.update(_prep_core(inputs, c))
        in_maps.append(m)
    res = run_bass_kernel_spmd(nc, in_maps, core_ids=list(range(NCORES)))
    y = np.empty((B, TFULL, D), np.float32)
    for c in range(NCORES):
        b, j = c // 4, c % 4
        y[b, j * T:(j + 1) * T, :] = res.results[c]["y_t"].T
    return y


if __name__ == "__main__":
    nc = _build()
    print("build ok; instructions:", nc.n_instructions())
